# version 11
# baseline (speedup 1.0000x reference)
import numpy as np
import ml_dtypes
from contextlib import ExitStack
import concourse.bass as bass
import concourse.mybir as mybir
from concourse.bass_utils import run_bass_kernel_spmd

F32 = mybir.dt.float32
BF16 = mybir.dt.bfloat16
I32 = mybir.dt.int32
AF = mybir.ActivationFunctionType
ALU = mybir.AluOpType
AX = mybir.AxisListType

T = 2048
D = 1024
DI = 2048
NST = 16
TC = 256
NTC = T // TC
STAGE = 9
DEBUG = False
R127 = float(np.float32(1.0 / 127.0))


class Sem:
    __slots__ = ("h", "n")

    def __init__(self, ctx, nc, name):
        self.h = ctx.enter_context(nc.semaphore(name))
        self.n = 0

    def inc(self, instr, k=1):
        instr.then_inc(self.h, k)
        self.n += k
        return (self, self.n)

    def tok(self):
        return (self, self.n)


class B:
    def __init__(self):
        self.nc = bass.Bass("TRN2", target_bir_lowering=False,
                            detect_race_conditions=False,
                            dynamic_dma_scratch_size=4096)
        self.ctx = ExitStack()
        self.waited = {e: {} for e in ("tensor", "vector", "scalar", "sync", "gpsimd")}
        self.cur = None
        self.bkfree = [None] * 8
        self.win_use = [None] * 4
        self.win2_use = [None] * 4
        self.tk = {}

    def on(self, engname):
        def reg(f):
            self.cur = engname
            getattr(self.block, engname)(f)
            self.cur = None
        return reg

    def w(self, eng, *tokens):
        d = self.waited[self.cur]
        for tk_ in tokens:
            if tk_ is None:
                continue
            sem, v = tk_
            if v > 0 and d.get(sem, 0) < v:
                eng.wait_ge(sem.h, v)
                d[sem] = v

    def tdrain(self, te):
        return self.sT.inc(te.drain())

    def ch(self, eng, instr):
        s = self.sVC if self.cur == "vector" else self.sSC
        self.w(eng, s.inc(instr))

    def bw(self, eng, *banks):
        for i in banks:
            self.w(eng, self.bkfree[i])

    def bfree(self, tok, *banks):
        for i in banks:
            self.bkfree[i] = tok

    def rln(self, r, c0, cnt, rev=False):
        t = self.RLN0 if r < 2 else self.RLN1
        p = (r % 2) * 32
        if not rev:
            return bass.AP(t, p * T + c0, [[T, 1], [1, cnt]])
        return bass.AP(t, p * T + c0 + cnt - 1, [[T, 1], [-1, cnt]])

    def selbc(self, p, K):
        return bass.AP(self.IDENT, p, [[128, K], [0, 128]])

    def rln_rhs(self, layer, c0, cnt, rev=False):
        t = self.RLN0 if layer == 0 else self.RLN1
        if not rev:
            return bass.AP(t, c0, [[T, 64], [1, cnt]])
        return bass.AP(t, c0 + cnt - 1, [[T, 64], [-1, cnt]])

    def build(self):
        nc, ctx = self.nc, self.ctx
        dram = lambda name, shape, dt=F32, kind="ExternalInput": \
            nc.dram_tensor(name, shape, dt, kind=kind).ap()
        sb = lambda name, shape, dt=F32: ctx.enter_context(nc.sbuf_tensor(name, shape, dt))

        self.h_x = dram("x_fm", [D, T])
        self.h_out = dram("out_tok", [T, D], kind="ExternalOutput")
        self.h_id = dram("consts", [128, 128])
        self.h_inT = dram("inT", [D, 2 * DI])
        self.h_outT = dram("outT", [DI, D])
        self.h_xpT = dram("xpT", [2 * DI, 96])
        self.h_dtT = dram("dtT", [2 * 64, DI])
        self.h_vecs0 = dram("vecs0", [128, 32 * 8])
        self.h_A0 = dram("A0", [128, 32 * NST])
        self.h_wgT = dram("wgT", [D, 4096])
        self.h_wuT = dram("wuT", [D, 4096])
        self.h_wdT = dram("wdT", [4096, D])
        self.h_wxz = dram("wxz", [D, 2 * D])
        self.h_wdt = dram("wdt", [D, D])
        self.h_wB = dram("wB", [D, NST])
        self.h_wC = dram("wC", [D, NST])
        self.h_wo1 = dram("wo1", [D, D])
        self.h_vecs1 = dram("vecs1", [128, 8 * 4])
        self.h_A1 = dram("A1", [128, 8 * NST])
        self.h_gate = dram("gateW", [D, 8])
        self.h_W1 = dram("W1", [8, D, 2048])
        self.h_W2 = dram("W2", [8, 2048, D])
        self.h_b1 = dram("b1c", [128, 128])
        self.h_b2 = dram("b2r", [8, D])
        if DEBUG:
            self.h_dbg = {k: dram(f"dbg_{k}", [128, 8 * T], kind="ExternalOutput")
                          for k in ("ffn", "pa", "pb")}
            self.h_dbgr = dram("dbg_rows", [64, T], kind="ExternalOutput")
            self.h_dbgxq = nc.dram_tensor("dbg_xqt", [128, 1024], BF16,
                                          kind="ExternalOutput").ap()
            self.h_dbggu = dram("dbg_gu", [128, 4096], kind="ExternalOutput")
            self.h_dbgxq2 = dram("dbg_xq2", [128, 4096], kind="ExternalOutput")
            self.h_dbgfm = nc.dram_tensor("dbg_xq2fm", [128, 4096], BF16,
                                          kind="ExternalOutput").ap()
            self.h_dbgrow = dram("dbg_row1", [1, 512], kind="ExternalOutput")
            self.h_dbgqc = dram("dbg_qc", [128, 8], kind="ExternalOutput")
            self.h_dbgxn = dram("dbg_xn", [128, 1024], kind="ExternalOutput")
            self.h_dbgst = dram("dbg_stat", [128, 16], kind="ExternalOutput")
        self.h_wqg = nc.dram_tensor("wqg_s", [D, 4096], BF16, kind="Internal").ap()
        self.h_wqu = nc.dram_tensor("wqu_s", [D, 4096], BF16, kind="Internal").ap()
        self.h_wqd = nc.dram_tensor("wqd_s", [4096, D], BF16, kind="Internal").ap()

        self.ACC = sb("ACC", [128, 8 * T])
        self.IDENT = sb("IDENT", [128, 128])
        self.IDENTB = sb("IDENTB", [128, 128], BF16)
        self.ONES = sb("ONESC", [128, 128])
        self.WIN = sb("WIN", [128, 4 * 512])
        self.WIN2 = sb("WIN2", [128, 4 * 512])
        self.TMP = sb("TMP", [128, 1024])
        self.STAT = sb("STAT", [128, 1024])
        self.RLN1 = sb("RLN1", [64, T])
        self.ROW1 = sb("ROW1", [1, 512])
        self.SCAL = sb("SCAL", [128, 32])
        self.VEC1 = sb("VEC1", [128, 8 * 4])
        self.A1SB = sb("A1SB", [128, 8 * NST])
        self.GATEW = sb("GATEW", [128, 8 * 8])
        self.WBSB = sb("WBSB", [128, 8 * NST])
        self.WCSB = sb("WCSB", [128, 8 * NST])
        self.B1C = sb("B1C", [128, 128])
        self.B2R = sb("B2R", [8, D])
        self.GROWS = sb("GROWS", [8, T])
        self.GTOK = sb("GTOK", [128, 8])

        self.pb = [ctx.enter_context(nc.psum_tensor(f"pb{i}", [128, 512], F32))
                   for i in range(8)]
        for name in ("sW", "sG", "sT", "sV", "sS", "sO", "sVC", "sSC",
                     "sB1", "sB2", "sD0", "sD1", "sX", "sWQ", "sXI"):
            setattr(self, name, Sem(ctx, nc, name))
        self.sQ = [Sem(ctx, nc, f"sQ{i}") for i in range(4)]
        self.sR = [Sem(ctx, nc, f"sR{i}") for i in range(4)]

        self.block = ctx.enter_context(nc.Block())

        with ExitStack() as l0stk:
            self.RLN0 = l0stk.enter_context(nc.sbuf_tensor("RLN0", [64, T], F32))
            self.VEC0 = l0stk.enter_context(nc.sbuf_tensor("VEC0", [128, 256], F32))
            self.A0SB = l0stk.enter_context(nc.sbuf_tensor("A0SB", [128, 512], F32))
            self.prologue()
            self.ln_stats(0)
            self.dbg_rows_dump(self.tk["ln0"])
            if STAGE >= 2:
                self.ffn0()
            else:
                self.tk["ffn"] = self.sV.tok()
            self.dbg_dump("ffn", self.tk["ffn"])
            if STAGE >= 3:
                self.attn_pass(0)
                self.dbg_dump("pa", self.sV.tok())
            if STAGE >= 4:
                self.attn_pass(1)
                self.dbg_dump("pb", self.sV.tok())
            self.tk["x1"] = self.sV.tok()
        if STAGE >= 5:
            self.ln_stats(1)
            self.layer1()
        self.tk["final"] = self.sV.tok()
        self.epilogue()

        self.ctx.close()
        return self.nc

    def dbg_dump(self, key, tok):
        if not DEBUG:
            return
        b = self

        @b.on("gpsimd")
        def _(g):
            b.w(g, tok)
            g.dma_start(out=b.h_dbg[key][:, :], in_=b.ACC[:, :]).then_inc(b.sO.h, 16)
            b.sO.n += 16
            b.tk[f"dbgd_{key}"] = b.sO.tok()

        @b.on("vector")
        def _(v):
            b.w(v, b.tk[f"dbgd_{key}"])

    def dbg_rows_dump(self, tok):
        if not DEBUG:
            return
        b = self

        @b.on("gpsimd")
        def _(g):
            b.w(g, tok)
            g.dma_start(out=b.h_dbgr[:, :], in_=b.RLN0[0:64, :]).then_inc(b.sO.h, 16)
            b.sO.n += 16

    def prologue(self):
        b = self

        @b.on("gpsimd")
        def _(g):
            g.dma_start(out=b.IDENT[:, :], in_=b.h_id[:, :]).then_inc(b.sG.h, 16)
            b.sG.n += 16
            for c in range(8):
                g.dma_start(out=b.ACC[:, c * T:(c + 1) * T],
                            in_=b.h_x[c * 128:(c + 1) * 128, :]).then_inc(b.sG.h, 16)
                b.sG.n += 16
        b.tk["x"] = b.sG.tok()

        @b.on("sync")
        def _(s):
            for dst, src in ((b.VEC0, b.h_vecs0), (b.A0SB, b.h_A0),
                             (b.VEC1, b.h_vecs1), (b.A1SB, b.h_A1),
                             (b.B1C, b.h_b1), (b.B2R, b.h_b2)):
                s.dma_start(out=dst[0:dst.shape[0], :], in_=src[:, :]).then_inc(b.sW.h, 16)
                b.sW.n += 16
            for dst, src, wdt in ((b.WBSB, b.h_wB, NST), (b.WCSB, b.h_wC, NST),
                                  (b.GATEW, b.h_gate, 8)):
                s.dma_start(out=dst[:, :8 * wdt], in_=bass.AP(
                    src.tensor, 0, [[wdt, 128], [128 * wdt, 8], [1, wdt]])
                ).then_inc(b.sW.h, 16)
                b.sW.n += 16
        b.tk["smallw"] = b.sW.tok()

        @b.on("scalar")
        def _(s):
            b.w(s, b.tk["smallw"])
            s.activation(b.A0SB[:, :], b.A0SB[:, :], AF.Exp)
            ins = s.activation(b.A1SB[:, :], b.A1SB[:, :], AF.Exp)
            b.tk["aexp"] = b.sS.inc(ins)

        @b.on("vector")
        def _(v):
            b.w(v, b.tk["smallw"], b.tk["x"], b.tk["aexp"])
            v.tensor_scalar_mul(b.A0SB[:, :], b.A0SB[:, :], -1.0)
            ins = v.tensor_scalar_mul(b.A1SB[:, :], b.A1SB[:, :], -1.0)
            b.tk["aprep"] = b.sV.inc(ins)
            v.memset(b.RLN0[:, :], 0.0)
            v.memset(b.RLN1[:, :], 0.0)
            v.memset(b.ONES[:, :], 1.0)
            v.tensor_copy(b.IDENTB[:, :], b.IDENT[:, :])
            v.memset(b.SCAL[:, 17:18], 1e-5)
            v.memset(b.SCAL[:, 18:19], 10.0)
            ins = v.tensor_scalar(b.SCAL[:, 8:16], bass.AP(b.VEC1, 2, [[32, 128], [4, 8]]),
                                  0.1, -10.0, ALU.max, ALU.mult)
            b.tk["ones"] = b.sV.inc(ins)

        @b.on("tensor")
        def _(te):
            b.w(te, b.tk["ones"])
            b.bw(te, 2)
            last = None
            for c in range(8):
                last = te.matmul(b.pb[2][0:8, 0:1], b.GATEW[:, c * 8:(c + 1) * 8],
                                 b.ONES[:, 0:1], start=(c == 0), stop=(c == 7))
            b.tk["wsume_ps"] = b.tdrain(te)

        @b.on("scalar")
        def _(s):
            b.w(s, b.tk["wsume_ps"])
            ins = s.copy(b.SCAL[0:8, 16:17], b.pb[2][0:8, 0:1])
            tok = b.sS.inc(ins)
            b.tk["wsume"] = tok
            b.bfree(tok, 2)

    def ln_stats(self, layer):
        b = self
        mr_row = 0 if layer == 0 else 2
        sr_row = 1 if layer == 0 else 3
        ready = [b.tk["x"], b.tk["ones"]] if layer == 0 else [b.tk["x1"], b.tk["ones"]]

        for g in range(4):
            s0 = g * 512
            sq = [None] * 8
            tebase = b.sT.n

            @b.on("scalar")
            def _(s, s0=s0, sq=sq, g=g, layer=layer, tebase=tebase):
                b.w(s, *ready)
                for c in range(8):
                    if c >= 2:
                        b.w(s, (b.sT, tebase + c - 1))
                    ins = s.activation(b.STAT[:, (c % 2) * 512:(c % 2) * 512 + 512],
                                       b.ACC[:, c * T + s0:c * T + s0 + 512], AF.Square)
                    sq[c] = b.sS.inc(ins)

            @b.on("tensor")
            def _(te, s0=s0, sq=sq, g=g, layer=layer):
                b.w(te, *ready)
                b.bw(te, 0, 1)
                for c in range(8):
                    te.matmul(b.pb[0][0:1, :], b.ONES[:, 0:1],
                              b.ACC[:, c * T + s0:c * T + s0 + 512],
                              start=(c == 0), stop=(c == 7))
                for c in range(8):
                    b.w(te, sq[c])
                    ins = te.matmul(b.pb[1][0:1, :], b.ONES[:, 0:1],
                                    b.STAT[:, (c % 2) * 512:(c % 2) * 512 + 512],
                                    start=(c == 0), stop=(c == 7))
                    b.sT.inc(ins)
                b.tk[f"lnmm{layer}_{g}"] = b.tdrain(te)
            assert b.sT.n == tebase + 9, (b.sT.n, tebase)

            @b.on("vector")
            def _(v, s0=s0, g=g, layer=layer):
                b.w(v, b.tk[f"lnmm{layer}_{g}"])
                mr = b.rln(mr_row, s0, 512)
                v.tensor_scalar(mr, b.pb[0][0:1, :512], float(np.float32(1.0 / D)), None, ALU.mult)
                v.tensor_mul(b.ROW1[0:1, :], mr, mr)
                v.tensor_scalar(b.ROW1[0:1, :], b.ROW1[0:1, :], -1.0, None, ALU.mult)
                ins = v.scalar_tensor_tensor(b.ROW1[0:1, :], b.pb[1][0:1, :512],
                                             float(np.float32(1.0 / D)), b.ROW1[0:1, :],
                                             ALU.mult, ALU.add)
                tok = b.sV.inc(ins)
                b.tk["var"] = tok
                b.bfree(tok, 0, 1)

            @b.on("scalar")
            def _(s, s0=s0, g=g, layer=layer):
                b.w(s, b.tk["var"])
                ins = s.activation(b.ROW1[0:1, :], b.ROW1[0:1, :],
                                   AF.Sqrt, bias=b.SCAL[0:1, 17:18], scale=1.0)
                b.tk["lnsqrt"] = b.sS.inc(ins)

            @b.on("vector")
            def _(v, s0=s0, g=g, layer=layer):
                b.w(v, b.tk["lnsqrt"])
                ins = v.reciprocal(b.rln(sr_row, s0, 512), b.ROW1[0:1, :])
                b.tk[f"ln{layer}_{g}"] = b.sV.inc(ins)
        b.tk[f"ln{layer}"] = b.sV.tok()

    def ffn0(self):
        b = self
        nc = self.nc
        with ExitStack() as fstk:
            fsb = lambda name, shape, dt=F32: fstk.enter_context(
                nc.sbuf_tensor(name, shape, dt))
            TMPI = fstk.enter_context(nc.sbuf_tensor("TMPI", [128, 1024], I32))
            FSCR = fsb("FSCR", [128, 4096])
            XQT = fsb("XQT", [128, 8 * 128], BF16)
            XQ2 = fsb("XQ2", [128, 4096])
            XQ2FM = fsb("XQ2FM", [128, 32 * 128], BF16)
            WQB = fsb("WQB", [128, 8 * 512], BF16)
            WQB2 = fsb("WQB2", [128, 8 * 512], BF16)
            WQD = fsb("WQD", [128, 2 * 1024], BF16)
            WQO = fsb("WQO", [128, 2 * 1024], BF16)
            QC = fsb("QC", [128, 8])

            self._ffn_wprep(TMPI, WQO)
            if DEBUG:
                self.dbg_dump("pa", b.tk["wq_scratch"])
            for tt in range(16):
                self._ffn_tile(tt, TMPI, FSCR, XQT, XQ2, XQ2FM, WQB, WQB2, WQD, QC)
            b.tk["ffn"] = b.sV.tok()

    def _ffn_wprep(self, TMPI, WQO):
        b = self
        for wi, (h_w, h_q) in enumerate(((b.h_wgT, b.h_wqg), (b.h_wuT, b.h_wqu),
                                         (b.h_wdT, b.h_wqd))):
            nrow = h_w.shape[0] // 128
            npc = h_w.shape[1] // 1024
            npieces = nrow * npc

            sbase = b.sS.n
            for k in range(npieces):
                c, p = divmod(k, npc)

                @b.on("sync")
                def _(s, k=k, c=c, p=p):
                    if k >= 2:
                        b.w(s, (b.sS, sbase + k - 1))
                    else:
                        b.w(s, b.win2_use[0], b.win2_use[1], b.win2_use[2],
                            b.win2_use[3])
                    sm = b.sR[(k % 2) * 2]
                    sm.n += 16
                    s.dma_start(out=b.WIN2[:, (k % 2) * 1024:(k % 2) * 1024 + 1024],
                                in_=h_w[c * 128:(c + 1) * 128, p * 1024:(p + 1) * 1024]
                                ).then_inc(sm.h, 16)
                    b.tk["wp_dma"] = sm.tok()

                @b.on("scalar")
                def _(s, k=k):
                    b.w(s, b.tk["wp_dma"])
                    if k == 0:
                        b.w(s, b.tk.get("wp_tmp_free"))
                        b.w(s, b.tk["ln0"])
                    ins = s.activation(b.STAT[:, :1024],
                                       b.WIN2[:, (k % 2) * 1024:(k % 2) * 1024 + 1024],
                                       AF.Abs, accum_out=b.TMP[:, k:k + 1])
                    b.sS.inc(ins)
            assert b.sS.n == sbase + npieces
            b.tk["wp_abs"] = b.sS.tok()
            for j in range(4):
                b.win2_use[j] = b.tk["wp_abs"]

            @b.on("vector")
            def _(v, npieces=npieces):
                b.w(v, b.tk["wp_abs"])
                ins = v.reduce_sum(b.SCAL[:, 20:21], b.TMP[:, 0:npieces], AX.X)
                tok = b.sV.inc(ins)
                b.tk["wp_red"] = tok
                b.tk["wp_tmp_free"] = tok

            @b.on("tensor")
            def _(te):
                b.w(te, b.tk["wp_red"])
                b.bw(te, 4)
                ins = te.matmul(b.pb[4][0:1, 0:1], b.ONES[:, 0:1], b.SCAL[:, 20:21],
                                start=True, stop=True)
                b.tk["wp_sum"] = b.tdrain(te)

            @b.on("vector")
            def _(v, wi=wi, h_w=h_w):
                b.w(v, b.tk["wp_sum"])
                n_el = float(h_w.shape[0] * h_w.shape[1])
                r = b.ROW1[0:1, 0:1]
                b.ch(v, v.tensor_scalar(r, b.pb[4][0:1, 0:1],
                                        float(np.float32(1.0 / n_el)), None, ALU.mult))
                b.ch(v, v.tensor_scalar(r, r, 1e-5, None, ALU.max))
                ins = v.reciprocal(b.ROW1[0:1, 1:2], r)
                tok = b.sV.inc(ins)
                b.tk["wp_row"] = tok
                b.bfree(tok, 4)

            @b.on("tensor")
            def _(te, wi=wi):
                b.w(te, b.tk["wp_row"])
                b.bw(te, 4)
                ins = te.matmul(b.pb[4][:, 8:9], b.ONES[0:1, :], b.ROW1[0:1, 1:2],
                                start=True, stop=True)
                b.tk["wp_bc"] = b.tdrain(te)

            @b.on("scalar")
            def _(s, wi=wi):
                b.w(s, b.tk["wp_bc"])
                ins = s.copy(b.SCAL[:, wi:wi + 1], b.pb[4][:, 8:9])
                tok = b.sS.inc(ins)
                b.tk["sw"] = tok
                b.bfree(tok, 4)

            vbase = b.sV.n
            gbase = b.sG.n
            for k in range(npieces):
                c, p = divmod(k, npc)

                @b.on("sync")
                def _(s, k=k, c=c, p=p):
                    if k >= 2:
                        b.w(s, (b.sV, vbase + k - 1))
                    else:
                        b.w(s, b.tk["wp_abs"])
                    sm = b.sR[(k % 2) * 2]
                    sm.n += 16
                    s.dma_start(out=b.WIN2[:, (k % 2) * 1024:(k % 2) * 1024 + 1024],
                                in_=h_w[c * 128:(c + 1) * 128, p * 1024:(p + 1) * 1024]
                                ).then_inc(sm.h, 16)
                    b.tk["wq_dma"] = sm.tok()

                @b.on("vector")
                def _(v, k=k, wi=wi):
                    b.w(v, b.tk["wq_dma"], b.tk["sw"])
                    b.w(v, b.tk.get(f"wq_half{k % 2}"))
                    wsrc = b.WIN2[:, (k % 2) * 1024:(k % 2) * 1024 + 1024]
                    t_ = b.TMP[:, :1024]
                    u_ = b.STAT[:, :1024]
                    v.tensor_scalar(t_, wsrc, b.SCAL[:, wi:wi + 1], None, ALU.mult)
                    v.tensor_copy(TMPI[:, :1024], t_)
                    v.tensor_sub(u_, t_, TMPI[:, :1024])
                    v.scalar_tensor_tensor(t_, u_, 0.5, TMPI[:, :1024],
                                           ALU.is_gt, ALU.add)
                    v.scalar_tensor_tensor(u_, u_, -0.5, t_, ALU.is_ge, ALU.add)
                    v.tensor_scalar(t_, u_, -1.0, None, ALU.add)
                    ins = v.tensor_scalar(WQO[:, (k % 2) * 1024:(k % 2) * 1024 + 1024],
                                          t_, -1.0, 1.0, ALU.max, ALU.min)
                    b.sV.inc(ins)
                    b.tk["wq_q"] = b.sV.tok()

                @b.on("scalar")
                def _(g, k=k, c=c, p=p, h_q=h_q):
                    b.w(g, b.tk["wq_q"])
                    sm = b.sWQ if k % 2 == 0 else b.sXI
                    sm.n += 16
                    g.dma_start(out=h_q[c * 128:(c + 1) * 128, p * 1024:(p + 1) * 1024],
                                in_=WQO[:, (k % 2) * 1024:(k % 2) * 1024 + 1024]
                                ).then_inc(sm.h, 16)
                    b.tk[f"wq_half{k % 2}"] = sm.tok()
            assert b.sV.n == vbase + npieces
            for j in range(4):
                b.win2_use[j] = b.sV.tok()
        b.tk["wq_scratch"] = b.sWQ.tok()
        b.tk["wq_scratch2"] = b.sXI.tok()

        @b.on("vector")
        def _(v):
            b.w(v, b.tk["sw"])
            v.reciprocal(b.SCAL[:, 21:22], b.SCAL[:, 0:1])
            v.reciprocal(b.SCAL[:, 22:23], b.SCAL[:, 1:2])
            ins = v.reciprocal(b.SCAL[:, 23:24], b.SCAL[:, 2:3])
            b.tk["swinv"] = b.sV.inc(ins)

    def _ffn_tile(self, tt, TMPI, FSCR, XQT, XQ2, XQ2FM, WQB, WQB2, WQD, QC):
        b = self
        t0 = tt * 128
        g = tt // 4

        @b.on("tensor")
        def _(te):
            b.w(te, b.tk[f"ln0_{g}"])
            b.bw(te, 2)
            te.matmul(b.pb[2][:, 0:128], b.selbc(0, 64), b.rln_rhs(0, t0, 128),
                      start=True, stop=True)
            ins = te.matmul(b.pb[2][:, 128:256], b.selbc(32, 64), b.rln_rhs(0, t0, 128),
                            start=True, stop=True)
            b.tk["fq_bc"] = b.tdrain(te)

        @b.on("vector")
        def _(v):
            b.w(v, b.tk["fq_bc"])
            am = b.STAT[:, 0:128]
            v.memset(am, 0.0)
            ins = None
            for c in range(8):
                xn = FSCR[:, c * 128:(c + 1) * 128]
                v.tensor_sub(xn, b.ACC[:, c * T + t0:c * T + t0 + 128], b.pb[2][:, 0:128])
                v.tensor_mul(xn, xn, b.pb[2][:, 128:256])
                v.scalar_tensor_tensor(b.TMP[:, 0:128], xn, -1.0, am, ALU.mult, ALU.max)
                ins = v.scalar_tensor_tensor(am, xn, 1.0, b.TMP[:, 0:128],
                                             ALU.mult, ALU.max)
            tok = b.sV.inc(ins)
            b.tk["fq_am"] = tok
            b.bfree(tok, 2)

        @b.on("tensor")
        def _(te):
            b.w(te, b.tk["fq_am"])
            b.bw(te, 3)
            ins = te.transpose(b.pb[3][:, 0:128], b.STAT[:, 0:128], b.IDENT[:, :])
            b.tk["fq_tp"] = b.tdrain(te)

        @b.on("vector")
        def _(v):
            b.w(v, b.tk["fq_tp"], b.tk["swinv"])
            b.ch(v, v.tensor_reduce(b.STAT[:, 128:129], b.pb[3][:, 0:128],
                                    AX.X, ALU.max))
            b.ch(v, v.tensor_scalar(b.STAT[:, 129:130], b.STAT[:, 128:129],
                                    1e-5, None, ALU.max))
            b.ch(v, v.reciprocal(b.STAT[:, 130:131], b.STAT[:, 129:130]))
            b.ch(v, v.tensor_scalar(b.STAT[:, 131:132], b.STAT[:, 130:131],
                                    127.0, None, ALU.mult))
            b.ch(v, v.reciprocal(b.STAT[:, 132:133], b.STAT[:, 131:132]))
            v.tensor_scalar(QC[:, 0:1], b.STAT[:, 132:133], b.SCAL[:, 21:22], None, ALU.mult)
            ins = v.tensor_scalar(QC[:, 1:2], b.STAT[:, 132:133], b.SCAL[:, 22:23],
                                  None, ALU.mult)
            tok = b.sV.inc(ins)
            b.tk["fq_sx"] = tok
            b.bfree(tok, 3)

        @b.on("tensor")
        def _(te):
            b.w(te, b.tk["fq_sx"])
            b.bw(te, 3)
            ins = te.transpose(b.pb[3][0:1, 128:256], b.STAT[:, 131:132], b.IDENT[:, :])
            b.tk["fq_sxr_ps"] = b.tdrain(te)

        @b.on("scalar")
        def _(s):
            b.w(s, b.tk["fq_sxr_ps"])
            ins = s.copy(b.ROW1[0:1, 0:128], b.pb[3][0:1, 128:256])
            tok = b.sS.inc(ins)
            b.tk["fq_sxr"] = tok
            b.bfree(tok, 3)

        @b.on("tensor")
        def _(te):
            b.w(te, b.tk["fq_sxr"])
            b.bw(te, 3)
            ins = te.matmul(b.pb[3][:, 128:256], b.ONES[0:1, :], b.ROW1[0:1, 0:128],
                            start=True, stop=True)
            b.tk["fq_sxbc"] = b.tdrain(te)

        @b.on("vector")
        def _(v):
            b.w(v, b.tk["fq_sxbc"])
            b.w(v, b.tk.get("fg_mm7"))
            ins = None
            for c in range(8):
                xn = FSCR[:, c * 128:(c + 1) * 128]
                t_ = b.TMP[:, 0:128]
                u_ = b.STAT[:, 256:384]
                v.tensor_mul(t_, xn, b.pb[3][:, 128:256])
                v.tensor_copy(TMPI[:, 0:128], t_)
                v.tensor_sub(u_, t_, TMPI[:, 0:128])
                v.scalar_tensor_tensor(t_, u_, 0.5, TMPI[:, 0:128], ALU.is_gt, ALU.add)
                v.scalar_tensor_tensor(u_, u_, -0.5, t_, ALU.is_ge, ALU.add)
                ins = v.tensor_scalar(XQT[:, c * 128:(c + 1) * 128], u_,
                                      -1.0, None, ALU.add)
            tok = b.sV.inc(ins)
            b.tk["fq_xq"] = tok
            b.bfree(tok, 3)

        for hg in range(8):
            @b.on("sync")
            def _(s, hg=hg):
                b.w(s, b.tk["wq_scratch"], b.tk["wq_scratch2"])
                b.w(s, b.tk[f"fg_mm{hg - 1}"] if hg > 0 else b.tk.get("fg_mm7"))
                for c in range(8):
                    b.sB1.n += 16
                    s.dma_start(out=WQB[:, c * 512:(c + 1) * 512],
                                in_=self.h_wqg[c * 128:(c + 1) * 128,
                                               hg * 512:(hg + 1) * 512]).then_inc(b.sB1.h, 16)
                for c in range(8):
                    b.sB2.n += 16
                    s.dma_start(out=WQB2[:, c * 512:(c + 1) * 512],
                                in_=self.h_wqu[c * 128:(c + 1) * 128,
                                               hg * 512:(hg + 1) * 512]).then_inc(b.sB2.h, 16)
                b.tk["fg_wdma"] = b.sB1.tok()
                b.tk["fg_wdma2"] = b.sB2.tok()

            @b.on("tensor")
            def _(te, hg=hg):
                b.w(te, b.tk["fg_wdma"], b.tk["fg_wdma2"], b.tk["fq_xq"])
                b.bw(te, 0, 1)
                for c in range(8):
                    te.matmul(b.pb[0][:, :], XQT[:, c * 128:(c + 1) * 128],
                              WQB[:, c * 512:(c + 1) * 512],
                              start=(c == 0), stop=(c == 7))
                last = None
                for c in range(8):
                    last = te.matmul(b.pb[1][:, :], XQT[:, c * 128:(c + 1) * 128],
                                     WQB2[:, c * 512:(c + 1) * 512],
                                     start=(c == 0), stop=(c == 7))
                tok = b.tdrain(te)
                b.tk[f"fg_mm{hg}"] = tok
                if hg == 7:
                    b.tk["fg_mm7"] = tok

            @b.on("vector")
            def _(v, hg=hg):
                b.w(v, b.tk[f"fg_mm{hg}"])
                ins = v.tensor_scalar(b.TMP[:, 0:512], b.pb[0][:, :], QC[:, 0:1],
                                      None, ALU.mult)
                b.tk["fg_gd"] = b.sV.inc(ins)

            @b.on("scalar")
            def _(s, hg=hg):
                b.w(s, b.tk["fg_gd"])
                ins = s.activation(b.STAT[:, 512:1024], b.TMP[:, 0:512], AF.Silu)
                b.tk["fg_silu"] = b.sS.inc(ins)

            @b.on("vector")
            def _(v, hg=hg):
                b.w(v, b.tk["fg_silu"])
                gu = FSCR[:, hg * 512:(hg + 1) * 512]
                v.tensor_scalar(b.TMP[:, 512:1024], b.pb[1][:, :], QC[:, 1:2],
                                None, ALU.mult)
                v.tensor_mul(gu, b.STAT[:, 512:1024], b.TMP[:, 512:1024])
                ins = v.tensor_reduce(b.STAT[:, 133 + hg:134 + hg], gu, AX.X,
                                      ALU.max, apply_absolute_value=True)
                tok = b.sV.inc(ins)
                b.tk[f"fg_ev{hg}"] = tok
                b.bfree(tok, 0, 1)

        @b.on("vector")
        def _(v):
            b.w(v, b.tk["fg_ev7"])
            b.ch(v, v.tensor_reduce(b.STAT[:, 141:142], b.STAT[:, 133:141],
                                    AX.X, ALU.max))
            b.ch(v, v.tensor_scalar(b.STAT[:, 142:143], b.STAT[:, 141:142],
                                    1e-5, None, ALU.max))
            b.ch(v, v.reciprocal(b.STAT[:, 143:144], b.STAT[:, 142:143]))
            b.ch(v, v.tensor_scalar(b.STAT[:, 144:145], b.STAT[:, 143:144],
                                    127.0, None, ALU.mult))
            b.ch(v, v.reciprocal(b.STAT[:, 145:146], b.STAT[:, 144:145]))
            t_ = b.TMP[:, 0:1024]
            ins = None
            for q in range(4):
                sl = slice(q * 1024, (q + 1) * 1024)
                v.tensor_scalar(t_, FSCR[:, sl], b.STAT[:, 144:145], None, ALU.mult)
                v.tensor_copy(TMPI[:, :1024], t_)
                u_ = FSCR[:, sl]
                v.tensor_sub(u_, t_, TMPI[:, :1024])
                v.scalar_tensor_tensor(t_, u_, 0.5, TMPI[:, :1024], ALU.is_gt, ALU.add)
                v.scalar_tensor_tensor(u_, u_, -0.5, t_, ALU.is_ge, ALU.add)
                ins = v.tensor_scalar(XQ2[:, sl], u_, -1.0, None, ALU.add)
            b.tk["f2_xq2"] = b.sV.inc(ins)

        @b.on("tensor")
        def _(te):
            b.w(te, b.tk["f2_xq2"])
            b.bw(te, 2)
            ins = te.transpose(b.pb[2][0:1, 0:128], b.STAT[:, 145:146], b.IDENT[:, :])
            b.tk["f2_qr_ps"] = b.tdrain(te)

        @b.on("scalar")
        def _(s):
            b.w(s, b.tk["f2_qr_ps"])
            ins = s.copy(b.ROW1[0:1, 128:256], b.pb[2][0:1, 0:128])
            tok = b.sS.inc(ins)
            b.tk["f2_qr"] = tok
            b.bfree(tok, 2)

        @b.on("vector")
        def _(v):
            b.w(v, b.tk["f2_qr"])
            ins = v.tensor_scalar(b.ROW1[0:1, 256:384], b.ROW1[0:1, 128:256],
                                  b.SCAL[0:1, 23:24], None, ALU.mult)
            b.tk["f2_c2r"] = b.sV.inc(ins)

        for q in range(4):
            @b.on("tensor")
            def _(te, q=q):
                b.w(te, b.tk["f2_xq2"])
                b.bw(te, 4, 5)
                last = None
                for j in range(8):
                    c2 = q * 8 + j
                    bank = b.pb[4 + (j // 4)]
                    last = te.transpose(bank[:, (j % 4) * 128:(j % 4 + 1) * 128],
                                        XQ2[:, c2 * 128:(c2 + 1) * 128], b.IDENT[:, :])
                b.tk[f"f2_tp{q}"] = b.tdrain(te)

            @b.on("scalar")
            def _(s, q=q):
                b.w(s, b.tk[f"f2_tp{q}"])
                ins = None
                for j in range(8):
                    c2 = q * 8 + j
                    bank = b.pb[4 + (j // 4)]
                    ins = s.copy(XQ2FM[:, c2 * 128:(c2 + 1) * 128],
                                 bank[:, (j % 4) * 128:(j % 4 + 1) * 128])
                tok = b.sS.inc(ins)
                b.tk[f"f2_tpc{q}"] = tok
                b.bfree(tok, 4, 5)

        if DEBUG and tt == 0:
            @b.on("gpsimd")
            def _(g):
                b.w(g, b.tk["fq_am"])
                g.dma_start(out=b.h_dbgxn, in_=FSCR[:, 0:1024]).then_inc(b.sO.h, 16)
                b.sO.n += 16
                b.tk["dbgxn"] = b.sO.tok()

            @b.on("vector")
            def _(v):
                b.w(v, b.tk["dbgxn"])

            @b.on("gpsimd")
            def _(g):
                b.w(g, b.tk["fq_sx"])
                g.dma_start(out=b.h_dbgst, in_=b.STAT[:, 130:146]).then_inc(b.sO.h, 16)
                b.sO.n += 16
                b.tk["dbgst"] = b.sO.tok()

            @b.on("vector")
            def _(v):
                b.w(v, b.tk["dbgst"])

            @b.on("gpsimd")
            def _(g):
                b.w(g, b.tk["fq_xq"], b.tk["fg_ev7"], b.tk["f2_xq2"],
                    b.tk["f2_tpc3"], b.tk["f2_c2r"], b.tk["fq_sx"])
                for dst, src in ((b.h_dbgxq, XQT[:, :]),
                                 (b.h_dbggu, FSCR[:, :]),
                                 (b.h_dbgxq2, XQ2[:, :]),
                                 (b.h_dbgfm, XQ2FM[:, :]),
                                 (b.h_dbgrow, b.ROW1[0:1, :]),
                                 (b.h_dbgqc, QC[:, :])):
                    g.dma_start(out=dst, in_=src).then_inc(b.sO.h, 16)
                    b.sO.n += 16
                b.tk["dbgf"] = b.sO.tok()

            @b.on("vector")
            def _(v):
                b.w(v, b.tk["dbgf"])

        tbase = b.sT.n
        for c2 in range(32):
            @b.on("sync")
            def _(s, c2=c2):
                if c2 >= 2:
                    b.w(s, (b.sT, tbase + c2 - 1))
                else:
                    b.w(s, (b.sT, tbase))
                sm = b.sD0 if c2 % 2 == 0 else b.sD1
                sm.n += 16
                s.dma_start(out=WQD[:, (c2 % 2) * 1024:(c2 % 2) * 1024 + 1024],
                            in_=self.h_wqd[c2 * 128:(c2 + 1) * 128, :]).then_inc(sm.h, 16)
                b.tk["fd_wdma"] = sm.tok()

            @b.on("tensor")
            def _(te, c2=c2):
                b.w(te, b.tk["fd_wdma"], b.tk["f2_tpc3"])
                if c2 == 0:
                    b.bw(te, 6, 7)
                last = None
                for dc in range(8):
                    bank = b.pb[6 + (dc // 4)]
                    last = te.matmul(bank[:, (dc % 4) * 128:(dc % 4 + 1) * 128],
                                     WQD[:, (c2 % 2) * 1024 + dc * 128:
                                         (c2 % 2) * 1024 + (dc + 1) * 128],
                                     XQ2FM[:, c2 * 128:(c2 + 1) * 128],
                                     start=(c2 == 0 and dc % 4 == 0),
                                     stop=(c2 == 31 and dc % 4 == 3))
                b.sT.inc(last)
                if c2 == 31:
                    b.tk["fd_mm"] = b.tdrain(te)
        assert b.sT.n == tbase + 33

        @b.on("tensor")
        def _(te):
            b.w(te, b.tk["f2_c2r"])
            b.bw(te, 2)
            ins = te.matmul(b.pb[2][:, 128:256], b.ONES[0:1, :], b.ROW1[0:1, 256:384],
                            start=True, stop=True)
            b.tk["fd_bc"] = b.tdrain(te)

        @b.on("scalar")
        def _(s):
            b.w(s, b.tk["fd_bc"])
            ins = s.copy(b.TMP[:, 896:1024], b.pb[2][:, 128:256])
            tok = b.sS.inc(ins)
            b.tk["fd_bcs"] = tok
            b.bfree(tok, 2)

        @b.on("vector")
        def _(v):
            b.w(v, b.tk["fd_mm"], b.tk["fd_bcs"])
            ins = None
            for dc in range(8):
                bank = b.pb[6 + (dc // 4)]
                t_ = b.TMP[:, 0:128]
                v.tensor_mul(t_, bank[:, (dc % 4) * 128:(dc % 4 + 1) * 128],
                             b.TMP[:, 896:1024])
                ins = v.tensor_add(b.ACC[:, dc * T + t0:dc * T + t0 + 128],
                                   b.ACC[:, dc * T + t0:dc * T + t0 + 128], t_)
            tok = b.sV.inc(ins)
            b.tk["fd_acc"] = tok
            b.bfree(tok, 6, 7)

    def attn_pass(self, dr):
        b = self
        nc = self.nc
        with ExitStack() as astk:
            asb = lambda name, shape, dt=F32: astk.enter_context(
                nc.sbuf_tensor(name, shape, dt))
            XB = asb(f"XB{dr}", [128, 16 * 264])
            DT = asb(f"DT{dr}", [128, 16 * 256])
            XCD = asb(f"XCD{dr}", [128, 16 * 256])
            Y = asb(f"Y{dr}", [128, 16 * 256])
            XIN = asb(f"XIN{dr}", [128, 8 * 256])
            XDBL = asb(f"XDBL{dr}", [128, 256])
            ADA = asb(f"ADA{dr}", [128, 8 * 256])
            DBX = asb(f"DBX{dr}", [128, 256])
            HS = asb(f"HS{dr}", [128, 256])
            HCAR = asb(f"HCAR{dr}", [128, 256])
            XPT = asb(f"XPT{dr}", [128, 16 * 96])

            @b.on("sync")
            def _(s):
                b.w(s, b.tk.get(f"apdone{dr - 1}"))
                b.sX.n += 16
                s.dma_start(out=XPT[:, :], in_=bass.AP(
                    b.h_xpT.tensor, dr * DI * 96, [[96, 128], [128 * 96, 16], [1, 96]])
                ).then_inc(b.sX.h, 16)
                b.tk["ap_w"] = b.sX.tok()

            @b.on("vector")
            def _(v):
                b.w(v, b.tk["ffn"])
                v.memset(HCAR[:, :], 0.0)
                ins = None
                for cc in range(16):
                    ins = v.memset(XB[:, cc * 264 + 1:cc * 264 + 4], 0.0)
                b.tk["ap_init"] = b.sV.inc(ins)

            for tau in range(NTC):
                self._attn_chunk(dr, tau, XB, DT, XCD, Y, XIN, XDBL,
                                 ADA, DBX, HS, HCAR, XPT)
            b.tk[f"apdone{dr}"] = b.sV.tok()

    def _attn_chunk(self, dr, tau, XB, DT, XCD, Y, XIN, XDBL,
                    ADA, DBX, HS, HCAR, XPT):
        b = self
        o0 = tau * TC if dr == 0 else T - TC * (tau + 1)
        vc0 = dr * 16

        def xstg_rhs(c):
            if dr == 0:
                return XIN[:, c * TC:(c + 1) * TC]
            return bass.AP(XIN, c * TC + TC - 1, [[8 * TC, 128], [-1, TC]])

        @b.on("scalar")
        def _(g):
            b.w(g, b.tk.get(f"ap_xstg_done{dr}_{tau - 1}"))
            for c in range(8):
                b.sXI.n += 16
                g.dma_start(out=XIN[:, c * TC:(c + 1) * TC],
                            in_=b.h_x[c * 128:(c + 1) * 128, o0:o0 + TC]
                            ).then_inc(b.sXI.h, 16)
            b.tk["ap_xin"] = b.sXI.tok()

        @b.on("tensor")
        def _(te):
            b.w(te, b.tk["ln0"])
            b.bw(te, 2)
            te.matmul(b.pb[2][:, 0:TC], b.selbc(0, 64),
                      b.rln_rhs(0, o0, TC), start=True, stop=True)
            ins = te.matmul(b.pb[2][:, TC:2 * TC], b.selbc(32, 64),
                            b.rln_rhs(0, o0, TC), start=True, stop=True)
            b.tk["ap_msbc"] = b.tdrain(te)

        @b.on("vector")
        def _(v):
            b.w(v, b.tk["ap_msbc"], b.tk["ap_xin"], b.tk["ap_init"])
            b.w(v, b.tk.get(f"ap_xstg_use{dr}_{tau - 1}"))
            ins = None
            for c in range(8):
                xs = XIN[:, c * TC:(c + 1) * TC]
                v.tensor_sub(xs, xs, b.pb[2][:, 0:TC])
                ins = v.tensor_mul(xs, xs, b.pb[2][:, TC:2 * TC])
            tok = b.sV.inc(ins)
            b.tk["ap_xstg"] = tok
            b.tk[f"ap_xstg_done{dr}_{tau}"] = tok
            b.bfree(tok, 2)

        for mw in range(4):
            self._inproj_mw(dr, tau, mw, xstg_rhs, XB, None, is_z=False)

        for cc in range(16):
            @b.on("vector")
            def _(v, cc=cc):
                b.w(v, b.tk[f"ap_evac{dr}_{tau}_{cc // 4}"])
                b.w(v, b.tk.get(f"ap_silu{dr}_{tau}_{cc - 4}"))
                x0 = cc * 264
                t_ = b.TMP[:, (cc % 4) * 256:(cc % 4) * 256 + 256]
                vb = (vc0 + cc) * 8
                v.tensor_scalar(t_, XB[:, x0 + 1:x0 + 257],
                                b.VEC0[:, vb:vb + 1], b.VEC0[:, vb + 4:vb + 5],
                                ALU.mult, ALU.add)
                for k in (1, 2, 3):
                    v.scalar_tensor_tensor(t_, XB[:, x0 + 1 + k:x0 + 257 + k],
                                           b.VEC0[:, vb + k:vb + k + 1], t_,
                                           ALU.mult, ALU.add)
                ins = v.tensor_copy(XB[:, x0 + 1:x0 + 4], XB[:, x0 + 257:x0 + 260])
                b.tk[f"ap_conv{dr}_{tau}_{cc}"] = b.sV.inc(ins)

            @b.on("scalar")
            def _(s, cc=cc):
                b.w(s, b.tk[f"ap_conv{dr}_{tau}_{cc}"])
                t_ = b.TMP[:, (cc % 4) * 256:(cc % 4) * 256 + 256]
                ins = s.activation(XCD[:, cc * 256:(cc + 1) * 256], t_, AF.Silu)
                b.tk[f"ap_silu{dr}_{tau}_{cc}"] = b.sS.inc(ins)

        @b.on("tensor")
        def _(te):
            b.w(te, b.tk["ap_w"])
            b.bw(te, 4)
            last = None
            for cc in range(16):
                b.w(te, b.tk[f"ap_silu{dr}_{tau}_{cc}"])
                last = te.matmul(b.pb[4][0:96, 0:TC], XPT[:, cc * 96:(cc + 1) * 96],
                                 XCD[:, cc * 256:(cc + 1) * 256],
                                 start=(cc == 0), stop=(cc == 15))
            b.tk["ap_xp"] = b.tdrain(te)

        @b.on("scalar")
        def _(s):
            b.w(s, b.tk["ap_xp"])
            ins = s.copy(XDBL[0:96, :], b.pb[4][0:96, 0:TC])
            tok = b.sS.inc(ins)
            b.tk["ap_xdbl"] = tok
            b.bfree(tok, 4)

        @b.on("sync")
        def _(s):
            b.w(s, b.win2_use[0], b.win2_use[1], b.win2_use[2], b.win2_use[3])
            b.sR[0].n += 16
            s.dma_start(out=b.WIN2[0:64, :], in_=b.h_dtT[dr * 64:(dr + 1) * 64, :]
                        ).then_inc(b.sR[0].h, 16)
            b.tk["ap_dtw"] = b.sR[0].tok()

        for mq in range(4):
            @b.on("tensor")
            def _(te, mq=mq):
                b.w(te, b.tk["ap_xdbl"], b.tk["ap_dtw"])
                b.bw(te, 0, 1, 2, 3)
                last = None
                for j in range(4):
                    m = mq * 4 + j
                    last = te.matmul(b.pb[j][:, 0:TC],
                                     b.WIN2[0:64, m * 128:(m + 1) * 128],
                                     XDBL[0:64, 0:TC], start=True, stop=True)
                tok = b.tdrain(te)
                b.tk[f"ap_dtmm{dr}_{tau}_{mq}"] = tok
                if mq == 3:
                    for j in range(4):
                        b.win2_use[j] = tok

            @b.on("scalar")
            def _(s, mq=mq):
                b.w(s, b.tk[f"ap_dtmm{dr}_{tau}_{mq}"])
                ins = None
                for j in range(4):
                    m = mq * 4 + j
                    vb = (vc0 + m) * 8
                    t_ = b.TMP[:, 512 + (j % 2) * 256:512 + (j % 2) * 256 + 256]
                    s.activation(t_, b.pb[j][:, 0:TC], AF.Exp,
                                 bias=b.VEC0[:, vb + 5:vb + 6], scale=1.0)
                    ins = s.activation(DT[:, m * 256:(m + 1) * 256], t_, AF.Ln,
                                       bias=1.0, scale=1.0)
                tok = b.sS.inc(ins)
                b.tk[f"ap_dtev{dr}_{tau}_{mq}"] = tok
                b.bfree(tok, 0, 1, 2, 3)

        @b.on("vector")
        def _(v):
            b.w(v, b.tk["ap_xp"])
            for mq in range(4):
                b.w(v, b.tk[f"ap_dtev{dr}_{tau}_{mq}"])
            b.w(v, b.tk.get(f"ap_opmm{dr}_{tau - 1}_1_15"))
            for cc in range(16):
                vb = (vc0 + cc) * 8
                v.tensor_scalar(Y[:, cc * 256:(cc + 1) * 256],
                                XCD[:, cc * 256:(cc + 1) * 256],
                                b.VEC0[:, vb + 6:vb + 7], None, ALU.mult)
            ins = None
            for cc in range(16):
                ins = v.tensor_mul(XCD[:, cc * 256:(cc + 1) * 256],
                                   DT[:, cc * 256:(cc + 1) * 256],
                                   XCD[:, cc * 256:(cc + 1) * 256])
            b.tk["ap_dtx"] = b.sV.inc(ins)

        for n in range(NST):
            bank = b.pb[4 + (n % 2)]

            @b.on("tensor")
            def _(te, n=n, bank=bank):
                b.w(te, b.tk["ap_xdbl"])
                b.bw(te, 4 + (n % 2))
                te.matmul(bank[:, 0:TC], b.selbc(64 + n, 96),
                          XDBL[0:96, 0:TC], start=True, stop=True)
                te.matmul(bank[:, TC:2 * TC], b.selbc(80 + n, 96),
                          XDBL[0:96, 0:TC], start=True, stop=True)
                b.tk[f"ap_bc{dr}_{tau}_{n}"] = b.tdrain(te)

            @b.on("scalar")
            def _(s, n=n, bank=bank):
                b.w(s, b.tk[f"ap_bc{dr}_{tau}_{n}"])
                b.w(s, b.tk["ap_dtx"] if n == 0 else b.tk[f"ap_scv{dr}_{tau}_{n - 1}_1"])
                s.copy(b.STAT[:, (n % 2) * 512:(n % 2) * 512 + 256], bank[:, 0:TC])
                ins = s.copy(b.STAT[:, (n % 2) * 512 + 256:(n % 2) * 512 + 512],
                             bank[:, TC:2 * TC])
                tok = b.sS.inc(ins)
                b.tk[f"ap_bccp{dr}_{tau}_{n}"] = tok
                b.bfree(tok, 4 + (n % 2))

            for h in range(2):
                @b.on("scalar")
                def _(s, n=n, h=h):
                    b.w(s, b.tk["aprep"])
                    if h == 1:
                        b.w(s, b.tk[f"ap_scv{dr}_{tau}_{n}_0"])
                    ins = None
                    for j in range(8):
                        cc = h * 8 + j
                        ins = s.activation(ADA[:, j * 256:(j + 1) * 256],
                                           DT[:, cc * 256:(cc + 1) * 256], AF.Exp,
                                           bias=0.0,
                                           scale=b.A0SB[:, (vc0 + cc) * NST + n:
                                                        (vc0 + cc) * NST + n + 1])
                    b.tk[f"ap_da{dr}_{tau}_{n}_{h}"] = b.sS.inc(ins)

                @b.on("vector")
                def _(v, n=n, h=h):
                    b.w(v, b.tk[f"ap_da{dr}_{tau}_{n}_{h}"],
                        b.tk[f"ap_bccp{dr}_{tau}_{n}"], b.tk["ap_dtx"])
                    Bbc = b.STAT[:, (n % 2) * 512:(n % 2) * 512 + 256]
                    Cbc = b.STAT[:, (n % 2) * 512 + 256:(n % 2) * 512 + 512]
                    ins = None
                    for j in range(8):
                        cc = h * 8 + j
                        q = cc * NST + n
                        v.tensor_mul(DBX[:, :], XCD[:, cc * 256:(cc + 1) * 256], Bbc)
                        v.tensor_tensor_scan(HS[:, :], ADA[:, j * 256:(j + 1) * 256],
                                             DBX[:, :], HCAR[:, q:q + 1],
                                             ALU.mult, ALU.add)
                        v.tensor_mul(DBX[:, :], HS[:, :], Cbc)
                        v.tensor_add(Y[:, cc * 256:(cc + 1) * 256],
                                     Y[:, cc * 256:(cc + 1) * 256], DBX[:, :])
                        ins = v.tensor_copy(HCAR[:, q:q + 1], HS[:, 255:256])
                    b.tk[f"ap_scv{dr}_{tau}_{n}_{h}"] = b.sV.inc(ins)
        b.tk[f"ap_scan{dr}_{tau}"] = b.sV.tok()

        for mw in range(4, 8):
            self._inproj_mw(dr, tau, mw, xstg_rhs, XB, Y, is_z=True)
        b.tk["ap_ydone"] = b.sV.tok()
        b.tk[f"ap_xstg_use{dr}_{tau}"] = b.sT.tok()

        for ow in range(2):
            for c in range(16):
                slot = c % 4

                @b.on("sync")
                def _(s, ow=ow, c=c, slot=slot):
                    b.w(s, b.win2_use[slot])
                    b.sR[slot].n += 16
                    s.dma_start(out=b.WIN2[:, slot * 512:slot * 512 + 512],
                                in_=b.h_outT[c * 128:(c + 1) * 128,
                                             ow * 512:(ow + 1) * 512]).then_inc(b.sR[slot].h, 16)
                    b.tk[f"ap_opw{dr}_{tau}_{ow}_{c}"] = b.sR[slot].tok()

                @b.on("tensor")
                def _(te, ow=ow, c=c, slot=slot):
                    b.w(te, b.tk[f"ap_opw{dr}_{tau}_{ow}_{c}"])
                    if c == 0:
                        b.w(te, b.tk["ap_ydone"])
                        b.bw(te, 6, 7)
                    last = None
                    for m in range(4):
                        bank = b.pb[6 + (m // 2)]
                        last = te.matmul(bank[:, (m % 2) * 256:(m % 2) * 256 + 256],
                                         b.WIN2[:, slot * 512 + m * 128:
                                                slot * 512 + (m + 1) * 128],
                                         Y[:, c * 256:(c + 1) * 256],
                                         start=(c == 0 and m % 2 == 0),
                                         stop=(c == 15 and m % 2 == 1))
                    tok = b.tdrain(te) if c == 15 else b.sT.inc(last)
                    b.tk[f"ap_opmm{dr}_{tau}_{ow}_{c}"] = tok
                    b.win2_use[slot] = tok
                    if ow == 1 and c == 15:
                        b.tk[f"ap_opmm{dr}_{tau}_1_15"] = tok

            @b.on("vector")
            def _(v, ow=ow):
                b.w(v, b.tk[f"ap_opmm{dr}_{tau}_{ow}_{15}"])
                ins = None
                for m in range(4):
                    mc = ow * 4 + m
                    bank = b.pb[6 + (m // 2)]
                    dst = b.ACC[:, mc * T + o0:mc * T + o0 + TC]
                    if dr == 0:
                        src = bank[:, (m % 2) * 256:(m % 2) * 256 + 256]
                    else:
                        src = bass.AP(bank, (m % 2) * 256 + 255, [[512, 128], [-1, 256]])
                    ins = v.tensor_add(dst, dst, src)
                tok = b.sV.inc(ins)
                b.tk[f"ap_opev{dr}_{tau}_{ow}"] = tok
                b.bfree(tok, 6, 7)

    def _inproj_mw(self, dr, tau, mw, xstg_rhs, XB, Y, is_z):
        b = self
        for c in range(8):
            slot = c % 4

            @b.on("sync")
            def _(s, mw=mw, c=c, slot=slot):
                b.w(s, b.win_use[slot])
                b.sQ[slot].n += 16
                s.dma_start(out=b.WIN[:, slot * 512:slot * 512 + 512],
                            in_=b.h_inT[c * 128:(c + 1) * 128,
                                        mw * 512:(mw + 1) * 512]).then_inc(b.sQ[slot].h, 16)
                b.tk[f"ap_ipw{dr}_{tau}_{mw}_{c}"] = b.sQ[slot].tok()

            @b.on("tensor")
            def _(te, mw=mw, c=c, slot=slot):
                b.w(te, b.tk[f"ap_ipw{dr}_{tau}_{mw}_{c}"])
                if c == 0:
                    b.w(te, b.tk["ap_xstg"])
                    b.bw(te, 0, 1, 2, 3)
                    if is_z:
                        b.w(te, b.tk[f"ap_scan{dr}_{tau}"])
                last = None
                for m in range(4):
                    last = te.matmul(b.pb[m][:, (mw % 2) * 256:(mw % 2) * 256 + 256],
                                     b.WIN[:, slot * 512 + m * 128:
                                            slot * 512 + (m + 1) * 128],
                                     xstg_rhs(c),
                                     start=(c == 0), stop=(c == 7))
                tok = b.tdrain(te) if c == 7 else b.sT.inc(last)
                b.win_use[slot] = tok
                if c == 7:
                    b.tk[f"ap_ipmm{dr}_{tau}_{mw}"] = tok

        if not is_z:
            @b.on("scalar")
            def _(s, mw=mw):
                b.w(s, b.tk[f"ap_ipmm{dr}_{tau}_{mw}"])
                b.w(s, b.tk.get(f"ap_conv{dr}_{tau - 1}_{mw * 4 + 3}"))
                ins = None
                for m in range(4):
                    cc = mw * 4 + m
                    ins = s.copy(XB[:, cc * 264 + 4:cc * 264 + 260],
                                 b.pb[m][:, (mw % 2) * 256:(mw % 2) * 256 + 256])
                tok = b.sS.inc(ins)
                b.tk[f"ap_evac{dr}_{tau}_{mw}"] = tok
                b.bfree(tok, 0, 1, 2, 3)
        else:
            @b.on("scalar")
            def _(s, mw=mw):
                b.w(s, b.tk[f"ap_ipmm{dr}_{tau}_{mw}"])
                ins = None
                for m in range(4):
                    ins = s.activation(b.TMP[:, m * 256:(m + 1) * 256],
                                       b.pb[m][:, (mw % 2) * 256:(mw % 2) * 256 + 256],
                                       AF.Silu)
                tok = b.sS.inc(ins)
                b.tk[f"ap_zsilu{dr}_{tau}_{mw}"] = tok
                b.bfree(tok, 0, 1, 2, 3)

            @b.on("vector")
            def _(v, mw=mw):
                b.w(v, b.tk[f"ap_zsilu{dr}_{tau}_{mw}"])
                ins = None
                for m in range(4):
                    cc = (mw - 4) * 4 + m
                    ins = v.tensor_mul(Y[:, cc * 256:(cc + 1) * 256],
                                       Y[:, cc * 256:(cc + 1) * 256],
                                       b.TMP[:, m * 256:(m + 1) * 256])
                b.tk[f"ap_zg{dr}_{tau}_{mw}"] = b.sV.inc(ins)

    def layer1(self):
        b = self
        nc = self.nc
        with ExitStack() as lstk:
            lsb = lambda name, shape, dt=F32: lstk.enter_context(
                nc.sbuf_tensor(name, shape, dt))
            X1P = lsb("LX1P", [128, 8 * 512])
            DT1 = lsb("LDT1", [128, 8 * 512])
            Y1 = lsb("LY1", [128, 8 * 512])
            XN1G = lsb("LXN1G", [128, 8 * 512])
            XD1B = lsb("LXD1B", [128, 512])
            XD1C = lsb("LXD1C", [128, 512])
            ADA1 = lsb("LADA1", [128, 4 * 512])
            DBX1 = lsb("LDBX1", [128, 512])
            HS1 = lsb("LHS1", [128, 512])
            HC1 = lsb("LHC1", [128, 128])
            XG = lsb("LXG", [128, 8 * 256])
            GBC = lsb("LGBC", [128, 256])
            HE = lsb("LHE", [128, 16 * 256])

            @b.on("vector")
            def _(v):
                b.w(v, b.tk["ln1"])
                ins = v.memset(HC1[:, :], 0.0)
                b.tk["l1_init"] = b.sV.inc(ins)

            for g in range(4):
                self._gating_group(g)
                self._xn1_group(g, XN1G)
                self._l1attn_group(g, X1P, DT1, Y1, XN1G, XD1B, XD1C,
                                   ADA1, DBX1, HS1, HC1)
                for sg in range(2):
                    self._moe_subgroup(g, sg, XN1G, XG, GBC, HE)

    def _gating_group(self, g):
        b = self
        s0 = g * 512

        @b.on("tensor")
        def _(te):
            b.w(te, b.tk["ln1"], b.tk["wsume"])
            b.bw(te, 2, 3, 4)
            for c in range(8):
                te.matmul(b.pb[2][0:8, 0:512], b.GATEW[:, c * 8:(c + 1) * 8],
                          b.ACC[:, c * T + s0:c * T + s0 + 512],
                          start=(c == 0), stop=(c == 7))
            te.matmul(b.pb[3][:, 0:512], b.selbc(0, 64), b.rln_rhs(1, s0, 512),
                      start=True, stop=True)
            ins = te.matmul(b.pb[4][:, 0:512], b.selbc(32, 64), b.rln_rhs(1, s0, 512),
                            start=True, stop=True)
            b.tk["gt_mm"] = b.tdrain(te)

        @b.on("vector")
        def _(v):
            b.w(v, b.tk["gt_mm"])
            GL = b.STAT[0:8, 0:512]
            v.tensor_copy(GL, b.pb[2][0:8, 0:512])
            v.scalar_tensor_tensor(GL, b.pb[3][0:8, 0:512], b.SCAL[0:8, 16:17],
                                   GL, ALU.mult, ALU.subtract)
            v.tensor_mul(GL, GL, b.pb[4][0:8, 0:512])
            ins = v.tensor_scalar(GL, GL, -1.0, None, ALU.mult)
            tok = b.sV.inc(ins)
            b.tk["gt_log"] = tok
            b.bfree(tok, 2, 3, 4)

        for j in range(4):
            @b.on("tensor")
            def _(te, j=j):
                b.w(te, b.tk["gt_log"])
                b.bw(te, 5)
                ins = te.transpose(b.pb[5][:, 0:8], b.STAT[0:8, j * 128:(j + 1) * 128],
                                   b.IDENT[0:8, 0:8])
                b.tk["gt_tp"] = b.tdrain(te)

            @b.on("scalar")
            def _(s, j=j):
                b.w(s, b.tk["gt_tp"])
                b.w(s, b.tk.get("gt_gtok_free"))
                ins = s.copy(b.GTOK[:, 0:8], b.pb[5][:, 0:8])
                tok = b.sS.inc(ins)
                b.tk["gt_tok"] = tok
                b.bfree(tok, 5)

            @b.on("vector")
            def _(v, j=j):
                b.w(v, b.tk["gt_tok"])
                ins = v.reduce_max(b.SCAL[:, 24:25], b.GTOK[:, 0:8], AX.X, negate=True)
                b.tk["gt_nm"] = b.sV.inc(ins)

            @b.on("scalar")
            def _(s, j=j):
                b.w(s, b.tk["gt_nm"])
                ins = s.activation(b.GTOK[:, 0:8], b.GTOK[:, 0:8], AF.Exp,
                                   bias=b.SCAL[:, 24:25], scale=1.0)
                b.tk["gt_exp"] = b.sS.inc(ins)

            @b.on("vector")
            def _(v, j=j):
                b.w(v, b.tk["gt_exp"])
                P = b.GTOK[:, 0:8]
                sc = b.SCAL
                b.ch(v, v.reduce_sum(sc[:, 25:26], P, AX.X))
                b.ch(v, v.reciprocal(sc[:, 26:27], sc[:, 25:26]))
                b.ch(v, v.tensor_scalar(P, P, sc[:, 26:27], None, ALU.mult))
                b.ch(v, v.reduce_max(sc[:, 27:28], P, AX.X))
                b.ch(v, v.scalar_tensor_tensor(b.TMP[:, 0:8], P, sc[:, 27:28], P,
                                               ALU.is_lt, ALU.mult))
                b.ch(v, v.reduce_max(sc[:, 28:29], b.TMP[:, 0:8], AX.X))
                b.ch(v, v.scalar_tensor_tensor(P, P, sc[:, 28:29], P,
                                               ALU.is_ge, ALU.mult))
                b.ch(v, v.reduce_sum(sc[:, 29:30], P, AX.X))
                b.ch(v, v.tensor_scalar(sc[:, 29:30], sc[:, 29:30], 1e-12,
                                        None, ALU.max))
                b.ch(v, v.reciprocal(sc[:, 30:31], sc[:, 29:30]))
                ins = v.tensor_scalar(P, P, sc[:, 30:31], None, ALU.mult)
                b.tk["gt_p"] = b.sV.inc(ins)

            @b.on("tensor")
            def _(te, j=j):
                b.w(te, b.tk["gt_p"])
                b.bw(te, 5)
                ins = te.transpose(b.pb[5][0:8, 128:256], b.GTOK[:, 0:8],
                                   b.IDENT[:, :])
                b.tk["gt_ptp"] = b.tdrain(te)

            @b.on("scalar")
            def _(s, j=j, g=g):
                b.w(s, b.tk["gt_ptp"])
                ins = s.copy(b.GROWS[0:8, s0 + j * 128:s0 + (j + 1) * 128],
                             b.pb[5][0:8, 128:256])
                tok = b.sS.inc(ins)
                b.bfree(tok, 5)
                b.tk["gt_gtok_free"] = tok
        b.tk[f"gt_done{g}"] = b.sS.tok()

    def _xn1_group(self, g, XN1G):
        b = self
        s0 = g * 512

        @b.on("tensor")
        def _(te):
            b.w(te, b.tk["ln1"])
            b.bw(te, 2, 3)
            te.matmul(b.pb[2][:, 0:512], b.selbc(0, 64), b.rln_rhs(1, s0, 512),
                      start=True, stop=True)
            ins = te.matmul(b.pb[3][:, 0:512], b.selbc(32, 64), b.rln_rhs(1, s0, 512),
                            start=True, stop=True)
            b.tk["xn1_bc"] = b.tdrain(te)

        @b.on("vector")
        def _(v, g=g):
            b.w(v, b.tk["xn1_bc"], b.tk.get(f"xn1_cons{g - 1}"))
            ins = None
            for c in range(8):
                xs = XN1G[:, c * 512:(c + 1) * 512]
                v.tensor_sub(xs, b.ACC[:, c * T + s0:c * T + s0 + 512], b.pb[2][:, 0:512])
                ins = v.tensor_mul(xs, xs, b.pb[3][:, 0:512])
            tok = b.sV.inc(ins)
            b.tk[f"xn1_{g}"] = tok
            b.bfree(tok, 2, 3)

    def _l1attn_group(self, g, X1P, DT1, Y1, XN1G, XD1B, XD1C,
                      ADA1, DBX1, HS1, HC1):
        b = self
        s0 = g * 512

        for mw in range(2):
            self._l1_proj_mw(g, b.h_wxz, mw * 512, mw, XN1G, "xz", X1P,
                             kind="copy", rdy=b.tk[f"xn1_{g}"])
        b.tk[f"l1x_xz{g}"] = b.sS.tok()
        for mw in range(2):
            self._l1_proj_mw(g, b.h_wdt, mw * 512, mw, X1P, "dt", DT1,
                             kind="softplus", rdy=b.tk[f"l1x_xz{g}"])
        b.tk[f"l1x_dt{g}"] = b.sS.tok()

        @b.on("tensor")
        def _(te):
            b.w(te, b.tk[f"xn1_{g}"])
            b.bw(te, 4, 5)
            last = None
            for c in range(8):
                last = te.matmul(b.pb[4][0:16, 0:512], b.WBSB[:, c * NST:(c + 1) * NST],
                                 XN1G[:, c * 512:(c + 1) * 512],
                                 start=(c == 0), stop=(c == 7))
            for c in range(8):
                last = te.matmul(b.pb[5][0:16, 0:512], b.WCSB[:, c * NST:(c + 1) * NST],
                                 XN1G[:, c * 512:(c + 1) * 512],
                                 start=(c == 0), stop=(c == 7))
            b.tk["l1_bcmm"] = b.tdrain(te)

        @b.on("scalar")
        def _(s):
            b.w(s, b.tk["l1_bcmm"])
            s.copy(XD1B[0:16, :], b.pb[4][0:16, 0:512])
            ins = s.copy(XD1C[0:16, :], b.pb[5][0:16, 0:512])
            tok = b.sS.inc(ins)
            b.tk["l1_bcc"] = tok
            b.bfree(tok, 4, 5)

        @b.on("vector")
        def _(v, g=g):
            b.w(v, b.tk[f"l1x_dt{g}"])
            for c in range(8):
                v.tensor_scalar(Y1[:, c * 512:(c + 1) * 512],
                                X1P[:, c * 512:(c + 1) * 512],
                                bass.AP(b.VEC1, c * 4 + 1, [[32, 128], [4, 1]]),
                                None, ALU.mult)
            ins = None
            for c in range(8):
                ins = v.tensor_mul(X1P[:, c * 512:(c + 1) * 512],
                                   DT1[:, c * 512:(c + 1) * 512],
                                   X1P[:, c * 512:(c + 1) * 512])
            b.tk["l1_dtx"] = b.sV.inc(ins)

        for n in range(NST):
            bB = b.pb[4 + 2 * (n % 2)]
            bC = b.pb[5 + 2 * (n % 2)]

            @b.on("tensor")
            def _(te, n=n, bB=bB, bC=bC):
                b.w(te, b.tk["l1_bcc"])
                b.bw(te, 4 + 2 * (n % 2), 5 + 2 * (n % 2))
                te.matmul(bB[:, 0:512], b.selbc(n, 16), XD1B[0:16, 0:512],
                          start=True, stop=True)
                te.matmul(bC[:, 0:512], b.selbc(n, 16), XD1C[0:16, 0:512],
                          start=True, stop=True)
                b.tk[f"l1_nbc{g}_{n}"] = b.tdrain(te)

            for h in range(2):
                @b.on("scalar")
                def _(s, n=n, h=h):
                    b.w(s, b.tk["aprep"])
                    if h == 0:
                        b.w(s, b.tk["l1_dtx"] if n == 0
                            else b.tk[f"l1_scv{g}_{n - 1}_1"])
                    else:
                        b.w(s, b.tk[f"l1_scv{g}_{n}_0"])
                    ins = None
                    for j in range(4):
                        c = h * 4 + j
                        ins = s.activation(ADA1[:, j * 512:(j + 1) * 512],
                                           DT1[:, c * 512:(c + 1) * 512], AF.Exp,
                                           bias=0.0,
                                           scale=b.A1SB[:, c * NST + n:c * NST + n + 1])
                    b.tk[f"l1_da{g}_{n}_{h}"] = b.sS.inc(ins)

                @b.on("vector")
                def _(v, n=n, g=g, h=h, bB=bB, bC=bC):
                    b.w(v, b.tk[f"l1_da{g}_{n}_{h}"], b.tk[f"l1_nbc{g}_{n}"],
                        b.tk["l1_dtx"])
                    ins = None
                    for j in range(4):
                        c = h * 4 + j
                        q = c * NST + n
                        v.tensor_mul(DBX1[:, :], X1P[:, c * 512:(c + 1) * 512],
                                     bB[:, 0:512])
                        v.tensor_tensor_scan(HS1[:, :], ADA1[:, j * 512:(j + 1) * 512],
                                             DBX1[:, :], HC1[:, q:q + 1],
                                             ALU.mult, ALU.add)
                        v.tensor_mul(DBX1[:, :], HS1[:, :], bC[:, 0:512])
                        v.tensor_add(Y1[:, c * 512:(c + 1) * 512],
                                     Y1[:, c * 512:(c + 1) * 512], DBX1[:, :])
                        ins = v.tensor_copy(HC1[:, q:q + 1], HS1[:, 511:512])
                    tok = b.sV.inc(ins)
                    b.tk[f"l1_scv{g}_{n}_{h}"] = tok
                    if h == 1:
                        b.bfree(tok, 4 + 2 * (n % 2), 5 + 2 * (n % 2))
        b.tk[f"l1_scan{g}"] = b.sV.tok()

        for c in range(8):
            @b.on("scalar")
            def _(s, c=c):
                b.w(s, b.tk[f"l1_scan{g}"])
                b.w(s, b.tk.get(f"l1_spk{g}_{c - 2}"))
                ins = s.activation(b.TMP[:, (c % 2) * 512:(c % 2) * 512 + 512],
                                   Y1[:, c * 512:(c + 1) * 512], AF.Sigmoid,
                                   bias=b.SCAL[:, 8 + c:9 + c],
                                   scale=b.SCAL[:, 18:19])
                b.tk[f"l1_sg{g}_{c}"] = b.sS.inc(ins)

            @b.on("vector")
            def _(v, c=c):
                b.w(v, b.tk[f"l1_sg{g}_{c}"])
                ins = v.tensor_mul(Y1[:, c * 512:(c + 1) * 512],
                                   Y1[:, c * 512:(c + 1) * 512],
                                   b.TMP[:, (c % 2) * 512:(c % 2) * 512 + 512])
                b.tk[f"l1_spk{g}_{c}"] = b.sV.inc(ins)

        spk_done = b.sV.tok()
        for mw in range(2):
            self._l1_proj_mw(g, b.h_wxz, D + mw * 512, mw, XN1G, f"z{mw}", Y1,
                             kind="siluz", zoff=mw * 4, rdy=spk_done)
        b.tk[f"l1_yg{g}"] = b.sV.tok()

        for ow in range(2):
            for c in range(8):
                slot = c % 4

                @b.on("sync")
                def _(s, ow=ow, c=c, slot=slot):
                    b.w(s, b.win2_use[slot])
                    b.sR[slot].n += 16
                    s.dma_start(out=b.WIN2[:, slot * 512:slot * 512 + 512],
                                in_=b.h_wo1[c * 128:(c + 1) * 128,
                                            ow * 512:(ow + 1) * 512]).then_inc(b.sR[slot].h, 16)
                    b.tk[f"l1_opw{g}_{ow}_{c}"] = b.sR[slot].tok()

                @b.on("tensor")
                def _(te, ow=ow, c=c, slot=slot):
                    b.w(te, b.tk[f"l1_opw{g}_{ow}_{c}"])
                    if c == 0:
                        b.w(te, b.tk[f"l1_yg{g}"])
                        b.bw(te, 0, 1, 2, 3)
                    last = None
                    for m in range(4):
                        last = te.matmul(b.pb[m][:, 0:512],
                                         b.WIN2[:, slot * 512 + m * 128:
                                                slot * 512 + (m + 1) * 128],
                                         Y1[:, c * 512:(c + 1) * 512],
                                         start=(c == 0), stop=(c == 7))
                    tok = b.tdrain(te) if c == 7 else b.sT.inc(last)
                    b.win2_use[slot] = tok
                    b.tk[f"l1_opmm{g}_{ow}_{c}"] = tok

            @b.on("vector")
            def _(v, ow=ow):
                b.w(v, b.tk[f"l1_opmm{g}_{ow}_{7}"])
                ins = None
                for m in range(4):
                    mc = ow * 4 + m
                    dst = b.ACC[:, mc * T + s0:mc * T + s0 + 512]
                    v.tensor_add(dst, dst, b.pb[m][:, 0:512])
                    ins = v.tensor_sub(dst, dst, XN1G[:, mc * 512:(mc + 1) * 512])
                tok = b.sV.inc(ins)
                b.tk[f"l1_opev{g}_{ow}"] = tok
                b.bfree(tok, 0, 1, 2, 3)
        b.tk[f"l1_done{g}"] = b.sV.tok()

    def _l1_proj_mw(self, g, h_w, colbase, mw, RHS, tag, DST, kind="copy",
                    zoff=0, rdy=None):
        b = self

        for c in range(8):
            slot = c % 4

            @b.on("sync")
            def _(s, c=c, slot=slot):
                b.w(s, b.win_use[slot])
                b.sQ[slot].n += 16
                s.dma_start(out=b.WIN[:, slot * 512:slot * 512 + 512],
                            in_=h_w[c * 128:(c + 1) * 128, colbase:colbase + 512]
                            ).then_inc(b.sQ[slot].h, 16)
                b.tk[f"l1p_{tag}_w{g}_{c}"] = b.sQ[slot].tok()

            @b.on("tensor")
            def _(te, c=c, slot=slot):
                b.w(te, b.tk[f"l1p_{tag}_w{g}_{c}"])
                if c == 0:
                    b.w(te, rdy)
                    b.bw(te, 0, 1, 2, 3)
                last = None
                for m in range(4):
                    last = te.matmul(b.pb[m][:, 0:512],
                                     b.WIN[:, slot * 512 + m * 128:
                                            slot * 512 + (m + 1) * 128],
                                     RHS[:, c * 512:(c + 1) * 512],
                                     start=(c == 0), stop=(c == 7))
                tok = b.tdrain(te) if c == 7 else b.sT.inc(last)
                b.win_use[slot] = tok
                if c == 7:
                    b.tk[f"l1p_{tag}_mm{g}"] = tok

        if kind == "copy":
            @b.on("scalar")
            def _(s):
                b.w(s, b.tk[f"l1p_{tag}_mm{g}"])
                ins = None
                for m in range(4):
                    cc = mw * 4 + m
                    ins = s.copy(DST[:, cc * 512:(cc + 1) * 512], b.pb[m][:, 0:512])
                tok = b.sS.inc(ins)
                b.bfree(tok, 0, 1, 2, 3)
        elif kind == "softplus":
            @b.on("scalar")
            def _(s):
                b.w(s, b.tk[f"l1p_{tag}_mm{g}"])
                ins = None
                for m in range(4):
                    cc = mw * 4 + m
                    t_ = b.TMP[:, (m % 2) * 512:(m % 2) * 512 + 512]
                    s.activation(t_, b.pb[m][:, 0:512], AF.Exp,
                                 bias=bass.AP(b.VEC1, cc * 4, [[32, 128], [4, 1]]),
                                 scale=1.0)
                    ins = s.activation(DST[:, cc * 512:(cc + 1) * 512], t_, AF.Ln,
                                       bias=1.0, scale=1.0)
                tok = b.sS.inc(ins)
                b.bfree(tok, 0, 1, 2, 3)
        else:
            for pair in range(2):
                @b.on("scalar")
                def _(s, pair=pair):
                    b.w(s, b.tk[f"l1p_{tag}_mm{g}"],
                        b.tk.get(f"l1p_{tag}_gv{g}_{pair - 1}"))
                    ins = None
                    for j in range(2):
                        m = pair * 2 + j
                        ins = s.activation(b.STAT[:, j * 512:(j + 1) * 512],
                                           b.pb[m][:, 0:512], AF.Silu)
                    b.tk[f"l1p_{tag}_sl{g}_{pair}"] = b.sS.inc(ins)

                @b.on("vector")
                def _(v, pair=pair):
                    b.w(v, b.tk[f"l1p_{tag}_sl{g}_{pair}"])
                    ins = None
                    for j in range(2):
                        cc = zoff + pair * 2 + j
                        ins = v.tensor_mul(DST[:, cc * 512:(cc + 1) * 512],
                                           DST[:, cc * 512:(cc + 1) * 512],
                                           b.STAT[:, j * 512:(j + 1) * 512])
                    tok = b.sV.inc(ins)
                    b.tk[f"l1p_{tag}_gv{g}_{pair}"] = tok
                    if pair == 1:
                        b.bfree(tok, 0, 1, 2, 3)

    def _moe_subgroup(self, g, sg, XN1G, XG, GBC, HE):
        b = self
        s0 = g * 512
        sgc = s0 + sg * 256

        for e in range(8):
            @b.on("tensor")
            def _(te, e=e):
                b.w(te, b.tk[f"gt_done{g}"], b.tk[f"xn1_{g}"])
                b.bw(te, 2)
                ins = te.matmul(b.pb[2][:, 0:256], b.selbc(e, 8),
                                b.GROWS[0:8, sgc:sgc + 256],
                                start=True, stop=True)
                b.tk["moe_gbc"] = b.tdrain(te)

            @b.on("vector")
            def _(v, e=e):
                b.w(v, b.tk["moe_gbc"])
                b.w(v, b.tk.get(f"moe_w1use{g}_{sg}_{e - 1}"))
                b.w(v, b.tk.get(f"moe_hev_v{g}_{sg}_{e - 1}_1"))
                v.tensor_copy(GBC[:, :], b.pb[2][:, 0:256])
                ins = None
                for c in range(8):
                    ins = v.tensor_mul(XG[:, c * 256:(c + 1) * 256],
                                       XN1G[:, c * 512 + sg * 256:
                                            c * 512 + sg * 256 + 256],
                                       b.pb[2][:, 0:256])
                tok = b.sV.inc(ins)
                b.tk[f"moe_xg{g}_{sg}_{e}"] = tok
                b.bfree(tok, 2)

            for mh in range(2):
                for c in range(8):
                    slot = c % 4

                    @b.on("sync")
                    def _(s, e=e, mh=mh, c=c, slot=slot):
                        b.w(s, b.win_use[slot], b.win2_use[slot])
                        for half in range(2):
                            src = bass.AP(b.h_W1.tensor,
                                          e * D * 2048 + c * 128 * 2048 +
                                          mh * 1024 + half * 512,
                                          [[2048, 128], [1, 512]])
                            dst = b.WIN if half == 0 else b.WIN2
                            sm = b.sQ[slot] if half == 0 else b.sR[slot]
                            sm.n += 16
                            s.dma_start(out=dst[:, slot * 512:slot * 512 + 512],
                                        in_=src).then_inc(sm.h, 16)
                        b.tk[f"moe_w1w{g}_{sg}_{e}_{mh}_{c}"] = b.sQ[slot].tok()
                        b.tk[f"moe_w1w2{g}_{sg}_{e}_{mh}_{c}"] = b.sR[slot].tok()

                    @b.on("tensor")
                    def _(te, e=e, mh=mh, c=c, slot=slot):
                        b.w(te, b.tk[f"moe_w1w{g}_{sg}_{e}_{mh}_{c}"],
                            b.tk[f"moe_w1w2{g}_{sg}_{e}_{mh}_{c}"],
                            b.tk[f"moe_xg{g}_{sg}_{e}"])
                        if c == 0:
                            b.bw(te, 0, 1, 2, 3)
                        last = None
                        for m in range(8):
                            buf = b.WIN if m < 4 else b.WIN2
                            mm_ = m % 4
                            last = te.matmul(
                                b.pb[m // 2][:, (m % 2) * 256:(m % 2) * 256 + 256],
                                buf[:, slot * 512 + mm_ * 128:
                                    slot * 512 + (mm_ + 1) * 128],
                                XG[:, c * 256:(c + 1) * 256],
                                start=(c == 0 and m % 2 == 0),
                                stop=(c == 7 and m % 2 == 1))
                        tok = b.tdrain(te) if c == 7 else b.sT.inc(last)
                        b.win_use[slot] = tok
                        b.win2_use[slot] = tok
                        b.tk[f"moe_w1mm{g}_{sg}_{e}_{mh}_{c}"] = tok
                        if mh == 1 and c == 7:
                            b.tk[f"moe_w1use{g}_{sg}_{e}"] = tok

                @b.on("vector")
                def _(v, e=e, mh=mh):
                    b.w(v, b.tk[f"moe_w1mm{g}_{sg}_{e}_{mh}_7"])
                    ins = None
                    for m in range(8):
                        hm = mh * 8 + m
                        ins = v.scalar_tensor_tensor(
                            HE[:, hm * 256:(hm + 1) * 256], GBC[:, :],
                            b.B1C[:, e * 16 + hm:e * 16 + hm + 1],
                            b.pb[m // 2][:, (m % 2) * 256:(m % 2) * 256 + 256],
                            ALU.mult, ALU.add)
                    tok = b.sV.inc(ins)
                    b.tk[f"moe_hev_v{g}_{sg}_{e}_{mh}"] = tok
                    b.bfree(tok, 0, 1, 2, 3)

                @b.on("scalar")
                def _(s, e=e, mh=mh):
                    b.w(s, b.tk[f"moe_hev_v{g}_{sg}_{e}_{mh}"])
                    ins = None
                    for m in range(8):
                        hm = mh * 8 + m
                        ins = s.activation(HE[:, hm * 256:(hm + 1) * 256],
                                           HE[:, hm * 256:(hm + 1) * 256], AF.Relu)
                    tok = b.sS.inc(ins)
                    b.tk[f"moe_hev{g}_{sg}_{e}_{mh}"] = tok

            for ch in range(16):
                @b.on("sync")
                def _(s, e=e, ch=ch):
                    b.w(s, b.tk.get(f"moe_w2use{g}_{sg}_{e}_{ch - 2}"))
                    if ch < 2:
                        b.w(s, b.win_use[0], b.win_use[1], b.win_use[2], b.win_use[3])
                    src = bass.AP(b.h_W2.tensor, e * 2048 * D + ch * 128 * D,
                                  [[D, 128], [1, D]])
                    sm = b.sQ[(ch % 2) * 2]
                    sm.n += 16
                    s.dma_start(out=b.WIN[:, (ch % 2) * 1024:(ch % 2) * 1024 + 1024],
                                in_=src).then_inc(sm.h, 16)
                    b.tk[f"moe_w2w{g}_{sg}_{e}_{ch}"] = sm.tok()

                @b.on("tensor")
                def _(te, e=e, ch=ch):
                    b.w(te, b.tk[f"moe_w2w{g}_{sg}_{e}_{ch}"],
                        b.tk[f"moe_hev{g}_{sg}_{e}_{ch // 8}"])
                    if e == 0 and ch == 0:
                        b.bw(te, 4, 5, 6, 7)
                    last = None
                    for m in range(8):
                        last = te.matmul(
                            b.pb[4 + m // 2][:, (m % 2) * 256:(m % 2) * 256 + 256],
                            b.WIN[:, (ch % 2) * 1024 + m * 128:
                                  (ch % 2) * 1024 + (m + 1) * 128],
                            HE[:, ch * 256:(ch + 1) * 256],
                            start=(e == 0 and ch == 0 and m % 2 == 0), stop=False)
                    tok = b.sT.inc(last)
                    b.tk[f"moe_w2use{g}_{sg}_{e}_{ch}"] = tok
                    if ch >= 14:
                        b.win_use[0] = tok
                        b.win_use[1] = tok
                        b.win_use[2] = tok
                        b.win_use[3] = tok

        @b.on("tensor")
        def _(te):
            b.w(te, b.tk[f"moe_w2use{g}_{sg}_7_15"])
            last = None
            for m in range(8):
                last = te.matmul(
                    b.pb[4 + m // 2][:, (m % 2) * 256:(m % 2) * 256 + 256],
                    b.B2R[0:8, m * 128:(m + 1) * 128],
                    b.GROWS[0:8, sgc:sgc + 256],
                    start=False, stop=(m % 2 == 1))
            b.tk[f"moe_b2{g}_{sg}"] = b.tdrain(te)

        @b.on("vector")
        def _(v):
            b.w(v, b.tk[f"moe_b2{g}_{sg}"])
            ins = None
            for m in range(8):
                dst = b.ACC[:, m * T + sgc:m * T + sgc + 256]
                ins = v.tensor_add(dst, dst,
                                   b.pb[4 + m // 2][:, (m % 2) * 256:(m % 2) * 256 + 256])
            tok = b.sV.inc(ins)
            b.tk[f"moe_acc{g}_{sg}"] = tok
            b.bfree(tok, 4, 5, 6, 7)
            if sg == 1:
                b.tk[f"xn1_cons{g}"] = tok

    def epilogue(self):
        b = self
        done = b.tk.get("final", b.tk["x"])
        for tt in range(16):
            @b.on("tensor")
            def _(te, tt=tt):
                b.w(te, done, b.tk["ones"])
                b.bw(te, 6, 7)
                last = None
                for c in range(8):
                    bank = b.pb[6 + (c // 4)]
                    last = te.transpose(bank[:, (c % 4) * 128:(c % 4 + 1) * 128],
                                        b.ACC[:, c * T + tt * 128: c * T + (tt + 1) * 128],
                                        b.IDENT[:, :])
                b.tk[f"eptp{tt}"] = b.tdrain(te)

            @b.on("vector")
            def _(v, tt=tt):
                b.w(v, b.tk[f"eptp{tt}"], b.tk.get(f"epst{tt - 1}"))
                v.tensor_copy(b.TMP[:, 0:512], b.pb[6][:, :512])
                ins = v.tensor_copy(b.TMP[:, 512:1024], b.pb[7][:, :512])
                tok = b.sV.inc(ins)
                b.tk[f"epcp{tt}"] = tok
                b.bfree(tok, 6, 7)

            @b.on("gpsimd")
            def _(g, tt=tt):
                b.w(g, b.tk[f"epcp{tt}"])
                g.dma_start(out=b.h_out[tt * 128:(tt + 1) * 128, :],
                            in_=b.TMP[:, :1024]).then_inc(b.sO.h, 16)
                b.sO.n += 16
                b.tk[f"epst{tt}"] = b.sO.tok()

        @b.on("gpsimd")
        def _(g):
            g.wait_ge(b.sO.h, b.sO.n)


def _prep_inputs(i):
    x = np.asarray(i["L0_attn_hidden_states"], dtype=np.float32)
    consts = np.eye(128, dtype=np.float32)
    inT = np.ascontiguousarray(np.asarray(i["L0_attn_in_proj_w"], np.float32).T)
    outT = np.ascontiguousarray(np.asarray(i["L0_attn_out_proj_w"], np.float32).T)
    xpT = np.concatenate([np.ascontiguousarray(np.asarray(i[f"L0_attn_x_proj_w_{d}"], np.float32).T)
                          for d in "fr"], axis=0)
    dtT = np.concatenate([np.ascontiguousarray(np.asarray(i[f"L0_attn_dt_proj_w_{d}"], np.float32).T)
                          for d in "fr"], axis=0)
    vecs0 = np.zeros((128, 32, 8), np.float32)
    A0 = np.zeros((128, 32 * NST), np.float32)
    for di, d in enumerate("fr"):
        cw = np.asarray(i[f"L0_attn_conv_w_{d}"], np.float32)
        cb = np.asarray(i[f"L0_attn_conv_b_{d}"], np.float32)
        db = np.asarray(i[f"L0_attn_dt_proj_b_{d}"], np.float32)
        Dd = np.asarray(i[f"L0_attn_D_{d}"], np.float32)
        Al = np.asarray(i[f"L0_attn_A_log_{d}"], np.float32)
        for c in range(16):
            slc = slice(c * 128, (c + 1) * 128)
            vecs0[:, di * 16 + c, 0:4] = cw[slc]
            vecs0[:, di * 16 + c, 4] = cb[slc]
            vecs0[:, di * 16 + c, 5] = db[slc]
            vecs0[:, di * 16 + c, 6] = Dd[slc]
            A0[:, (di * 16 + c) * NST:(di * 16 + c + 1) * NST] = Al[slc]
    vecs0 = vecs0.reshape(128, 256)
    wgT = np.ascontiguousarray(np.asarray(i["L0_ffn_w_gate"], np.float32).T)
    wuT = np.ascontiguousarray(np.asarray(i["L0_ffn_w_up"], np.float32).T)
    wdT = np.ascontiguousarray(np.asarray(i["L0_ffn_w_down"], np.float32).T)
    wxz = np.ascontiguousarray(np.asarray(i["L1_attn_W_xz"], np.float32))
    wdt = np.ascontiguousarray(np.asarray(i["L1_attn_W_dt"], np.float32))
    wB = np.ascontiguousarray(np.asarray(i["L1_attn_W_B"], np.float32))
    wC = np.ascontiguousarray(np.asarray(i["L1_attn_W_C"], np.float32))
    wo1 = np.ascontiguousarray(np.asarray(i["L1_attn_W_out"], np.float32))
    vecs1 = np.zeros((128, 8, 4), np.float32)
    A1 = np.zeros((128, 8 * NST), np.float32)
    bdt = np.asarray(i["L1_attn_b_dt"], np.float32)
    dsk = np.asarray(i["L1_attn_D_skip"], np.float32)
    vth = np.asarray(i["L1_attn_v_th"], np.float32)
    Al1 = np.asarray(i["L1_attn_A_log"], np.float32)
    for c in range(8):
        slc = slice(c * 128, (c + 1) * 128)
        vecs1[:, c, 0] = bdt[slc]
        vecs1[:, c, 1] = dsk[slc]
        vecs1[:, c, 2] = vth[slc]
        A1[:, c * NST:(c + 1) * NST] = Al1[slc]
    vecs1 = vecs1.reshape(128, 32)
    gate = np.ascontiguousarray(np.asarray(i["L1_ffn_gate"], np.float32))
    W1 = np.ascontiguousarray(np.asarray(i["L1_ffn_W1"], np.float32))
    W2 = np.ascontiguousarray(np.asarray(i["L1_ffn_W2"], np.float32))
    b1c = np.ascontiguousarray(
        np.asarray(i["L1_ffn_b1"], np.float32).reshape(8, 16, 128)
        .transpose(2, 0, 1).reshape(128, 128))
    b2r = np.ascontiguousarray(np.asarray(i["L1_ffn_b2"], np.float32))

    shared = dict(consts=consts, inT=inT, outT=outT, xpT=xpT, dtT=dtT,
                  vecs0=vecs0, A0=A0, wgT=wgT, wuT=wuT, wdT=wdT,
                  wxz=wxz, wdt=wdt, wB=wB, wC=wC, wo1=wo1, vecs1=vecs1, A1=A1,
                  gateW=gate, W1=W1, W2=W2, b1c=b1c, b2r=b2r)
    in_maps = []
    for core in range(8):
        m = dict(shared)
        m["x_fm"] = np.ascontiguousarray(x[core].T)
        in_maps.append(m)
    return in_maps


def kernel(**inputs):
    in_maps = _prep_inputs(inputs)
    nc = B().build()
    res = run_bass_kernel_spmd(nc, in_maps, core_ids=list(range(8)))
    return np.stack([np.asarray(r["out_tok"]) for r in res.results])


# version 12 (final)
# speedup vs baseline: 1.0443x; 1.0443x over previous
import numpy as np
import ml_dtypes
from contextlib import ExitStack
import concourse.bass as bass
import concourse.mybir as mybir
from concourse.bass_utils import run_bass_kernel_spmd

F32 = mybir.dt.float32
BF16 = mybir.dt.bfloat16
I32 = mybir.dt.int32
AF = mybir.ActivationFunctionType
ALU = mybir.AluOpType
AX = mybir.AxisListType

T = 2048
D = 1024
DI = 2048
NST = 16
TC = 256
NTC = T // TC
STAGE = 9
DEBUG = False
R127 = float(np.float32(1.0 / 127.0))


class Sem:
    __slots__ = ("h", "n")

    def __init__(self, ctx, nc, name):
        self.h = ctx.enter_context(nc.semaphore(name))
        self.n = 0

    def inc(self, instr, k=1):
        instr.then_inc(self.h, k)
        self.n += k
        return (self, self.n)

    def tok(self):
        return (self, self.n)


class B:
    def __init__(self):
        self.nc = bass.Bass("TRN2", target_bir_lowering=False,
                            detect_race_conditions=False,
                            dynamic_dma_scratch_size=4096)
        self.ctx = ExitStack()
        self.waited = {e: {} for e in ("tensor", "vector", "scalar", "sync", "gpsimd")}
        self.cur = None
        self.bkfree = [None] * 8
        self.win_use = [None] * 4
        self.win2_use = [None] * 4
        self.tk = {}

    def on(self, engname):
        def reg(f):
            self.cur = engname
            getattr(self.block, engname)(f)
            self.cur = None
        return reg

    def w(self, eng, *tokens):
        d = self.waited[self.cur]
        for tk_ in tokens:
            if tk_ is None:
                continue
            sem, v = tk_
            if v > 0 and d.get(sem, 0) < v:
                eng.wait_ge(sem.h, v)
                d[sem] = v

    def tdrain(self, te):
        return self.sT.inc(te.drain())

    def ch(self, eng, instr):
        s = self.sVC if self.cur == "vector" else self.sSC
        self.w(eng, s.inc(instr))

    def bw(self, eng, *banks):
        for i in banks:
            self.w(eng, self.bkfree[i])

    def bfree(self, tok, *banks):
        for i in banks:
            self.bkfree[i] = tok

    def rln(self, r, c0, cnt, rev=False):
        t = self.RLN0 if r < 2 else self.RLN1
        p = (r % 2) * 32
        if not rev:
            return bass.AP(t, p * T + c0, [[T, 1], [1, cnt]])
        return bass.AP(t, p * T + c0 + cnt - 1, [[T, 1], [-1, cnt]])

    def selbc(self, p, K):
        return bass.AP(self.IDENT, p, [[128, K], [0, 128]])

    def rln_rhs(self, layer, c0, cnt, rev=False):
        t = self.RLN0 if layer == 0 else self.RLN1
        if not rev:
            return bass.AP(t, c0, [[T, 64], [1, cnt]])
        return bass.AP(t, c0 + cnt - 1, [[T, 64], [-1, cnt]])

    def build(self):
        nc, ctx = self.nc, self.ctx
        dram = lambda name, shape, dt=F32, kind="ExternalInput": \
            nc.dram_tensor(name, shape, dt, kind=kind).ap()
        sb = lambda name, shape, dt=F32: ctx.enter_context(nc.sbuf_tensor(name, shape, dt))

        self.h_x = dram("x_fm", [D, T])
        self.h_out = dram("out_tok", [T, D], kind="ExternalOutput")
        self.h_id = dram("consts", [128, 128])
        self.h_inT = dram("inT", [D, 2 * DI])
        self.h_outT = dram("outT", [DI, D])
        self.h_xpT = dram("xpT", [2 * DI, 96])
        self.h_dtT = dram("dtT", [2 * 64, DI])
        self.h_vecs0 = dram("vecs0", [128, 32 * 8])
        self.h_A0 = dram("A0", [128, 32 * NST])
        self.h_wgT = dram("wgT", [D, 4096])
        self.h_wuT = dram("wuT", [D, 4096])
        self.h_wdT = dram("wdT", [4096, D])
        self.h_wxz = dram("wxz", [D, 2 * D])
        self.h_wdt = dram("wdt", [D, D])
        self.h_wB = dram("wB", [D, NST])
        self.h_wC = dram("wC", [D, NST])
        self.h_wo1 = dram("wo1", [D, D])
        self.h_vecs1 = dram("vecs1", [128, 8 * 4])
        self.h_A1 = dram("A1", [128, 8 * NST])
        self.h_gate = dram("gateW", [D, 8])
        self.h_W1 = dram("W1", [8, D, 2048])
        self.h_W2 = dram("W2", [8, 2048, D])
        self.h_b1 = dram("b1c", [128, 128])
        self.h_b2 = dram("b2r", [8, D])
        if DEBUG:
            self.h_dbg = {k: dram(f"dbg_{k}", [128, 8 * T], kind="ExternalOutput")
                          for k in ("ffn", "pa", "pb")}
            self.h_dbgr = dram("dbg_rows", [64, T], kind="ExternalOutput")
            self.h_dbgxq = nc.dram_tensor("dbg_xqt", [128, 1024], BF16,
                                          kind="ExternalOutput").ap()
            self.h_dbggu = dram("dbg_gu", [128, 4096], kind="ExternalOutput")
            self.h_dbgxq2 = dram("dbg_xq2", [128, 4096], kind="ExternalOutput")
            self.h_dbgfm = nc.dram_tensor("dbg_xq2fm", [128, 4096], BF16,
                                          kind="ExternalOutput").ap()
            self.h_dbgrow = dram("dbg_row1", [1, 512], kind="ExternalOutput")
            self.h_dbgqc = dram("dbg_qc", [128, 8], kind="ExternalOutput")
            self.h_dbgxn = dram("dbg_xn", [128, 1024], kind="ExternalOutput")
            self.h_dbgst = dram("dbg_stat", [128, 16], kind="ExternalOutput")
        self.h_wqg = nc.dram_tensor("wqg_s", [D, 4096], BF16, kind="Internal").ap()
        self.h_wqu = nc.dram_tensor("wqu_s", [D, 4096], BF16, kind="Internal").ap()
        self.h_wqd = nc.dram_tensor("wqd_s", [4096, D], BF16, kind="Internal").ap()

        self.ACC = sb("ACC", [128, 8 * T])
        self.IDENT = sb("IDENT", [128, 128])
        self.IDENTB = sb("IDENTB", [128, 128], BF16)
        self.ONES = sb("ONESC", [128, 128])
        self.WIN = sb("WIN", [128, 4 * 512])
        self.WIN2 = sb("WIN2", [128, 4 * 512])
        self.TMP = sb("TMP", [128, 1024])
        self.STAT = sb("STAT", [128, 1024])
        self.RLN1 = sb("RLN1", [64, T])
        self.ROW1 = sb("ROW1", [1, 512])
        self.SCAL = sb("SCAL", [128, 32])
        self.VEC1 = sb("VEC1", [128, 8 * 4])
        self.A1SB = sb("A1SB", [128, 8 * NST])
        self.GATEW = sb("GATEW", [128, 8 * 8])
        self.WBSB = sb("WBSB", [128, 8 * NST])
        self.WCSB = sb("WCSB", [128, 8 * NST])
        self.B1C = sb("B1C", [128, 128])
        self.B2R = sb("B2R", [8, D])
        self.GROWS = sb("GROWS", [8, T])
        self.GTOK = sb("GTOK", [128, 8])

        self.pb = [ctx.enter_context(nc.psum_tensor(f"pb{i}", [128, 512], F32))
                   for i in range(8)]
        for name in ("sW", "sG", "sT", "sV", "sS", "sO", "sVC", "sSC",
                     "sB1", "sB2", "sD0", "sD1", "sX", "sWQ", "sXI"):
            setattr(self, name, Sem(ctx, nc, name))
        self.sQ = [Sem(ctx, nc, f"sQ{i}") for i in range(4)]
        self.sR = [Sem(ctx, nc, f"sR{i}") for i in range(4)]

        self.block = ctx.enter_context(nc.Block())

        with ExitStack() as l0stk:
            self.RLN0 = l0stk.enter_context(nc.sbuf_tensor("RLN0", [64, T], F32))
            self.VEC0 = l0stk.enter_context(nc.sbuf_tensor("VEC0", [128, 256], F32))
            self.A0SB = l0stk.enter_context(nc.sbuf_tensor("A0SB", [128, 512], F32))
            self.prologue()
            self.ln_stats(0)
            self.dbg_rows_dump(self.tk["ln0"])
            if STAGE >= 2:
                self.ffn0()
            else:
                self.tk["ffn"] = self.sV.tok()
            self.dbg_dump("ffn", self.tk["ffn"])
            if STAGE >= 3:
                self.attn_pass(0)
                self.dbg_dump("pa", self.sV.tok())
            if STAGE >= 4:
                self.attn_pass(1)
                self.dbg_dump("pb", self.sV.tok())
            self.tk["x1"] = self.sV.tok()
        if STAGE >= 5:
            self.ln_stats(1)
            self.layer1()
        self.tk["final"] = self.sV.tok()
        self.epilogue()

        self.ctx.close()
        return self.nc

    def dbg_dump(self, key, tok):
        if not DEBUG:
            return
        b = self

        @b.on("gpsimd")
        def _(g):
            b.w(g, tok)
            g.dma_start(out=b.h_dbg[key][:, :], in_=b.ACC[:, :]).then_inc(b.sO.h, 16)
            b.sO.n += 16
            b.tk[f"dbgd_{key}"] = b.sO.tok()

        @b.on("vector")
        def _(v):
            b.w(v, b.tk[f"dbgd_{key}"])

    def dbg_rows_dump(self, tok):
        if not DEBUG:
            return
        b = self

        @b.on("gpsimd")
        def _(g):
            b.w(g, tok)
            g.dma_start(out=b.h_dbgr[:, :], in_=b.RLN0[0:64, :]).then_inc(b.sO.h, 16)
            b.sO.n += 16

    def prologue(self):
        b = self

        @b.on("gpsimd")
        def _(g):
            g.dma_start(out=b.IDENT[:, :], in_=b.h_id[:, :]).then_inc(b.sG.h, 16)
            b.sG.n += 16
            for c in range(8):
                g.dma_start(out=b.ACC[:, c * T:(c + 1) * T],
                            in_=b.h_x[c * 128:(c + 1) * 128, :]).then_inc(b.sG.h, 16)
                b.sG.n += 16
        b.tk["x"] = b.sG.tok()

        @b.on("sync")
        def _(s):
            for dst, src in ((b.VEC0, b.h_vecs0), (b.A0SB, b.h_A0),
                             (b.VEC1, b.h_vecs1), (b.A1SB, b.h_A1),
                             (b.B1C, b.h_b1), (b.B2R, b.h_b2)):
                s.dma_start(out=dst[0:dst.shape[0], :], in_=src[:, :]).then_inc(b.sW.h, 16)
                b.sW.n += 16
            for dst, src, wdt in ((b.WBSB, b.h_wB, NST), (b.WCSB, b.h_wC, NST),
                                  (b.GATEW, b.h_gate, 8)):
                s.dma_start(out=dst[:, :8 * wdt], in_=bass.AP(
                    src.tensor, 0, [[wdt, 128], [128 * wdt, 8], [1, wdt]])
                ).then_inc(b.sW.h, 16)
                b.sW.n += 16
        b.tk["smallw"] = b.sW.tok()

        @b.on("scalar")
        def _(s):
            b.w(s, b.tk["smallw"])
            s.activation(b.A0SB[:, :], b.A0SB[:, :], AF.Exp)
            ins = s.activation(b.A1SB[:, :], b.A1SB[:, :], AF.Exp)
            b.tk["aexp"] = b.sS.inc(ins)

        @b.on("vector")
        def _(v):
            b.w(v, b.tk["smallw"], b.tk["x"], b.tk["aexp"])
            v.tensor_scalar_mul(b.A0SB[:, :], b.A0SB[:, :], -1.0)
            ins = v.tensor_scalar_mul(b.A1SB[:, :], b.A1SB[:, :], -1.0)
            b.tk["aprep"] = b.sV.inc(ins)
            v.memset(b.RLN0[:, :], 0.0)
            v.memset(b.RLN1[:, :], 0.0)
            v.memset(b.ONES[:, :], 1.0)
            v.tensor_copy(b.IDENTB[:, :], b.IDENT[:, :])
            v.memset(b.SCAL[:, 17:18], 1e-5)
            v.memset(b.SCAL[:, 18:19], 10.0)
            ins = v.tensor_scalar(b.SCAL[:, 8:16], bass.AP(b.VEC1, 2, [[32, 128], [4, 8]]),
                                  0.1, -10.0, ALU.max, ALU.mult)
            b.tk["ones"] = b.sV.inc(ins)

        @b.on("tensor")
        def _(te):
            b.w(te, b.tk["ones"])
            b.bw(te, 2)
            last = None
            for c in range(8):
                last = te.matmul(b.pb[2][0:8, 0:1], b.GATEW[:, c * 8:(c + 1) * 8],
                                 b.ONES[:, 0:1], start=(c == 0), stop=(c == 7))
            b.tk["wsume_ps"] = b.tdrain(te)

        @b.on("scalar")
        def _(s):
            b.w(s, b.tk["wsume_ps"])
            ins = s.copy(b.SCAL[0:8, 16:17], b.pb[2][0:8, 0:1])
            tok = b.sS.inc(ins)
            b.tk["wsume"] = tok
            b.bfree(tok, 2)

    def ln_stats(self, layer):
        b = self
        mr_row = 0 if layer == 0 else 2
        sr_row = 1 if layer == 0 else 3
        ready = [b.tk["x"], b.tk["ones"]] if layer == 0 else [b.tk["x1"], b.tk["ones"]]

        for g in range(4):
            s0 = g * 512
            sq = [None] * 8
            tebase = b.sT.n

            @b.on("scalar")
            def _(s, s0=s0, sq=sq, g=g, layer=layer, tebase=tebase):
                b.w(s, *ready)
                for c in range(8):
                    if c >= 2:
                        b.w(s, (b.sT, tebase + c - 1))
                    ins = s.activation(b.STAT[:, (c % 2) * 512:(c % 2) * 512 + 512],
                                       b.ACC[:, c * T + s0:c * T + s0 + 512], AF.Square)
                    sq[c] = b.sS.inc(ins)

            @b.on("tensor")
            def _(te, s0=s0, sq=sq, g=g, layer=layer):
                b.w(te, *ready)
                b.bw(te, 0, 1)
                for c in range(8):
                    te.matmul(b.pb[0][0:1, :], b.ONES[:, 0:1],
                              b.ACC[:, c * T + s0:c * T + s0 + 512],
                              start=(c == 0), stop=(c == 7))
                for c in range(8):
                    b.w(te, sq[c])
                    ins = te.matmul(b.pb[1][0:1, :], b.ONES[:, 0:1],
                                    b.STAT[:, (c % 2) * 512:(c % 2) * 512 + 512],
                                    start=(c == 0), stop=(c == 7))
                    b.sT.inc(ins)
                b.tk[f"lnmm{layer}_{g}"] = b.tdrain(te)
            assert b.sT.n == tebase + 9, (b.sT.n, tebase)

            @b.on("vector")
            def _(v, s0=s0, g=g, layer=layer):
                b.w(v, b.tk[f"lnmm{layer}_{g}"])
                mr = b.rln(mr_row, s0, 512)
                v.tensor_scalar(mr, b.pb[0][0:1, :512], float(np.float32(1.0 / D)), None, ALU.mult)
                v.tensor_mul(b.ROW1[0:1, :], mr, mr)
                v.tensor_scalar(b.ROW1[0:1, :], b.ROW1[0:1, :], -1.0, None, ALU.mult)
                ins = v.scalar_tensor_tensor(b.ROW1[0:1, :], b.pb[1][0:1, :512],
                                             float(np.float32(1.0 / D)), b.ROW1[0:1, :],
                                             ALU.mult, ALU.add)
                tok = b.sV.inc(ins)
                b.tk["var"] = tok
                b.bfree(tok, 0, 1)

            @b.on("scalar")
            def _(s, s0=s0, g=g, layer=layer):
                b.w(s, b.tk["var"])
                ins = s.activation(b.ROW1[0:1, :], b.ROW1[0:1, :],
                                   AF.Sqrt, bias=b.SCAL[0:1, 17:18], scale=1.0)
                b.tk["lnsqrt"] = b.sS.inc(ins)

            @b.on("vector")
            def _(v, s0=s0, g=g, layer=layer):
                b.w(v, b.tk["lnsqrt"])
                ins = v.reciprocal(b.rln(sr_row, s0, 512), b.ROW1[0:1, :])
                b.tk[f"ln{layer}_{g}"] = b.sV.inc(ins)
        b.tk[f"ln{layer}"] = b.sV.tok()

    def ffn0(self):
        b = self
        nc = self.nc
        with ExitStack() as fstk:
            fsb = lambda name, shape, dt=F32: fstk.enter_context(
                nc.sbuf_tensor(name, shape, dt))
            TMPI = fstk.enter_context(nc.sbuf_tensor("TMPI", [128, 1024], I32))
            FSCR = fsb("FSCR", [128, 4096])
            XQT = fsb("XQT", [128, 8 * 128], BF16)
            XQ2 = fsb("XQ2", [128, 4096])
            XQ2FM = fsb("XQ2FM", [128, 32 * 128], BF16)
            WQB = fsb("WQB", [128, 8 * 512], BF16)
            WQB2 = fsb("WQB2", [128, 8 * 512], BF16)
            WQD = fsb("WQD", [128, 2 * 1024], BF16)
            WQO = fsb("WQO", [128, 2 * 1024], BF16)
            QC = fsb("QC", [128, 8])

            self._ffn_wprep(TMPI, WQO)
            if DEBUG:
                self.dbg_dump("pa", b.tk["wq_scratch"])
            for tt in range(16):
                self._ffn_tile(tt, TMPI, FSCR, XQT, XQ2, XQ2FM, WQB, WQB2, WQD, QC)
            b.tk["ffn"] = b.sV.tok()

    def _ffn_wprep(self, TMPI, WQO):
        b = self
        for wi, (h_w, h_q) in enumerate(((b.h_wgT, b.h_wqg), (b.h_wuT, b.h_wqu),
                                         (b.h_wdT, b.h_wqd))):
            nrow = h_w.shape[0] // 128
            npc = h_w.shape[1] // 1024
            npieces = nrow * npc

            sbase = b.sS.n
            for k in range(npieces):
                c, p = divmod(k, npc)

                @b.on("sync")
                def _(s, k=k, c=c, p=p):
                    if k >= 2:
                        b.w(s, (b.sS, sbase + k - 1))
                    else:
                        b.w(s, b.win2_use[0], b.win2_use[1], b.win2_use[2],
                            b.win2_use[3])
                    sm = b.sR[(k % 2) * 2]
                    sm.n += 16
                    s.dma_start(out=b.WIN2[:, (k % 2) * 1024:(k % 2) * 1024 + 1024],
                                in_=h_w[c * 128:(c + 1) * 128, p * 1024:(p + 1) * 1024]
                                ).then_inc(sm.h, 16)
                    b.tk["wp_dma"] = sm.tok()

                @b.on("scalar")
                def _(s, k=k):
                    b.w(s, b.tk["wp_dma"])
                    if k == 0:
                        b.w(s, b.tk.get("wp_tmp_free"))
                        b.w(s, b.tk["ln0"])
                    ins = s.activation(b.STAT[:, :1024],
                                       b.WIN2[:, (k % 2) * 1024:(k % 2) * 1024 + 1024],
                                       AF.Abs, accum_out=b.TMP[:, k:k + 1])
                    b.sS.inc(ins)
            assert b.sS.n == sbase + npieces
            b.tk["wp_abs"] = b.sS.tok()
            for j in range(4):
                b.win2_use[j] = b.tk["wp_abs"]

            @b.on("vector")
            def _(v, npieces=npieces):
                b.w(v, b.tk["wp_abs"])
                ins = v.reduce_sum(b.SCAL[:, 20:21], b.TMP[:, 0:npieces], AX.X)
                tok = b.sV.inc(ins)
                b.tk["wp_red"] = tok
                b.tk["wp_tmp_free"] = tok

            @b.on("tensor")
            def _(te):
                b.w(te, b.tk["wp_red"])
                b.bw(te, 4)
                ins = te.matmul(b.pb[4][0:1, 0:1], b.ONES[:, 0:1], b.SCAL[:, 20:21],
                                start=True, stop=True)
                b.tk["wp_sum"] = b.tdrain(te)

            @b.on("vector")
            def _(v, wi=wi, h_w=h_w):
                b.w(v, b.tk["wp_sum"])
                n_el = float(h_w.shape[0] * h_w.shape[1])
                r = b.ROW1[0:1, 0:1]
                b.ch(v, v.tensor_scalar(r, b.pb[4][0:1, 0:1],
                                        float(np.float32(1.0 / n_el)), None, ALU.mult))
                b.ch(v, v.tensor_scalar(r, r, 1e-5, None, ALU.max))
                ins = v.reciprocal(b.ROW1[0:1, 1:2], r)
                tok = b.sV.inc(ins)
                b.tk["wp_row"] = tok
                b.bfree(tok, 4)

            @b.on("tensor")
            def _(te, wi=wi):
                b.w(te, b.tk["wp_row"])
                b.bw(te, 4)
                ins = te.matmul(b.pb[4][:, 8:9], b.ONES[0:1, :], b.ROW1[0:1, 1:2],
                                start=True, stop=True)
                b.tk["wp_bc"] = b.tdrain(te)

            @b.on("scalar")
            def _(s, wi=wi):
                b.w(s, b.tk["wp_bc"])
                ins = s.copy(b.SCAL[:, wi:wi + 1], b.pb[4][:, 8:9])
                tok = b.sS.inc(ins)
                b.tk["sw"] = tok
                b.bfree(tok, 4)

            vbase = b.sV.n
            gbase = b.sG.n
            for k in range(npieces):
                c, p = divmod(k, npc)

                @b.on("sync")
                def _(s, k=k, c=c, p=p):
                    if k >= 2:
                        b.w(s, (b.sV, vbase + k - 1))
                    else:
                        b.w(s, b.tk["wp_abs"])
                    sm = b.sR[(k % 2) * 2]
                    sm.n += 16
                    s.dma_start(out=b.WIN2[:, (k % 2) * 1024:(k % 2) * 1024 + 1024],
                                in_=h_w[c * 128:(c + 1) * 128, p * 1024:(p + 1) * 1024]
                                ).then_inc(sm.h, 16)
                    b.tk["wq_dma"] = sm.tok()

                @b.on("vector")
                def _(v, k=k, wi=wi):
                    b.w(v, b.tk["wq_dma"], b.tk["sw"])
                    b.w(v, b.tk.get(f"wq_half{k % 2}"))
                    wsrc = b.WIN2[:, (k % 2) * 1024:(k % 2) * 1024 + 1024]
                    t_ = b.TMP[:, :1024]
                    u_ = b.STAT[:, :1024]
                    v.tensor_scalar(t_, wsrc, b.SCAL[:, wi:wi + 1], None, ALU.mult)
                    v.tensor_copy(TMPI[:, :1024], t_)
                    v.tensor_sub(u_, t_, TMPI[:, :1024])
                    v.scalar_tensor_tensor(t_, u_, 0.5, TMPI[:, :1024],
                                           ALU.is_gt, ALU.add)
                    v.scalar_tensor_tensor(u_, u_, -0.5, t_, ALU.is_ge, ALU.add)
                    v.tensor_scalar(t_, u_, -1.0, None, ALU.add)
                    ins = v.tensor_scalar(WQO[:, (k % 2) * 1024:(k % 2) * 1024 + 1024],
                                          t_, -1.0, 1.0, ALU.max, ALU.min)
                    b.sV.inc(ins)
                    b.tk["wq_q"] = b.sV.tok()

                @b.on("scalar")
                def _(g, k=k, c=c, p=p, h_q=h_q):
                    b.w(g, b.tk["wq_q"])
                    sm = b.sWQ if k % 2 == 0 else b.sXI
                    sm.n += 16
                    g.dma_start(out=h_q[c * 128:(c + 1) * 128, p * 1024:(p + 1) * 1024],
                                in_=WQO[:, (k % 2) * 1024:(k % 2) * 1024 + 1024]
                                ).then_inc(sm.h, 16)
                    b.tk[f"wq_half{k % 2}"] = sm.tok()
            assert b.sV.n == vbase + npieces
            for j in range(4):
                b.win2_use[j] = b.sV.tok()
        b.tk["wq_scratch"] = b.sWQ.tok()
        b.tk["wq_scratch2"] = b.sXI.tok()

        @b.on("vector")
        def _(v):
            b.w(v, b.tk["sw"])
            v.reciprocal(b.SCAL[:, 21:22], b.SCAL[:, 0:1])
            v.reciprocal(b.SCAL[:, 22:23], b.SCAL[:, 1:2])
            ins = v.reciprocal(b.SCAL[:, 23:24], b.SCAL[:, 2:3])
            b.tk["swinv"] = b.sV.inc(ins)

    def _ffn_tile(self, tt, TMPI, FSCR, XQT, XQ2, XQ2FM, WQB, WQB2, WQD, QC):
        b = self
        t0 = tt * 128
        g = tt // 4

        @b.on("tensor")
        def _(te):
            b.w(te, b.tk[f"ln0_{g}"])
            b.bw(te, 2)
            te.matmul(b.pb[2][:, 0:128], b.selbc(0, 64), b.rln_rhs(0, t0, 128),
                      start=True, stop=True)
            ins = te.matmul(b.pb[2][:, 128:256], b.selbc(32, 64), b.rln_rhs(0, t0, 128),
                            start=True, stop=True)
            b.tk["fq_bc"] = b.tdrain(te)

        @b.on("vector")
        def _(v):
            b.w(v, b.tk["fq_bc"])
            am = b.STAT[:, 0:128]
            v.memset(am, 0.0)
            ins = None
            for c in range(8):
                xn = FSCR[:, c * 128:(c + 1) * 128]
                v.tensor_sub(xn, b.ACC[:, c * T + t0:c * T + t0 + 128], b.pb[2][:, 0:128])
                v.tensor_mul(xn, xn, b.pb[2][:, 128:256])
                v.scalar_tensor_tensor(b.TMP[:, 0:128], xn, -1.0, am, ALU.mult, ALU.max)
                ins = v.scalar_tensor_tensor(am, xn, 1.0, b.TMP[:, 0:128],
                                             ALU.mult, ALU.max)
            tok = b.sV.inc(ins)
            b.tk["fq_am"] = tok
            b.bfree(tok, 2)

        @b.on("tensor")
        def _(te):
            b.w(te, b.tk["fq_am"])
            b.bw(te, 3)
            ins = te.transpose(b.pb[3][:, 0:128], b.STAT[:, 0:128], b.IDENT[:, :])
            b.tk["fq_tp"] = b.tdrain(te)

        @b.on("vector")
        def _(v):
            b.w(v, b.tk["fq_tp"], b.tk["swinv"])
            b.ch(v, v.tensor_reduce(b.STAT[:, 128:129], b.pb[3][:, 0:128],
                                    AX.X, ALU.max))
            b.ch(v, v.tensor_scalar(b.STAT[:, 129:130], b.STAT[:, 128:129],
                                    1e-5, None, ALU.max))
            b.ch(v, v.reciprocal(b.STAT[:, 130:131], b.STAT[:, 129:130]))
            b.ch(v, v.tensor_scalar(b.STAT[:, 131:132], b.STAT[:, 130:131],
                                    127.0, None, ALU.mult))
            b.ch(v, v.reciprocal(b.STAT[:, 132:133], b.STAT[:, 131:132]))
            v.tensor_scalar(QC[:, 0:1], b.STAT[:, 132:133], b.SCAL[:, 21:22], None, ALU.mult)
            ins = v.tensor_scalar(QC[:, 1:2], b.STAT[:, 132:133], b.SCAL[:, 22:23],
                                  None, ALU.mult)
            tok = b.sV.inc(ins)
            b.tk["fq_sx"] = tok
            b.bfree(tok, 3)

        @b.on("tensor")
        def _(te):
            b.w(te, b.tk["fq_sx"])
            b.bw(te, 3)
            ins = te.transpose(b.pb[3][0:1, 128:256], b.STAT[:, 131:132], b.IDENT[:, :])
            b.tk["fq_sxr_ps"] = b.tdrain(te)

        @b.on("scalar")
        def _(s):
            b.w(s, b.tk["fq_sxr_ps"])
            ins = s.copy(b.ROW1[0:1, 0:128], b.pb[3][0:1, 128:256])
            tok = b.sS.inc(ins)
            b.tk["fq_sxr"] = tok
            b.bfree(tok, 3)

        @b.on("tensor")
        def _(te):
            b.w(te, b.tk["fq_sxr"])
            b.bw(te, 3)
            ins = te.matmul(b.pb[3][:, 128:256], b.ONES[0:1, :], b.ROW1[0:1, 0:128],
                            start=True, stop=True)
            b.tk["fq_sxbc"] = b.tdrain(te)

        @b.on("vector")
        def _(v):
            b.w(v, b.tk["fq_sxbc"])
            b.w(v, b.tk.get("fg_mm7"))
            ins = None
            for c in range(8):
                xn = FSCR[:, c * 128:(c + 1) * 128]
                t_ = b.TMP[:, 0:128]
                u_ = b.STAT[:, 256:384]
                v.tensor_mul(t_, xn, b.pb[3][:, 128:256])
                v.tensor_copy(TMPI[:, 0:128], t_)
                v.tensor_sub(u_, t_, TMPI[:, 0:128])
                v.scalar_tensor_tensor(t_, u_, 0.5, TMPI[:, 0:128], ALU.is_gt, ALU.add)
                v.scalar_tensor_tensor(u_, u_, -0.5, t_, ALU.is_ge, ALU.add)
                ins = v.tensor_scalar(XQT[:, c * 128:(c + 1) * 128], u_,
                                      -1.0, None, ALU.add)
            tok = b.sV.inc(ins)
            b.tk["fq_xq"] = tok
            b.bfree(tok, 3)

        for hg in range(8):
            @b.on("sync")
            def _(s, hg=hg):
                b.w(s, b.tk["wq_scratch"], b.tk["wq_scratch2"])
                b.w(s, b.tk[f"fg_mm{hg - 1}"] if hg > 0 else b.tk.get("fg_mm7"))
                for c in range(8):
                    b.sB1.n += 16
                    s.dma_start(out=WQB[:, c * 512:(c + 1) * 512],
                                in_=self.h_wqg[c * 128:(c + 1) * 128,
                                               hg * 512:(hg + 1) * 512]).then_inc(b.sB1.h, 16)
                for c in range(8):
                    b.sB2.n += 16
                    s.dma_start(out=WQB2[:, c * 512:(c + 1) * 512],
                                in_=self.h_wqu[c * 128:(c + 1) * 128,
                                               hg * 512:(hg + 1) * 512]).then_inc(b.sB2.h, 16)
                b.tk["fg_wdma"] = b.sB1.tok()
                b.tk["fg_wdma2"] = b.sB2.tok()

            @b.on("tensor")
            def _(te, hg=hg):
                b.w(te, b.tk["fg_wdma"], b.tk["fg_wdma2"], b.tk["fq_xq"])
                b.bw(te, 0, 1)
                for c in range(8):
                    te.matmul(b.pb[0][:, :], XQT[:, c * 128:(c + 1) * 128],
                              WQB[:, c * 512:(c + 1) * 512],
                              start=(c == 0), stop=(c == 7))
                last = None
                for c in range(8):
                    last = te.matmul(b.pb[1][:, :], XQT[:, c * 128:(c + 1) * 128],
                                     WQB2[:, c * 512:(c + 1) * 512],
                                     start=(c == 0), stop=(c == 7))
                tok = b.tdrain(te)
                b.tk[f"fg_mm{hg}"] = tok
                if hg == 7:
                    b.tk["fg_mm7"] = tok

            @b.on("vector")
            def _(v, hg=hg):
                b.w(v, b.tk[f"fg_mm{hg}"])
                ins = v.tensor_scalar(b.TMP[:, 0:512], b.pb[0][:, :], QC[:, 0:1],
                                      None, ALU.mult)
                b.tk["fg_gd"] = b.sV.inc(ins)

            @b.on("scalar")
            def _(s, hg=hg):
                b.w(s, b.tk["fg_gd"])
                ins = s.activation(b.STAT[:, 512:1024], b.TMP[:, 0:512], AF.Silu)
                b.tk["fg_silu"] = b.sS.inc(ins)

            @b.on("vector")
            def _(v, hg=hg):
                b.w(v, b.tk["fg_silu"])
                gu = FSCR[:, hg * 512:(hg + 1) * 512]
                v.tensor_scalar(b.TMP[:, 512:1024], b.pb[1][:, :], QC[:, 1:2],
                                None, ALU.mult)
                v.tensor_mul(gu, b.STAT[:, 512:1024], b.TMP[:, 512:1024])
                ins = v.tensor_reduce(b.STAT[:, 133 + hg:134 + hg], gu, AX.X,
                                      ALU.max, apply_absolute_value=True)
                tok = b.sV.inc(ins)
                b.tk[f"fg_ev{hg}"] = tok
                b.bfree(tok, 0, 1)

        @b.on("vector")
        def _(v):
            b.w(v, b.tk["fg_ev7"])
            b.ch(v, v.tensor_reduce(b.STAT[:, 141:142], b.STAT[:, 133:141],
                                    AX.X, ALU.max))
            b.ch(v, v.tensor_scalar(b.STAT[:, 142:143], b.STAT[:, 141:142],
                                    1e-5, None, ALU.max))
            b.ch(v, v.reciprocal(b.STAT[:, 143:144], b.STAT[:, 142:143]))
            b.ch(v, v.tensor_scalar(b.STAT[:, 144:145], b.STAT[:, 143:144],
                                    127.0, None, ALU.mult))
            b.ch(v, v.reciprocal(b.STAT[:, 145:146], b.STAT[:, 144:145]))
            t_ = b.TMP[:, 0:1024]
            ins = None
            for q in range(4):
                sl = slice(q * 1024, (q + 1) * 1024)
                v.tensor_scalar(t_, FSCR[:, sl], b.STAT[:, 144:145], None, ALU.mult)
                v.tensor_copy(TMPI[:, :1024], t_)
                u_ = FSCR[:, sl]
                v.tensor_sub(u_, t_, TMPI[:, :1024])
                v.scalar_tensor_tensor(t_, u_, 0.5, TMPI[:, :1024], ALU.is_gt, ALU.add)
                v.scalar_tensor_tensor(u_, u_, -0.5, t_, ALU.is_ge, ALU.add)
                ins = v.tensor_scalar(XQ2[:, sl], u_, -1.0, None, ALU.add)
            b.tk["f2_xq2"] = b.sV.inc(ins)

        @b.on("tensor")
        def _(te):
            b.w(te, b.tk["f2_xq2"])
            b.bw(te, 2)
            ins = te.transpose(b.pb[2][0:1, 0:128], b.STAT[:, 145:146], b.IDENT[:, :])
            b.tk["f2_qr_ps"] = b.tdrain(te)

        @b.on("scalar")
        def _(s):
            b.w(s, b.tk["f2_qr_ps"])
            ins = s.copy(b.ROW1[0:1, 128:256], b.pb[2][0:1, 0:128])
            tok = b.sS.inc(ins)
            b.tk["f2_qr"] = tok
            b.bfree(tok, 2)

        @b.on("vector")
        def _(v):
            b.w(v, b.tk["f2_qr"])
            ins = v.tensor_scalar(b.ROW1[0:1, 256:384], b.ROW1[0:1, 128:256],
                                  b.SCAL[0:1, 23:24], None, ALU.mult)
            b.tk["f2_c2r"] = b.sV.inc(ins)

        for q in range(4):
            @b.on("tensor")
            def _(te, q=q):
                b.w(te, b.tk["f2_xq2"])
                b.bw(te, 4, 5)
                last = None
                for j in range(8):
                    c2 = q * 8 + j
                    bank = b.pb[4 + (j // 4)]
                    last = te.transpose(bank[:, (j % 4) * 128:(j % 4 + 1) * 128],
                                        XQ2[:, c2 * 128:(c2 + 1) * 128], b.IDENT[:, :])
                b.tk[f"f2_tp{q}"] = b.tdrain(te)

            @b.on("scalar")
            def _(s, q=q):
                b.w(s, b.tk[f"f2_tp{q}"])
                ins = None
                for j in range(8):
                    c2 = q * 8 + j
                    bank = b.pb[4 + (j // 4)]
                    ins = s.copy(XQ2FM[:, c2 * 128:(c2 + 1) * 128],
                                 bank[:, (j % 4) * 128:(j % 4 + 1) * 128])
                tok = b.sS.inc(ins)
                b.tk[f"f2_tpc{q}"] = tok
                b.bfree(tok, 4, 5)

        if DEBUG and tt == 0:
            @b.on("gpsimd")
            def _(g):
                b.w(g, b.tk["fq_am"])
                g.dma_start(out=b.h_dbgxn, in_=FSCR[:, 0:1024]).then_inc(b.sO.h, 16)
                b.sO.n += 16
                b.tk["dbgxn"] = b.sO.tok()

            @b.on("vector")
            def _(v):
                b.w(v, b.tk["dbgxn"])

            @b.on("gpsimd")
            def _(g):
                b.w(g, b.tk["fq_sx"])
                g.dma_start(out=b.h_dbgst, in_=b.STAT[:, 130:146]).then_inc(b.sO.h, 16)
                b.sO.n += 16
                b.tk["dbgst"] = b.sO.tok()

            @b.on("vector")
            def _(v):
                b.w(v, b.tk["dbgst"])

            @b.on("gpsimd")
            def _(g):
                b.w(g, b.tk["fq_xq"], b.tk["fg_ev7"], b.tk["f2_xq2"],
                    b.tk["f2_tpc3"], b.tk["f2_c2r"], b.tk["fq_sx"])
                for dst, src in ((b.h_dbgxq, XQT[:, :]),
                                 (b.h_dbggu, FSCR[:, :]),
                                 (b.h_dbgxq2, XQ2[:, :]),
                                 (b.h_dbgfm, XQ2FM[:, :]),
                                 (b.h_dbgrow, b.ROW1[0:1, :]),
                                 (b.h_dbgqc, QC[:, :])):
                    g.dma_start(out=dst, in_=src).then_inc(b.sO.h, 16)
                    b.sO.n += 16
                b.tk["dbgf"] = b.sO.tok()

            @b.on("vector")
            def _(v):
                b.w(v, b.tk["dbgf"])

        tbase = b.sT.n
        for c2 in range(32):
            @b.on("sync")
            def _(s, c2=c2):
                if c2 >= 2:
                    b.w(s, (b.sT, tbase + c2 - 1))
                else:
                    b.w(s, (b.sT, tbase))
                sm = b.sD0 if c2 % 2 == 0 else b.sD1
                sm.n += 16
                s.dma_start(out=WQD[:, (c2 % 2) * 1024:(c2 % 2) * 1024 + 1024],
                            in_=self.h_wqd[c2 * 128:(c2 + 1) * 128, :]).then_inc(sm.h, 16)
                b.tk["fd_wdma"] = sm.tok()

            @b.on("tensor")
            def _(te, c2=c2):
                b.w(te, b.tk["fd_wdma"], b.tk["f2_tpc3"])
                if c2 == 0:
                    b.bw(te, 6, 7)
                last = None
                for dc in range(8):
                    bank = b.pb[6 + (dc // 4)]
                    last = te.matmul(bank[:, (dc % 4) * 128:(dc % 4 + 1) * 128],
                                     WQD[:, (c2 % 2) * 1024 + dc * 128:
                                         (c2 % 2) * 1024 + (dc + 1) * 128],
                                     XQ2FM[:, c2 * 128:(c2 + 1) * 128],
                                     start=(c2 == 0 and dc % 4 == 0),
                                     stop=(c2 == 31 and dc % 4 == 3))
                b.sT.inc(last)
                if c2 == 31:
                    b.tk["fd_mm"] = b.tdrain(te)
        assert b.sT.n == tbase + 33

        @b.on("tensor")
        def _(te):
            b.w(te, b.tk["f2_c2r"])
            b.bw(te, 2)
            ins = te.matmul(b.pb[2][:, 128:256], b.ONES[0:1, :], b.ROW1[0:1, 256:384],
                            start=True, stop=True)
            b.tk["fd_bc"] = b.tdrain(te)

        @b.on("scalar")
        def _(s):
            b.w(s, b.tk["fd_bc"])
            ins = s.copy(b.TMP[:, 896:1024], b.pb[2][:, 128:256])
            tok = b.sS.inc(ins)
            b.tk["fd_bcs"] = tok
            b.bfree(tok, 2)

        @b.on("vector")
        def _(v):
            b.w(v, b.tk["fd_mm"], b.tk["fd_bcs"])
            ins = None
            for dc in range(8):
                bank = b.pb[6 + (dc // 4)]
                t_ = b.TMP[:, 0:128]
                v.tensor_mul(t_, bank[:, (dc % 4) * 128:(dc % 4 + 1) * 128],
                             b.TMP[:, 896:1024])
                ins = v.tensor_add(b.ACC[:, dc * T + t0:dc * T + t0 + 128],
                                   b.ACC[:, dc * T + t0:dc * T + t0 + 128], t_)
            tok = b.sV.inc(ins)
            b.tk["fd_acc"] = tok
            b.bfree(tok, 6, 7)

    def attn_pass(self, dr):
        b = self
        nc = self.nc
        with ExitStack() as astk:
            asb = lambda name, shape, dt=F32: astk.enter_context(
                nc.sbuf_tensor(name, shape, dt))
            XB = asb(f"XB{dr}", [128, 16 * 264])
            DT = asb(f"DT{dr}", [128, 16 * 256])
            XCD = asb(f"XCD{dr}", [128, 16 * 256])
            Y = asb(f"Y{dr}", [128, 16 * 256])
            XIN = asb(f"XIN{dr}", [128, 8 * 256])
            XDBL = asb(f"XDBL{dr}", [128, 256])
            ADA = asb(f"ADA{dr}", [128, 8 * 256])
            DBX = asb(f"DBX{dr}", [128, 256])
            HS = asb(f"HS{dr}", [128, 256])
            HCAR = asb(f"HCAR{dr}", [128, 256])
            XPT = asb(f"XPT{dr}", [128, 16 * 96])

            @b.on("sync")
            def _(s):
                b.w(s, b.tk.get(f"apdone{dr - 1}"))
                b.sX.n += 16
                s.dma_start(out=XPT[:, :], in_=bass.AP(
                    b.h_xpT.tensor, dr * DI * 96, [[96, 128], [128 * 96, 16], [1, 96]])
                ).then_inc(b.sX.h, 16)
                b.tk["ap_w"] = b.sX.tok()

            @b.on("vector")
            def _(v):
                b.w(v, b.tk["ffn"])
                v.memset(HCAR[:, :], 0.0)
                ins = None
                for cc in range(16):
                    ins = v.memset(XB[:, cc * 264 + 1:cc * 264 + 4], 0.0)
                b.tk["ap_init"] = b.sV.inc(ins)

            for tau in range(NTC):
                self._attn_chunk(dr, tau, XB, DT, XCD, Y, XIN, XDBL,
                                 ADA, DBX, HS, HCAR, XPT)
            b.tk[f"apdone{dr}"] = b.sV.tok()

    def _attn_chunk(self, dr, tau, XB, DT, XCD, Y, XIN, XDBL,
                    ADA, DBX, HS, HCAR, XPT):
        b = self
        o0 = tau * TC if dr == 0 else T - TC * (tau + 1)
        vc0 = dr * 16

        def xstg_rhs(c):
            if dr == 0:
                return XIN[:, c * TC:(c + 1) * TC]
            return bass.AP(XIN, c * TC + TC - 1, [[8 * TC, 128], [-1, TC]])

        @b.on("scalar")
        def _(g):
            b.w(g, b.tk.get(f"ap_xstg_done{dr}_{tau - 1}"))
            for c in range(8):
                b.sXI.n += 16
                g.dma_start(out=XIN[:, c * TC:(c + 1) * TC],
                            in_=b.h_x[c * 128:(c + 1) * 128, o0:o0 + TC]
                            ).then_inc(b.sXI.h, 16)
            b.tk["ap_xin"] = b.sXI.tok()

        @b.on("tensor")
        def _(te):
            b.w(te, b.tk["ln0"])
            b.bw(te, 2)
            te.matmul(b.pb[2][:, 0:TC], b.selbc(0, 64),
                      b.rln_rhs(0, o0, TC), start=True, stop=True)
            ins = te.matmul(b.pb[2][:, TC:2 * TC], b.selbc(32, 64),
                            b.rln_rhs(0, o0, TC), start=True, stop=True)
            b.tk["ap_msbc"] = b.tdrain(te)

        @b.on("vector")
        def _(v):
            b.w(v, b.tk["ap_msbc"], b.tk["ap_xin"], b.tk["ap_init"])
            b.w(v, b.tk.get(f"ap_xstg_use{dr}_{tau - 1}"))
            ins = None
            for c in range(8):
                xs = XIN[:, c * TC:(c + 1) * TC]
                v.tensor_sub(xs, xs, b.pb[2][:, 0:TC])
                ins = v.tensor_mul(xs, xs, b.pb[2][:, TC:2 * TC])
            tok = b.sV.inc(ins)
            b.tk["ap_xstg"] = tok
            b.tk[f"ap_xstg_done{dr}_{tau}"] = tok
            b.bfree(tok, 2)

        for mw in range(4):
            self._inproj_mw(dr, tau, mw, xstg_rhs, XB, None, is_z=False)

        for cc in range(16):
            @b.on("vector")
            def _(v, cc=cc):
                b.w(v, b.tk[f"ap_evac{dr}_{tau}_{cc // 4}"])
                b.w(v, b.tk.get(f"ap_silu{dr}_{tau}_{cc - 4}"))
                x0 = cc * 264
                t_ = b.TMP[:, (cc % 4) * 256:(cc % 4) * 256 + 256]
                vb = (vc0 + cc) * 8
                v.tensor_scalar(t_, XB[:, x0 + 1:x0 + 257],
                                b.VEC0[:, vb:vb + 1], b.VEC0[:, vb + 4:vb + 5],
                                ALU.mult, ALU.add)
                for k in (1, 2, 3):
                    v.scalar_tensor_tensor(t_, XB[:, x0 + 1 + k:x0 + 257 + k],
                                           b.VEC0[:, vb + k:vb + k + 1], t_,
                                           ALU.mult, ALU.add)
                ins = v.tensor_copy(XB[:, x0 + 1:x0 + 4], XB[:, x0 + 257:x0 + 260])
                b.tk[f"ap_conv{dr}_{tau}_{cc}"] = b.sV.inc(ins)

            @b.on("scalar")
            def _(s, cc=cc):
                b.w(s, b.tk[f"ap_conv{dr}_{tau}_{cc}"])
                t_ = b.TMP[:, (cc % 4) * 256:(cc % 4) * 256 + 256]
                ins = s.activation(XCD[:, cc * 256:(cc + 1) * 256], t_, AF.Silu)
                b.tk[f"ap_silu{dr}_{tau}_{cc}"] = b.sS.inc(ins)

        @b.on("tensor")
        def _(te):
            b.w(te, b.tk["ap_w"])
            b.bw(te, 4)
            last = None
            for cc in range(16):
                b.w(te, b.tk[f"ap_silu{dr}_{tau}_{cc}"])
                last = te.matmul(b.pb[4][0:96, 0:TC], XPT[:, cc * 96:(cc + 1) * 96],
                                 XCD[:, cc * 256:(cc + 1) * 256],
                                 start=(cc == 0), stop=(cc == 15))
            b.tk["ap_xp"] = b.tdrain(te)

        @b.on("scalar")
        def _(s):
            b.w(s, b.tk["ap_xp"])
            ins = s.copy(XDBL[0:96, :], b.pb[4][0:96, 0:TC])
            tok = b.sS.inc(ins)
            b.tk["ap_xdbl"] = tok
            b.bfree(tok, 4)

        @b.on("sync")
        def _(s):
            b.w(s, b.win2_use[0], b.win2_use[1], b.win2_use[2], b.win2_use[3])
            b.sR[0].n += 16
            s.dma_start(out=b.WIN2[0:64, :], in_=b.h_dtT[dr * 64:(dr + 1) * 64, :]
                        ).then_inc(b.sR[0].h, 16)
            b.tk["ap_dtw"] = b.sR[0].tok()

        for mq in range(4):
            @b.on("tensor")
            def _(te, mq=mq):
                b.w(te, b.tk["ap_xdbl"], b.tk["ap_dtw"])
                b.bw(te, 0, 1, 2, 3)
                last = None
                for j in range(4):
                    m = mq * 4 + j
                    last = te.matmul(b.pb[j][:, 0:TC],
                                     b.WIN2[0:64, m * 128:(m + 1) * 128],
                                     XDBL[0:64, 0:TC], start=True, stop=True)
                tok = b.tdrain(te)
                b.tk[f"ap_dtmm{dr}_{tau}_{mq}"] = tok
                if mq == 3:
                    for j in range(4):
                        b.win2_use[j] = tok

            @b.on("scalar")
            def _(s, mq=mq):
                b.w(s, b.tk[f"ap_dtmm{dr}_{tau}_{mq}"])
                ins = None
                for j in range(4):
                    m = mq * 4 + j
                    vb = (vc0 + m) * 8
                    t_ = b.TMP[:, 512 + (j % 2) * 256:512 + (j % 2) * 256 + 256]
                    s.activation(t_, b.pb[j][:, 0:TC], AF.Exp,
                                 bias=b.VEC0[:, vb + 5:vb + 6], scale=1.0)
                    ins = s.activation(DT[:, m * 256:(m + 1) * 256], t_, AF.Ln,
                                       bias=1.0, scale=1.0)
                tok = b.sS.inc(ins)
                b.tk[f"ap_dtev{dr}_{tau}_{mq}"] = tok
                b.bfree(tok, 0, 1, 2, 3)

        @b.on("vector")
        def _(v):
            b.w(v, b.tk["ap_xp"])
            for mq in range(4):
                b.w(v, b.tk[f"ap_dtev{dr}_{tau}_{mq}"])
            b.w(v, b.tk.get(f"ap_opmm{dr}_{tau - 1}_1_15"))
            for cc in range(16):
                vb = (vc0 + cc) * 8
                v.tensor_scalar(Y[:, cc * 256:(cc + 1) * 256],
                                XCD[:, cc * 256:(cc + 1) * 256],
                                b.VEC0[:, vb + 6:vb + 7], None, ALU.mult)
            ins = None
            for cc in range(16):
                ins = v.tensor_mul(XCD[:, cc * 256:(cc + 1) * 256],
                                   DT[:, cc * 256:(cc + 1) * 256],
                                   XCD[:, cc * 256:(cc + 1) * 256])
            b.tk["ap_dtx"] = b.sV.inc(ins)

        for n in range(NST):
            bank = b.pb[4 + (n % 2)]

            @b.on("tensor")
            def _(te, n=n, bank=bank):
                b.w(te, b.tk["ap_xdbl"])
                b.bw(te, 4 + (n % 2))
                te.matmul(bank[:, 0:TC], b.selbc(64 + n, 96),
                          XDBL[0:96, 0:TC], start=True, stop=True)
                te.matmul(bank[:, TC:2 * TC], b.selbc(80 + n, 96),
                          XDBL[0:96, 0:TC], start=True, stop=True)
                b.tk[f"ap_bc{dr}_{tau}_{n}"] = b.tdrain(te)

            @b.on("scalar")
            def _(s, n=n, bank=bank):
                b.w(s, b.tk[f"ap_bc{dr}_{tau}_{n}"])
                b.w(s, b.tk["ap_dtx"] if n == 0 else b.tk[f"ap_scv{dr}_{tau}_{n - 1}_1"])
                s.copy(b.STAT[:, (n % 2) * 512:(n % 2) * 512 + 256], bank[:, 0:TC])
                ins = s.copy(b.STAT[:, (n % 2) * 512 + 256:(n % 2) * 512 + 512],
                             bank[:, TC:2 * TC])
                tok = b.sS.inc(ins)
                b.tk[f"ap_bccp{dr}_{tau}_{n}"] = tok
                b.bfree(tok, 4 + (n % 2))

            for h in range(2):
                @b.on("scalar")
                def _(s, n=n, h=h):
                    b.w(s, b.tk["aprep"])
                    if h == 1:
                        b.w(s, b.tk[f"ap_scv{dr}_{tau}_{n}_0"])
                    ins = None
                    for j in range(8):
                        cc = h * 8 + j
                        ins = s.activation(ADA[:, j * 256:(j + 1) * 256],
                                           DT[:, cc * 256:(cc + 1) * 256], AF.Exp,
                                           bias=0.0,
                                           scale=b.A0SB[:, (vc0 + cc) * NST + n:
                                                        (vc0 + cc) * NST + n + 1])
                    b.tk[f"ap_da{dr}_{tau}_{n}_{h}"] = b.sS.inc(ins)

                @b.on("vector")
                def _(v, n=n, h=h):
                    b.w(v, b.tk[f"ap_da{dr}_{tau}_{n}_{h}"],
                        b.tk[f"ap_bccp{dr}_{tau}_{n}"], b.tk["ap_dtx"])
                    Bbc = b.STAT[:, (n % 2) * 512:(n % 2) * 512 + 256]
                    Cbc = b.STAT[:, (n % 2) * 512 + 256:(n % 2) * 512 + 512]
                    ins = None
                    for j in range(8):
                        cc = h * 8 + j
                        q = cc * NST + n
                        v.tensor_mul(DBX[:, :], XCD[:, cc * 256:(cc + 1) * 256], Bbc)
                        v.tensor_tensor_scan(HS[:, :], ADA[:, j * 256:(j + 1) * 256],
                                             DBX[:, :], HCAR[:, q:q + 1],
                                             ALU.mult, ALU.add)
                        v.tensor_mul(DBX[:, :], HS[:, :], Cbc)
                        v.tensor_add(Y[:, cc * 256:(cc + 1) * 256],
                                     Y[:, cc * 256:(cc + 1) * 256], DBX[:, :])
                        ins = v.tensor_copy(HCAR[:, q:q + 1], HS[:, 255:256])
                    b.tk[f"ap_scv{dr}_{tau}_{n}_{h}"] = b.sV.inc(ins)
        b.tk[f"ap_scan{dr}_{tau}"] = b.sV.tok()

        for mw in range(4, 8):
            self._inproj_mw(dr, tau, mw, xstg_rhs, XB, Y, is_z=True)
        b.tk["ap_ydone"] = b.sV.tok()
        b.tk[f"ap_xstg_use{dr}_{tau}"] = b.sT.tok()

        for ow in range(2):
            for c in range(16):
                slot = c % 4

                @b.on("sync")
                def _(s, ow=ow, c=c, slot=slot):
                    b.w(s, b.win2_use[slot])
                    b.sR[slot].n += 16
                    s.dma_start(out=b.WIN2[:, slot * 512:slot * 512 + 512],
                                in_=b.h_outT[c * 128:(c + 1) * 128,
                                             ow * 512:(ow + 1) * 512]).then_inc(b.sR[slot].h, 16)
                    b.tk[f"ap_opw{dr}_{tau}_{ow}_{c}"] = b.sR[slot].tok()

                @b.on("tensor")
                def _(te, ow=ow, c=c, slot=slot):
                    b.w(te, b.tk[f"ap_opw{dr}_{tau}_{ow}_{c}"])
                    if c == 0:
                        b.w(te, b.tk["ap_ydone"])
                        b.bw(te, 6, 7)
                    last = None
                    for m in range(4):
                        bank = b.pb[6 + (m // 2)]
                        last = te.matmul(bank[:, (m % 2) * 256:(m % 2) * 256 + 256],
                                         b.WIN2[:, slot * 512 + m * 128:
                                                slot * 512 + (m + 1) * 128],
                                         Y[:, c * 256:(c + 1) * 256],
                                         start=(c == 0 and m % 2 == 0),
                                         stop=(c == 15 and m % 2 == 1))
                    tok = b.tdrain(te) if c == 15 else b.sT.inc(last)
                    b.tk[f"ap_opmm{dr}_{tau}_{ow}_{c}"] = tok
                    b.win2_use[slot] = tok
                    if ow == 1 and c == 15:
                        b.tk[f"ap_opmm{dr}_{tau}_1_15"] = tok

            @b.on("vector")
            def _(v, ow=ow):
                b.w(v, b.tk[f"ap_opmm{dr}_{tau}_{ow}_{15}"])
                ins = None
                for m in range(4):
                    mc = ow * 4 + m
                    bank = b.pb[6 + (m // 2)]
                    dst = b.ACC[:, mc * T + o0:mc * T + o0 + TC]
                    if dr == 0:
                        src = bank[:, (m % 2) * 256:(m % 2) * 256 + 256]
                    else:
                        src = bass.AP(bank, (m % 2) * 256 + 255, [[512, 128], [-1, 256]])
                    ins = v.tensor_add(dst, dst, src)
                tok = b.sV.inc(ins)
                b.tk[f"ap_opev{dr}_{tau}_{ow}"] = tok
                b.bfree(tok, 6, 7)

    def _inproj_mw(self, dr, tau, mw, xstg_rhs, XB, Y, is_z):
        b = self
        for c in range(8):
            slot = c % 4

            @b.on("sync")
            def _(s, mw=mw, c=c, slot=slot):
                b.w(s, b.win_use[slot])
                b.sQ[slot].n += 16
                s.dma_start(out=b.WIN[:, slot * 512:slot * 512 + 512],
                            in_=b.h_inT[c * 128:(c + 1) * 128,
                                        mw * 512:(mw + 1) * 512]).then_inc(b.sQ[slot].h, 16)
                b.tk[f"ap_ipw{dr}_{tau}_{mw}_{c}"] = b.sQ[slot].tok()

            @b.on("tensor")
            def _(te, mw=mw, c=c, slot=slot):
                b.w(te, b.tk[f"ap_ipw{dr}_{tau}_{mw}_{c}"])
                if c == 0:
                    b.w(te, b.tk["ap_xstg"])
                    b.bw(te, 0, 1, 2, 3)
                last = None
                for m in range(4):
                    last = te.matmul(b.pb[m][:, (mw % 2) * 256:(mw % 2) * 256 + 256],
                                     b.WIN[:, slot * 512 + m * 128:
                                            slot * 512 + (m + 1) * 128],
                                     xstg_rhs(c),
                                     start=(c == 0), stop=(c == 7))
                tok = b.tdrain(te) if c == 7 else b.sT.inc(last)
                b.win_use[slot] = tok
                if c == 7:
                    b.tk[f"ap_ipmm{dr}_{tau}_{mw}"] = tok

        if not is_z:
            @b.on("scalar")
            def _(s, mw=mw):
                b.w(s, b.tk[f"ap_ipmm{dr}_{tau}_{mw}"])
                b.w(s, b.tk.get(f"ap_conv{dr}_{tau - 1}_{mw * 4 + 3}"))
                ins = None
                for m in range(4):
                    cc = mw * 4 + m
                    ins = s.copy(XB[:, cc * 264 + 4:cc * 264 + 260],
                                 b.pb[m][:, (mw % 2) * 256:(mw % 2) * 256 + 256])
                tok = b.sS.inc(ins)
                b.tk[f"ap_evac{dr}_{tau}_{mw}"] = tok
                b.bfree(tok, 0, 1, 2, 3)
        else:
            @b.on("scalar")
            def _(s, mw=mw):
                b.w(s, b.tk[f"ap_ipmm{dr}_{tau}_{mw}"])
                ins = None
                for m in range(4):
                    ins = s.activation(b.TMP[:, m * 256:(m + 1) * 256],
                                       b.pb[m][:, (mw % 2) * 256:(mw % 2) * 256 + 256],
                                       AF.Silu)
                tok = b.sS.inc(ins)
                b.tk[f"ap_zsilu{dr}_{tau}_{mw}"] = tok
                b.bfree(tok, 0, 1, 2, 3)

            @b.on("vector")
            def _(v, mw=mw):
                b.w(v, b.tk[f"ap_zsilu{dr}_{tau}_{mw}"])
                ins = None
                for m in range(4):
                    cc = (mw - 4) * 4 + m
                    ins = v.tensor_mul(Y[:, cc * 256:(cc + 1) * 256],
                                       Y[:, cc * 256:(cc + 1) * 256],
                                       b.TMP[:, m * 256:(m + 1) * 256])
                b.tk[f"ap_zg{dr}_{tau}_{mw}"] = b.sV.inc(ins)

    def layer1(self):
        b = self
        nc = self.nc
        with ExitStack() as lstk:
            lsb = lambda name, shape, dt=F32: lstk.enter_context(
                nc.sbuf_tensor(name, shape, dt))
            X1P = lsb("LX1P", [128, 8 * 512])
            DT1 = lsb("LDT1", [128, 8 * 512])
            Y1 = lsb("LY1", [128, 8 * 512])
            XN1G = lsb("LXN1G", [128, 8 * 512])
            XD1B = lsb("LXD1B", [128, 512])
            XD1C = lsb("LXD1C", [128, 512])
            ADA1 = lsb("LADA1", [128, 4 * 512])
            DBX1 = lsb("LDBX1", [128, 512])
            HS1 = lsb("LHS1", [128, 512])
            HC1 = lsb("LHC1", [128, 128])
            XG = lsb("LXG", [128, 8 * 256])
            GBC = lsb("LGBC", [128, 256])
            HE = lsb("LHE", [128, 16 * 256])

            @b.on("vector")
            def _(v):
                b.w(v, b.tk["ln1"])
                ins = v.memset(HC1[:, :], 0.0)
                b.tk["l1_init"] = b.sV.inc(ins)

            for g in range(4):
                self._gating_group(g)
                self._xn1_group(g, XN1G)
                self._l1attn_group(g, X1P, DT1, Y1, XN1G, XD1B, XD1C,
                                   ADA1, DBX1, HS1, HC1)
                for sg in range(2):
                    self._moe_subgroup(g, sg, XN1G, XG, GBC, HE)

    def _gating_group(self, g):
        b = self
        s0 = g * 512

        @b.on("tensor")
        def _(te):
            b.w(te, b.tk["ln1"], b.tk["wsume"])
            b.bw(te, 2, 3, 4)
            for c in range(8):
                te.matmul(b.pb[2][0:8, 0:512], b.GATEW[:, c * 8:(c + 1) * 8],
                          b.ACC[:, c * T + s0:c * T + s0 + 512],
                          start=(c == 0), stop=(c == 7))
            te.matmul(b.pb[3][:, 0:512], b.selbc(0, 64), b.rln_rhs(1, s0, 512),
                      start=True, stop=True)
            ins = te.matmul(b.pb[4][:, 0:512], b.selbc(32, 64), b.rln_rhs(1, s0, 512),
                            start=True, stop=True)
            b.tk["gt_mm"] = b.tdrain(te)

        @b.on("vector")
        def _(v):
            b.w(v, b.tk["gt_mm"])
            GL = b.STAT[0:8, 0:512]
            v.tensor_copy(GL, b.pb[2][0:8, 0:512])
            v.scalar_tensor_tensor(GL, b.pb[3][0:8, 0:512], b.SCAL[0:8, 16:17],
                                   GL, ALU.mult, ALU.subtract)
            v.tensor_mul(GL, GL, b.pb[4][0:8, 0:512])
            ins = v.tensor_scalar(GL, GL, -1.0, None, ALU.mult)
            tok = b.sV.inc(ins)
            b.tk["gt_log"] = tok
            b.bfree(tok, 2, 3, 4)

        for j in range(4):
            @b.on("tensor")
            def _(te, j=j):
                b.w(te, b.tk["gt_log"])
                b.bw(te, 5)
                ins = te.transpose(b.pb[5][:, 0:8], b.STAT[0:8, j * 128:(j + 1) * 128],
                                   b.IDENT[0:8, 0:8])
                b.tk["gt_tp"] = b.tdrain(te)

            @b.on("scalar")
            def _(s, j=j):
                b.w(s, b.tk["gt_tp"])
                b.w(s, b.tk.get("gt_gtok_free"))
                ins = s.copy(b.GTOK[:, 0:8], b.pb[5][:, 0:8])
                tok = b.sS.inc(ins)
                b.tk["gt_tok"] = tok
                b.bfree(tok, 5)

            @b.on("vector")
            def _(v, j=j):
                b.w(v, b.tk["gt_tok"])
                ins = v.reduce_max(b.SCAL[:, 24:25], b.GTOK[:, 0:8], AX.X, negate=True)
                b.tk["gt_nm"] = b.sV.inc(ins)

            @b.on("scalar")
            def _(s, j=j):
                b.w(s, b.tk["gt_nm"])
                ins = s.activation(b.GTOK[:, 0:8], b.GTOK[:, 0:8], AF.Exp,
                                   bias=b.SCAL[:, 24:25], scale=1.0)
                b.tk["gt_exp"] = b.sS.inc(ins)

            @b.on("vector")
            def _(v, j=j):
                b.w(v, b.tk["gt_exp"])
                P = b.GTOK[:, 0:8]
                sc = b.SCAL
                b.ch(v, v.reduce_sum(sc[:, 25:26], P, AX.X))
                b.ch(v, v.reciprocal(sc[:, 26:27], sc[:, 25:26]))
                b.ch(v, v.tensor_scalar(P, P, sc[:, 26:27], None, ALU.mult))
                b.ch(v, v.reduce_max(sc[:, 27:28], P, AX.X))
                b.ch(v, v.scalar_tensor_tensor(b.TMP[:, 0:8], P, sc[:, 27:28], P,
                                               ALU.is_lt, ALU.mult))
                b.ch(v, v.reduce_max(sc[:, 28:29], b.TMP[:, 0:8], AX.X))
                b.ch(v, v.scalar_tensor_tensor(P, P, sc[:, 28:29], P,
                                               ALU.is_ge, ALU.mult))
                b.ch(v, v.reduce_sum(sc[:, 29:30], P, AX.X))
                b.ch(v, v.tensor_scalar(sc[:, 29:30], sc[:, 29:30], 1e-12,
                                        None, ALU.max))
                b.ch(v, v.reciprocal(sc[:, 30:31], sc[:, 29:30]))
                ins = v.tensor_scalar(P, P, sc[:, 30:31], None, ALU.mult)
                b.tk["gt_p"] = b.sV.inc(ins)

            @b.on("tensor")
            def _(te, j=j):
                b.w(te, b.tk["gt_p"])
                b.bw(te, 5)
                ins = te.transpose(b.pb[5][0:8, 128:256], b.GTOK[:, 0:8],
                                   b.IDENT[:, :])
                b.tk["gt_ptp"] = b.tdrain(te)

            @b.on("scalar")
            def _(s, j=j, g=g):
                b.w(s, b.tk["gt_ptp"])
                ins = s.copy(b.GROWS[0:8, s0 + j * 128:s0 + (j + 1) * 128],
                             b.pb[5][0:8, 128:256])
                tok = b.sS.inc(ins)
                b.bfree(tok, 5)
                b.tk["gt_gtok_free"] = tok
        b.tk[f"gt_done{g}"] = b.sS.tok()

    def _xn1_group(self, g, XN1G):
        b = self
        s0 = g * 512

        @b.on("tensor")
        def _(te):
            b.w(te, b.tk["ln1"])
            b.bw(te, 2, 3)
            te.matmul(b.pb[2][:, 0:512], b.selbc(0, 64), b.rln_rhs(1, s0, 512),
                      start=True, stop=True)
            ins = te.matmul(b.pb[3][:, 0:512], b.selbc(32, 64), b.rln_rhs(1, s0, 512),
                            start=True, stop=True)
            b.tk["xn1_bc"] = b.tdrain(te)

        @b.on("vector")
        def _(v, g=g):
            b.w(v, b.tk["xn1_bc"], b.tk.get(f"xn1_cons{g - 1}"))
            ins = None
            for c in range(8):
                xs = XN1G[:, c * 512:(c + 1) * 512]
                v.tensor_sub(xs, b.ACC[:, c * T + s0:c * T + s0 + 512], b.pb[2][:, 0:512])
                ins = v.tensor_mul(xs, xs, b.pb[3][:, 0:512])
            tok = b.sV.inc(ins)
            b.tk[f"xn1_{g}"] = tok
            b.bfree(tok, 2, 3)

    def _l1attn_group(self, g, X1P, DT1, Y1, XN1G, XD1B, XD1C,
                      ADA1, DBX1, HS1, HC1):
        b = self
        s0 = g * 512

        for mw in range(2):
            self._l1_proj_mw(g, b.h_wxz, mw * 512, mw, XN1G, "xz", X1P,
                             kind="copy", rdy=b.tk[f"xn1_{g}"])
        b.tk[f"l1x_xz{g}"] = b.sS.tok()
        for mw in range(2):
            self._l1_proj_mw(g, b.h_wdt, mw * 512, mw, X1P, "dt", DT1,
                             kind="softplus", rdy=b.tk[f"l1x_xz{g}"])
        b.tk[f"l1x_dt{g}"] = b.sS.tok()

        @b.on("tensor")
        def _(te):
            b.w(te, b.tk[f"xn1_{g}"])
            b.bw(te, 4, 5)
            last = None
            for c in range(8):
                last = te.matmul(b.pb[4][0:16, 0:512], b.WBSB[:, c * NST:(c + 1) * NST],
                                 XN1G[:, c * 512:(c + 1) * 512],
                                 start=(c == 0), stop=(c == 7))
            for c in range(8):
                last = te.matmul(b.pb[5][0:16, 0:512], b.WCSB[:, c * NST:(c + 1) * NST],
                                 XN1G[:, c * 512:(c + 1) * 512],
                                 start=(c == 0), stop=(c == 7))
            b.tk["l1_bcmm"] = b.tdrain(te)

        @b.on("scalar")
        def _(s):
            b.w(s, b.tk["l1_bcmm"])
            s.copy(XD1B[0:16, :], b.pb[4][0:16, 0:512])
            ins = s.copy(XD1C[0:16, :], b.pb[5][0:16, 0:512])
            tok = b.sS.inc(ins)
            b.tk["l1_bcc"] = tok
            b.bfree(tok, 4, 5)

        @b.on("vector")
        def _(v, g=g):
            b.w(v, b.tk[f"l1x_dt{g}"])
            for c in range(8):
                v.tensor_scalar(Y1[:, c * 512:(c + 1) * 512],
                                X1P[:, c * 512:(c + 1) * 512],
                                bass.AP(b.VEC1, c * 4 + 1, [[32, 128], [4, 1]]),
                                None, ALU.mult)
            ins = None
            for c in range(8):
                ins = v.tensor_mul(X1P[:, c * 512:(c + 1) * 512],
                                   DT1[:, c * 512:(c + 1) * 512],
                                   X1P[:, c * 512:(c + 1) * 512])
            b.tk["l1_dtx"] = b.sV.inc(ins)

        for n in range(NST):
            bB = b.pb[4 + 2 * (n % 2)]
            bC = b.pb[5 + 2 * (n % 2)]

            @b.on("tensor")
            def _(te, n=n, bB=bB, bC=bC):
                b.w(te, b.tk["l1_bcc"])
                b.bw(te, 4 + 2 * (n % 2), 5 + 2 * (n % 2))
                te.matmul(bB[:, 0:512], b.selbc(n, 16), XD1B[0:16, 0:512],
                          start=True, stop=True)
                te.matmul(bC[:, 0:512], b.selbc(n, 16), XD1C[0:16, 0:512],
                          start=True, stop=True)
                b.tk[f"l1_nbc{g}_{n}"] = b.tdrain(te)

            for h in range(2):
                @b.on("scalar")
                def _(s, n=n, h=h):
                    b.w(s, b.tk["aprep"])
                    if h == 0:
                        b.w(s, b.tk["l1_dtx"] if n == 0
                            else b.tk[f"l1_scv{g}_{n - 1}_1"])
                    else:
                        b.w(s, b.tk[f"l1_scv{g}_{n}_0"])
                    ins = None
                    for j in range(4):
                        c = h * 4 + j
                        ins = s.activation(ADA1[:, j * 512:(j + 1) * 512],
                                           DT1[:, c * 512:(c + 1) * 512], AF.Exp,
                                           bias=0.0,
                                           scale=b.A1SB[:, c * NST + n:c * NST + n + 1])
                    b.tk[f"l1_da{g}_{n}_{h}"] = b.sS.inc(ins)

                @b.on("vector")
                def _(v, n=n, g=g, h=h, bB=bB, bC=bC):
                    b.w(v, b.tk[f"l1_da{g}_{n}_{h}"], b.tk[f"l1_nbc{g}_{n}"],
                        b.tk["l1_dtx"])
                    ins = None
                    for j in range(4):
                        c = h * 4 + j
                        q = c * NST + n
                        v.tensor_mul(DBX1[:, :], X1P[:, c * 512:(c + 1) * 512],
                                     bB[:, 0:512])
                        v.tensor_tensor_scan(HS1[:, :], ADA1[:, j * 512:(j + 1) * 512],
                                             DBX1[:, :], HC1[:, q:q + 1],
                                             ALU.mult, ALU.add)
                        v.tensor_mul(DBX1[:, :], HS1[:, :], bC[:, 0:512])
                        v.tensor_add(Y1[:, c * 512:(c + 1) * 512],
                                     Y1[:, c * 512:(c + 1) * 512], DBX1[:, :])
                        ins = v.tensor_copy(HC1[:, q:q + 1], HS1[:, 511:512])
                    tok = b.sV.inc(ins)
                    b.tk[f"l1_scv{g}_{n}_{h}"] = tok
                    if h == 1:
                        b.bfree(tok, 4 + 2 * (n % 2), 5 + 2 * (n % 2))
        b.tk[f"l1_scan{g}"] = b.sV.tok()

        for c in range(8):
            @b.on("scalar")
            def _(s, c=c):
                b.w(s, b.tk[f"l1_scan{g}"])
                b.w(s, b.tk.get(f"l1_spk{g}_{c - 2}"))
                ins = s.activation(b.TMP[:, (c % 2) * 512:(c % 2) * 512 + 512],
                                   Y1[:, c * 512:(c + 1) * 512], AF.Sigmoid,
                                   bias=b.SCAL[:, 8 + c:9 + c],
                                   scale=b.SCAL[:, 18:19])
                b.tk[f"l1_sg{g}_{c}"] = b.sS.inc(ins)

            @b.on("vector")
            def _(v, c=c):
                b.w(v, b.tk[f"l1_sg{g}_{c}"])
                ins = v.tensor_mul(Y1[:, c * 512:(c + 1) * 512],
                                   Y1[:, c * 512:(c + 1) * 512],
                                   b.TMP[:, (c % 2) * 512:(c % 2) * 512 + 512])
                b.tk[f"l1_spk{g}_{c}"] = b.sV.inc(ins)

        for mw in range(2):
            self._l1_proj_mw(g, b.h_wxz, D + mw * 512, mw, XN1G, f"z{mw}", Y1,
                             kind="siluz", zoff=mw * 4, rdy=b.tk[f"xn1_{g}"])
        b.tk[f"l1_yg{g}"] = b.sV.tok()

        for ow in range(2):
            for c in range(8):
                slot = c % 4

                @b.on("sync")
                def _(s, ow=ow, c=c, slot=slot):
                    b.w(s, b.win2_use[slot])
                    b.sR[slot].n += 16
                    s.dma_start(out=b.WIN2[:, slot * 512:slot * 512 + 512],
                                in_=b.h_wo1[c * 128:(c + 1) * 128,
                                            ow * 512:(ow + 1) * 512]).then_inc(b.sR[slot].h, 16)
                    b.tk[f"l1_opw{g}_{ow}_{c}"] = b.sR[slot].tok()

                @b.on("tensor")
                def _(te, ow=ow, c=c, slot=slot):
                    b.w(te, b.tk[f"l1_opw{g}_{ow}_{c}"])
                    if c == 0:
                        b.w(te, b.tk[f"l1_yg{g}"])
                        b.bw(te, 0, 1, 2, 3)
                    last = None
                    for m in range(4):
                        last = te.matmul(b.pb[m][:, 0:512],
                                         b.WIN2[:, slot * 512 + m * 128:
                                                slot * 512 + (m + 1) * 128],
                                         Y1[:, c * 512:(c + 1) * 512],
                                         start=(c == 0), stop=(c == 7))
                    tok = b.tdrain(te) if c == 7 else b.sT.inc(last)
                    b.win2_use[slot] = tok
                    b.tk[f"l1_opmm{g}_{ow}_{c}"] = tok

            @b.on("vector")
            def _(v, ow=ow):
                b.w(v, b.tk[f"l1_opmm{g}_{ow}_{7}"])
                ins = None
                for m in range(4):
                    mc = ow * 4 + m
                    dst = b.ACC[:, mc * T + s0:mc * T + s0 + 512]
                    v.tensor_add(dst, dst, b.pb[m][:, 0:512])
                    ins = v.tensor_sub(dst, dst, XN1G[:, mc * 512:(mc + 1) * 512])
                tok = b.sV.inc(ins)
                b.tk[f"l1_opev{g}_{ow}"] = tok
                b.bfree(tok, 0, 1, 2, 3)
        b.tk[f"l1_done{g}"] = b.sV.tok()

    def _l1_proj_mw(self, g, h_w, colbase, mw, RHS, tag, DST, kind="copy",
                    zoff=0, rdy=None):
        b = self

        for c in range(8):
            slot = c % 4

            @b.on("sync")
            def _(s, c=c, slot=slot):
                b.w(s, b.win_use[slot])
                b.sQ[slot].n += 16
                s.dma_start(out=b.WIN[:, slot * 512:slot * 512 + 512],
                            in_=h_w[c * 128:(c + 1) * 128, colbase:colbase + 512]
                            ).then_inc(b.sQ[slot].h, 16)
                b.tk[f"l1p_{tag}_w{g}_{c}"] = b.sQ[slot].tok()

            @b.on("tensor")
            def _(te, c=c, slot=slot):
                b.w(te, b.tk[f"l1p_{tag}_w{g}_{c}"])
                if c == 0:
                    b.w(te, rdy)
                    b.bw(te, 0, 1, 2, 3)
                last = None
                for m in range(4):
                    last = te.matmul(b.pb[m][:, 0:512],
                                     b.WIN[:, slot * 512 + m * 128:
                                            slot * 512 + (m + 1) * 128],
                                     RHS[:, c * 512:(c + 1) * 512],
                                     start=(c == 0), stop=(c == 7))
                tok = b.tdrain(te) if c == 7 else b.sT.inc(last)
                b.win_use[slot] = tok
                if c == 7:
                    b.tk[f"l1p_{tag}_mm{g}"] = tok

        if kind == "copy":
            @b.on("scalar")
            def _(s):
                b.w(s, b.tk[f"l1p_{tag}_mm{g}"])
                ins = None
                for m in range(4):
                    cc = mw * 4 + m
                    ins = s.copy(DST[:, cc * 512:(cc + 1) * 512], b.pb[m][:, 0:512])
                tok = b.sS.inc(ins)
                b.bfree(tok, 0, 1, 2, 3)
        elif kind == "softplus":
            @b.on("scalar")
            def _(s):
                b.w(s, b.tk[f"l1p_{tag}_mm{g}"])
                ins = None
                for m in range(4):
                    cc = mw * 4 + m
                    t_ = b.TMP[:, (m % 2) * 512:(m % 2) * 512 + 512]
                    s.activation(t_, b.pb[m][:, 0:512], AF.Exp,
                                 bias=bass.AP(b.VEC1, cc * 4, [[32, 128], [4, 1]]),
                                 scale=1.0)
                    ins = s.activation(DST[:, cc * 512:(cc + 1) * 512], t_, AF.Ln,
                                       bias=1.0, scale=1.0)
                tok = b.sS.inc(ins)
                b.bfree(tok, 0, 1, 2, 3)
        else:
            for pair in range(2):
                @b.on("scalar")
                def _(s, pair=pair):
                    b.w(s, b.tk[f"l1p_{tag}_mm{g}"],
                        b.tk.get(f"l1p_{tag}_gv{g}_{pair - 1}"))
                    ins = None
                    for j in range(2):
                        m = pair * 2 + j
                        ins = s.activation(b.STAT[:, j * 512:(j + 1) * 512],
                                           b.pb[m][:, 0:512], AF.Silu)
                    b.tk[f"l1p_{tag}_sl{g}_{pair}"] = b.sS.inc(ins)

                @b.on("vector")
                def _(v, pair=pair):
                    b.w(v, b.tk[f"l1p_{tag}_sl{g}_{pair}"])
                    ins = None
                    for j in range(2):
                        cc = zoff + pair * 2 + j
                        ins = v.tensor_mul(DST[:, cc * 512:(cc + 1) * 512],
                                           DST[:, cc * 512:(cc + 1) * 512],
                                           b.STAT[:, j * 512:(j + 1) * 512])
                    tok = b.sV.inc(ins)
                    b.tk[f"l1p_{tag}_gv{g}_{pair}"] = tok
                    if pair == 1:
                        b.bfree(tok, 0, 1, 2, 3)

    def _moe_subgroup(self, g, sg, XN1G, XG, GBC, HE):
        b = self
        s0 = g * 512
        sgc = s0 + sg * 256

        for e in range(8):
            @b.on("tensor")
            def _(te, e=e):
                b.w(te, b.tk[f"gt_done{g}"], b.tk[f"xn1_{g}"])
                b.bw(te, 2)
                ins = te.matmul(b.pb[2][:, 0:256], b.selbc(e, 8),
                                b.GROWS[0:8, sgc:sgc + 256],
                                start=True, stop=True)
                b.tk["moe_gbc"] = b.tdrain(te)

            @b.on("vector")
            def _(v, e=e):
                b.w(v, b.tk["moe_gbc"])
                b.w(v, b.tk.get(f"moe_w1use{g}_{sg}_{e - 1}"))
                b.w(v, b.tk.get(f"moe_hev_v{g}_{sg}_{e - 1}_1"))
                v.tensor_copy(GBC[:, :], b.pb[2][:, 0:256])
                ins = None
                for c in range(8):
                    ins = v.tensor_mul(XG[:, c * 256:(c + 1) * 256],
                                       XN1G[:, c * 512 + sg * 256:
                                            c * 512 + sg * 256 + 256],
                                       b.pb[2][:, 0:256])
                tok = b.sV.inc(ins)
                b.tk[f"moe_xg{g}_{sg}_{e}"] = tok
                b.bfree(tok, 2)

            for mh in range(2):
                for c in range(8):
                    slot = c % 4

                    @b.on("sync")
                    def _(s, e=e, mh=mh, c=c, slot=slot):
                        b.w(s, b.win_use[slot], b.win2_use[slot])
                        for half in range(2):
                            src = bass.AP(b.h_W1.tensor,
                                          e * D * 2048 + c * 128 * 2048 +
                                          mh * 1024 + half * 512,
                                          [[2048, 128], [1, 512]])
                            dst = b.WIN if half == 0 else b.WIN2
                            sm = b.sQ[slot] if half == 0 else b.sR[slot]
                            sm.n += 16
                            s.dma_start(out=dst[:, slot * 512:slot * 512 + 512],
                                        in_=src).then_inc(sm.h, 16)
                        b.tk[f"moe_w1w{g}_{sg}_{e}_{mh}_{c}"] = b.sQ[slot].tok()
                        b.tk[f"moe_w1w2{g}_{sg}_{e}_{mh}_{c}"] = b.sR[slot].tok()

                    @b.on("tensor")
                    def _(te, e=e, mh=mh, c=c, slot=slot):
                        b.w(te, b.tk[f"moe_w1w{g}_{sg}_{e}_{mh}_{c}"],
                            b.tk[f"moe_w1w2{g}_{sg}_{e}_{mh}_{c}"],
                            b.tk[f"moe_xg{g}_{sg}_{e}"])
                        if c == 0:
                            b.bw(te, 0, 1, 2, 3)
                        last = None
                        for m in range(8):
                            buf = b.WIN if m < 4 else b.WIN2
                            mm_ = m % 4
                            last = te.matmul(
                                b.pb[m // 2][:, (m % 2) * 256:(m % 2) * 256 + 256],
                                buf[:, slot * 512 + mm_ * 128:
                                    slot * 512 + (mm_ + 1) * 128],
                                XG[:, c * 256:(c + 1) * 256],
                                start=(c == 0 and m % 2 == 0),
                                stop=(c == 7 and m % 2 == 1))
                        tok = b.tdrain(te) if c == 7 else b.sT.inc(last)
                        b.win_use[slot] = tok
                        b.win2_use[slot] = tok
                        b.tk[f"moe_w1mm{g}_{sg}_{e}_{mh}_{c}"] = tok
                        if mh == 1 and c == 7:
                            b.tk[f"moe_w1use{g}_{sg}_{e}"] = tok

                @b.on("vector")
                def _(v, e=e, mh=mh):
                    b.w(v, b.tk[f"moe_w1mm{g}_{sg}_{e}_{mh}_7"])
                    ins = None
                    for m in range(8):
                        hm = mh * 8 + m
                        ins = v.scalar_tensor_tensor(
                            HE[:, hm * 256:(hm + 1) * 256], GBC[:, :],
                            b.B1C[:, e * 16 + hm:e * 16 + hm + 1],
                            b.pb[m // 2][:, (m % 2) * 256:(m % 2) * 256 + 256],
                            ALU.mult, ALU.add)
                    tok = b.sV.inc(ins)
                    b.tk[f"moe_hev_v{g}_{sg}_{e}_{mh}"] = tok
                    b.bfree(tok, 0, 1, 2, 3)

                @b.on("scalar")
                def _(s, e=e, mh=mh):
                    b.w(s, b.tk[f"moe_hev_v{g}_{sg}_{e}_{mh}"])
                    ins = None
                    for m in range(8):
                        hm = mh * 8 + m
                        ins = s.activation(HE[:, hm * 256:(hm + 1) * 256],
                                           HE[:, hm * 256:(hm + 1) * 256], AF.Relu)
                    tok = b.sS.inc(ins)
                    b.tk[f"moe_hev{g}_{sg}_{e}_{mh}"] = tok

            for ch in range(16):
                @b.on("sync")
                def _(s, e=e, ch=ch):
                    b.w(s, b.tk.get(f"moe_w2use{g}_{sg}_{e}_{ch - 2}"))
                    if ch < 2:
                        b.w(s, b.win_use[0], b.win_use[1], b.win_use[2], b.win_use[3])
                    src = bass.AP(b.h_W2.tensor, e * 2048 * D + ch * 128 * D,
                                  [[D, 128], [1, D]])
                    sm = b.sQ[(ch % 2) * 2]
                    sm.n += 16
                    s.dma_start(out=b.WIN[:, (ch % 2) * 1024:(ch % 2) * 1024 + 1024],
                                in_=src).then_inc(sm.h, 16)
                    b.tk[f"moe_w2w{g}_{sg}_{e}_{ch}"] = sm.tok()

                @b.on("tensor")
                def _(te, e=e, ch=ch):
                    b.w(te, b.tk[f"moe_w2w{g}_{sg}_{e}_{ch}"],
                        b.tk[f"moe_hev{g}_{sg}_{e}_{ch // 8}"])
                    if e == 0 and ch == 0:
                        b.bw(te, 4, 5, 6, 7)
                    last = None
                    for m in range(8):
                        last = te.matmul(
                            b.pb[4 + m // 2][:, (m % 2) * 256:(m % 2) * 256 + 256],
                            b.WIN[:, (ch % 2) * 1024 + m * 128:
                                  (ch % 2) * 1024 + (m + 1) * 128],
                            HE[:, ch * 256:(ch + 1) * 256],
                            start=(e == 0 and ch == 0 and m % 2 == 0), stop=False)
                    tok = b.sT.inc(last)
                    b.tk[f"moe_w2use{g}_{sg}_{e}_{ch}"] = tok
                    if ch >= 14:
                        b.win_use[0] = tok
                        b.win_use[1] = tok
                        b.win_use[2] = tok
                        b.win_use[3] = tok

        @b.on("tensor")
        def _(te):
            b.w(te, b.tk[f"moe_w2use{g}_{sg}_7_15"])
            last = None
            for m in range(8):
                last = te.matmul(
                    b.pb[4 + m // 2][:, (m % 2) * 256:(m % 2) * 256 + 256],
                    b.B2R[0:8, m * 128:(m + 1) * 128],
                    b.GROWS[0:8, sgc:sgc + 256],
                    start=False, stop=(m % 2 == 1))
            b.tk[f"moe_b2{g}_{sg}"] = b.tdrain(te)

        @b.on("vector")
        def _(v):
            b.w(v, b.tk[f"moe_b2{g}_{sg}"])
            ins = None
            for m in range(8):
                dst = b.ACC[:, m * T + sgc:m * T + sgc + 256]
                ins = v.tensor_add(dst, dst,
                                   b.pb[4 + m // 2][:, (m % 2) * 256:(m % 2) * 256 + 256])
            tok = b.sV.inc(ins)
            b.tk[f"moe_acc{g}_{sg}"] = tok
            b.bfree(tok, 4, 5, 6, 7)
            if sg == 1:
                b.tk[f"xn1_cons{g}"] = tok

    def epilogue(self):
        b = self
        done = b.tk.get("final", b.tk["x"])
        for tt in range(16):
            @b.on("tensor")
            def _(te, tt=tt):
                b.w(te, done, b.tk["ones"])
                b.bw(te, 6, 7)
                last = None
                for c in range(8):
                    bank = b.pb[6 + (c // 4)]
                    last = te.transpose(bank[:, (c % 4) * 128:(c % 4 + 1) * 128],
                                        b.ACC[:, c * T + tt * 128: c * T + (tt + 1) * 128],
                                        b.IDENT[:, :])
                b.tk[f"eptp{tt}"] = b.tdrain(te)

            @b.on("vector")
            def _(v, tt=tt):
                b.w(v, b.tk[f"eptp{tt}"], b.tk.get(f"epst{tt - 1}"))
                v.tensor_copy(b.TMP[:, 0:512], b.pb[6][:, :512])
                ins = v.tensor_copy(b.TMP[:, 512:1024], b.pb[7][:, :512])
                tok = b.sV.inc(ins)
                b.tk[f"epcp{tt}"] = tok
                b.bfree(tok, 6, 7)

            @b.on("gpsimd")
            def _(g, tt=tt):
                b.w(g, b.tk[f"epcp{tt}"])
                g.dma_start(out=b.h_out[tt * 128:(tt + 1) * 128, :],
                            in_=b.TMP[:, :1024]).then_inc(b.sO.h, 16)
                b.sO.n += 16
                b.tk[f"epst{tt}"] = b.sO.tok()

        @b.on("gpsimd")
        def _(g):
            g.wait_ge(b.sO.h, b.sO.n)


def _prep_inputs(i):
    x = np.asarray(i["L0_attn_hidden_states"], dtype=np.float32)
    consts = np.eye(128, dtype=np.float32)
    inT = np.ascontiguousarray(np.asarray(i["L0_attn_in_proj_w"], np.float32).T)
    outT = np.ascontiguousarray(np.asarray(i["L0_attn_out_proj_w"], np.float32).T)
    xpT = np.concatenate([np.ascontiguousarray(np.asarray(i[f"L0_attn_x_proj_w_{d}"], np.float32).T)
                          for d in "fr"], axis=0)
    dtT = np.concatenate([np.ascontiguousarray(np.asarray(i[f"L0_attn_dt_proj_w_{d}"], np.float32).T)
                          for d in "fr"], axis=0)
    vecs0 = np.zeros((128, 32, 8), np.float32)
    A0 = np.zeros((128, 32 * NST), np.float32)
    for di, d in enumerate("fr"):
        cw = np.asarray(i[f"L0_attn_conv_w_{d}"], np.float32)
        cb = np.asarray(i[f"L0_attn_conv_b_{d}"], np.float32)
        db = np.asarray(i[f"L0_attn_dt_proj_b_{d}"], np.float32)
        Dd = np.asarray(i[f"L0_attn_D_{d}"], np.float32)
        Al = np.asarray(i[f"L0_attn_A_log_{d}"], np.float32)
        for c in range(16):
            slc = slice(c * 128, (c + 1) * 128)
            vecs0[:, di * 16 + c, 0:4] = cw[slc]
            vecs0[:, di * 16 + c, 4] = cb[slc]
            vecs0[:, di * 16 + c, 5] = db[slc]
            vecs0[:, di * 16 + c, 6] = Dd[slc]
            A0[:, (di * 16 + c) * NST:(di * 16 + c + 1) * NST] = Al[slc]
    vecs0 = vecs0.reshape(128, 256)
    wgT = np.ascontiguousarray(np.asarray(i["L0_ffn_w_gate"], np.float32).T)
    wuT = np.ascontiguousarray(np.asarray(i["L0_ffn_w_up"], np.float32).T)
    wdT = np.ascontiguousarray(np.asarray(i["L0_ffn_w_down"], np.float32).T)
    wxz = np.ascontiguousarray(np.asarray(i["L1_attn_W_xz"], np.float32))
    wdt = np.ascontiguousarray(np.asarray(i["L1_attn_W_dt"], np.float32))
    wB = np.ascontiguousarray(np.asarray(i["L1_attn_W_B"], np.float32))
    wC = np.ascontiguousarray(np.asarray(i["L1_attn_W_C"], np.float32))
    wo1 = np.ascontiguousarray(np.asarray(i["L1_attn_W_out"], np.float32))
    vecs1 = np.zeros((128, 8, 4), np.float32)
    A1 = np.zeros((128, 8 * NST), np.float32)
    bdt = np.asarray(i["L1_attn_b_dt"], np.float32)
    dsk = np.asarray(i["L1_attn_D_skip"], np.float32)
    vth = np.asarray(i["L1_attn_v_th"], np.float32)
    Al1 = np.asarray(i["L1_attn_A_log"], np.float32)
    for c in range(8):
        slc = slice(c * 128, (c + 1) * 128)
        vecs1[:, c, 0] = bdt[slc]
        vecs1[:, c, 1] = dsk[slc]
        vecs1[:, c, 2] = vth[slc]
        A1[:, c * NST:(c + 1) * NST] = Al1[slc]
    vecs1 = vecs1.reshape(128, 32)
    gate = np.ascontiguousarray(np.asarray(i["L1_ffn_gate"], np.float32))
    W1 = np.ascontiguousarray(np.asarray(i["L1_ffn_W1"], np.float32))
    W2 = np.ascontiguousarray(np.asarray(i["L1_ffn_W2"], np.float32))
    b1c = np.ascontiguousarray(
        np.asarray(i["L1_ffn_b1"], np.float32).reshape(8, 16, 128)
        .transpose(2, 0, 1).reshape(128, 128))
    b2r = np.ascontiguousarray(np.asarray(i["L1_ffn_b2"], np.float32))

    shared = dict(consts=consts, inT=inT, outT=outT, xpT=xpT, dtT=dtT,
                  vecs0=vecs0, A0=A0, wgT=wgT, wuT=wuT, wdT=wdT,
                  wxz=wxz, wdt=wdt, wB=wB, wC=wC, wo1=wo1, vecs1=vecs1, A1=A1,
                  gateW=gate, W1=W1, W2=W2, b1c=b1c, b2r=b2r)
    in_maps = []
    for core in range(8):
        m = dict(shared)
        m["x_fm"] = np.ascontiguousarray(x[core].T)
        in_maps.append(m)
    return in_maps


def kernel(**inputs):
    in_maps = _prep_inputs(inputs)
    nc = B().build()
    res = run_bass_kernel_spmd(nc, in_maps, core_ids=list(range(8)))
    return np.stack([np.asarray(r["out_tok"]) for r in res.results])


# version 13
# speedup vs baseline: 1.0856x; 1.0396x over previous
import numpy as np
import ml_dtypes
from contextlib import ExitStack
import concourse.bass as bass
import concourse.mybir as mybir
from concourse.bass_utils import run_bass_kernel_spmd

F32 = mybir.dt.float32
BF16 = mybir.dt.bfloat16
I32 = mybir.dt.int32
AF = mybir.ActivationFunctionType
ALU = mybir.AluOpType
AX = mybir.AxisListType

T = 2048
D = 1024
DI = 2048
NST = 16
TC = 256
NTC = T // TC
STAGE = 9
DEBUG = False
R127 = float(np.float32(1.0 / 127.0))


class Sem:
    __slots__ = ("h", "n")

    def __init__(self, ctx, nc, name):
        self.h = ctx.enter_context(nc.semaphore(name))
        self.n = 0

    def inc(self, instr, k=1):
        instr.then_inc(self.h, k)
        self.n += k
        return (self, self.n)

    def tok(self):
        return (self, self.n)


class B:
    def __init__(self):
        self.nc = bass.Bass("TRN2", target_bir_lowering=False,
                            detect_race_conditions=False,
                            dynamic_dma_scratch_size=4096)
        self.ctx = ExitStack()
        self.waited = {e: {} for e in ("tensor", "vector", "scalar", "sync", "gpsimd")}
        self.cur = None
        self.bkfree = [None] * 8
        self.win_use = [None] * 4
        self.win2_use = [None] * 4
        self.tk = {}

    def on(self, engname):
        def reg(f):
            self.cur = engname
            getattr(self.block, engname)(f)
            self.cur = None
        return reg

    def w(self, eng, *tokens):
        d = self.waited[self.cur]
        for tk_ in tokens:
            if tk_ is None:
                continue
            sem, v = tk_
            if v > 0 and d.get(sem, 0) < v:
                eng.wait_ge(sem.h, v)
                d[sem] = v

    def tdrain(self, te):
        return self.sT.inc(te.drain())

    def ch(self, eng, instr):
        s = self.sVC if self.cur == "vector" else self.sSC
        self.w(eng, s.inc(instr))

    def bw(self, eng, *banks):
        for i in banks:
            self.w(eng, self.bkfree[i])

    def bfree(self, tok, *banks):
        for i in banks:
            self.bkfree[i] = tok

    def rln(self, r, c0, cnt, rev=False):
        t = self.RLN0 if r < 2 else self.RLN1
        p = (r % 2) * 32
        if not rev:
            return bass.AP(t, p * T + c0, [[T, 1], [1, cnt]])
        return bass.AP(t, p * T + c0 + cnt - 1, [[T, 1], [-1, cnt]])

    def selbc(self, p, K):
        return bass.AP(self.IDENT, p, [[128, K], [0, 128]])

    def rln_rhs(self, layer, c0, cnt, rev=False):
        t = self.RLN0 if layer == 0 else self.RLN1
        if not rev:
            return bass.AP(t, c0, [[T, 64], [1, cnt]])
        return bass.AP(t, c0 + cnt - 1, [[T, 64], [-1, cnt]])

    def build(self):
        nc, ctx = self.nc, self.ctx
        dram = lambda name, shape, dt=F32, kind="ExternalInput": \
            nc.dram_tensor(name, shape, dt, kind=kind).ap()
        sb = lambda name, shape, dt=F32: ctx.enter_context(nc.sbuf_tensor(name, shape, dt))

        self.h_x = dram("x_fm", [D, T])
        self.h_out = dram("out_tok", [T, D], kind="ExternalOutput")
        self.h_id = dram("consts", [128, 128])
        self.h_inT = dram("inT", [D, 2 * DI])
        self.h_outT = dram("outT", [DI, D])
        self.h_xpT = dram("xpT", [2 * DI, 96])
        self.h_dtT = dram("dtT", [2 * 64, DI])
        self.h_vecs0 = dram("vecs0", [128, 32 * 8])
        self.h_A0 = dram("A0", [128, 32 * NST])
        self.h_wgT = dram("wgT", [D, 4096])
        self.h_wuT = dram("wuT", [D, 4096])
        self.h_wdT = dram("wdT", [4096, D])
        self.h_wxz = dram("wxz", [D, 2 * D])
        self.h_wdt = dram("wdt", [D, D])
        self.h_wB = dram("wB", [D, NST])
        self.h_wC = dram("wC", [D, NST])
        self.h_wo1 = dram("wo1", [D, D])
        self.h_vecs1 = dram("vecs1", [128, 8 * 4])
        self.h_A1 = dram("A1", [128, 8 * NST])
        self.h_gate = dram("gateW", [D, 8])
        self.h_W1 = dram("W1", [8, D, 2048])
        self.h_W2 = dram("W2", [8, 2048, D])
        self.h_b1 = dram("b1c", [128, 128])
        self.h_b2 = dram("b2r", [8, D])
        if DEBUG:
            self.h_dbg = {k: dram(f"dbg_{k}", [128, 8 * T], kind="ExternalOutput")
                          for k in ("ffn", "pa", "pb")}
            self.h_dbgr = dram("dbg_rows", [64, T], kind="ExternalOutput")
            self.h_dbgxq = nc.dram_tensor("dbg_xqt", [128, 1024], BF16,
                                          kind="ExternalOutput").ap()
            self.h_dbggu = dram("dbg_gu", [128, 4096], kind="ExternalOutput")
            self.h_dbgxq2 = dram("dbg_xq2", [128, 4096], kind="ExternalOutput")
            self.h_dbgfm = nc.dram_tensor("dbg_xq2fm", [128, 4096], BF16,
                                          kind="ExternalOutput").ap()
            self.h_dbgrow = dram("dbg_row1", [1, 512], kind="ExternalOutput")
            self.h_dbgqc = dram("dbg_qc", [128, 8], kind="ExternalOutput")
            self.h_dbgxn = dram("dbg_xn", [128, 1024], kind="ExternalOutput")
            self.h_dbgst = dram("dbg_stat", [128, 16], kind="ExternalOutput")
        self.h_wqg = nc.dram_tensor("wqg_s", [D, 4096], BF16, kind="Internal").ap()
        self.h_wqu = nc.dram_tensor("wqu_s", [D, 4096], BF16, kind="Internal").ap()
        self.h_wqd = nc.dram_tensor("wqd_s", [4096, D], BF16, kind="Internal").ap()

        self.ACC = sb("ACC", [128, 8 * T])
        self.IDENT = sb("IDENT", [128, 128])
        self.IDENTB = sb("IDENTB", [128, 128], BF16)
        self.ONES = sb("ONESC", [128, 128])
        self.WIN = sb("WIN", [128, 4 * 512])
        self.WIN2 = sb("WIN2", [128, 4 * 512])
        self.TMP = sb("TMP", [128, 1024])
        self.STAT = sb("STAT", [128, 1024])
        self.RLN1 = sb("RLN1", [64, T])
        self.ROW1 = sb("ROW1", [1, 512])
        self.SCAL = sb("SCAL", [128, 32])
        self.VEC1 = sb("VEC1", [128, 8 * 4])
        self.A1SB = sb("A1SB", [128, 8 * NST])
        self.GATEW = sb("GATEW", [128, 8 * 8])
        self.WBSB = sb("WBSB", [128, 8 * NST])
        self.WCSB = sb("WCSB", [128, 8 * NST])
        self.B1C = sb("B1C", [128, 128])
        self.B2R = sb("B2R", [8, D])
        self.GROWS = sb("GROWS", [8, T])
        self.GTOK = sb("GTOK", [128, 8])

        self.pb = [ctx.enter_context(nc.psum_tensor(f"pb{i}", [128, 512], F32))
                   for i in range(8)]
        for name in ("sW", "sG", "sT", "sV", "sS", "sO", "sVC", "sSC",
                     "sB1", "sB2", "sD0", "sD1", "sX", "sWQ", "sXI"):
            setattr(self, name, Sem(ctx, nc, name))
        self.sQ = [Sem(ctx, nc, f"sQ{i}") for i in range(4)]
        self.sR = [Sem(ctx, nc, f"sR{i}") for i in range(4)]

        self.block = ctx.enter_context(nc.Block())

        with ExitStack() as l0stk:
            self.RLN0 = l0stk.enter_context(nc.sbuf_tensor("RLN0", [64, T], F32))
            self.VEC0 = l0stk.enter_context(nc.sbuf_tensor("VEC0", [128, 256], F32))
            self.A0SB = l0stk.enter_context(nc.sbuf_tensor("A0SB", [128, 512], F32))
            self.prologue()
            self.ln_stats(0)
            self.dbg_rows_dump(self.tk["ln0"])
            if STAGE >= 2:
                self.ffn0()
            else:
                self.tk["ffn"] = self.sV.tok()
            self.dbg_dump("ffn", self.tk["ffn"])
            if STAGE >= 3:
                self.attn_pass(0)
                self.dbg_dump("pa", self.sV.tok())
            if STAGE >= 4:
                self.attn_pass(1)
                self.dbg_dump("pb", self.sV.tok())
            self.tk["x1"] = self.sV.tok()
        if STAGE >= 5:
            self.ln_stats(1)
            self.layer1()
        self.tk["final"] = self.sV.tok()
        self.epilogue()

        self.ctx.close()
        return self.nc

    def dbg_dump(self, key, tok):
        if not DEBUG:
            return
        b = self

        @b.on("gpsimd")
        def _(g):
            b.w(g, tok)
            g.dma_start(out=b.h_dbg[key][:, :], in_=b.ACC[:, :]).then_inc(b.sO.h, 16)
            b.sO.n += 16
            b.tk[f"dbgd_{key}"] = b.sO.tok()

        @b.on("vector")
        def _(v):
            b.w(v, b.tk[f"dbgd_{key}"])

    def dbg_rows_dump(self, tok):
        if not DEBUG:
            return
        b = self

        @b.on("gpsimd")
        def _(g):
            b.w(g, tok)
            g.dma_start(out=b.h_dbgr[:, :], in_=b.RLN0[0:64, :]).then_inc(b.sO.h, 16)
            b.sO.n += 16

    def prologue(self):
        b = self

        @b.on("gpsimd")
        def _(g):
            g.dma_start(out=b.IDENT[:, :], in_=b.h_id[:, :]).then_inc(b.sG.h, 16)
            b.sG.n += 16
            for c in range(8):
                g.dma_start(out=b.ACC[:, c * T:(c + 1) * T],
                            in_=b.h_x[c * 128:(c + 1) * 128, :]).then_inc(b.sG.h, 16)
                b.sG.n += 16
        b.tk["x"] = b.sG.tok()

        @b.on("sync")
        def _(s):
            for dst, src in ((b.VEC0, b.h_vecs0), (b.A0SB, b.h_A0),
                             (b.VEC1, b.h_vecs1), (b.A1SB, b.h_A1),
                             (b.B1C, b.h_b1), (b.B2R, b.h_b2)):
                s.dma_start(out=dst[0:dst.shape[0], :], in_=src[:, :]).then_inc(b.sW.h, 16)
                b.sW.n += 16
            for dst, src, wdt in ((b.WBSB, b.h_wB, NST), (b.WCSB, b.h_wC, NST),
                                  (b.GATEW, b.h_gate, 8)):
                s.dma_start(out=dst[:, :8 * wdt], in_=bass.AP(
                    src.tensor, 0, [[wdt, 128], [128 * wdt, 8], [1, wdt]])
                ).then_inc(b.sW.h, 16)
                b.sW.n += 16
        b.tk["smallw"] = b.sW.tok()

        @b.on("scalar")
        def _(s):
            b.w(s, b.tk["smallw"])
            s.activation(b.A0SB[:, :], b.A0SB[:, :], AF.Exp)
            ins = s.activation(b.A1SB[:, :], b.A1SB[:, :], AF.Exp)
            b.tk["aexp"] = b.sS.inc(ins)

        @b.on("vector")
        def _(v):
            b.w(v, b.tk["smallw"], b.tk["x"], b.tk["aexp"])
            v.tensor_scalar_mul(b.A0SB[:, :], b.A0SB[:, :], -1.0)
            ins = v.tensor_scalar_mul(b.A1SB[:, :], b.A1SB[:, :], -1.0)
            b.tk["aprep"] = b.sV.inc(ins)
            v.memset(b.RLN0[:, :], 0.0)
            v.memset(b.RLN1[:, :], 0.0)
            v.memset(b.ONES[:, :], 1.0)
            v.tensor_copy(b.IDENTB[:, :], b.IDENT[:, :])
            v.memset(b.SCAL[:, 17:18], 1e-5)
            v.memset(b.SCAL[:, 18:19], 10.0)
            ins = v.tensor_scalar(b.SCAL[:, 8:16], bass.AP(b.VEC1, 2, [[32, 128], [4, 8]]),
                                  0.1, -10.0, ALU.max, ALU.mult)
            b.tk["ones"] = b.sV.inc(ins)

        @b.on("tensor")
        def _(te):
            b.w(te, b.tk["ones"])
            b.bw(te, 2)
            last = None
            for c in range(8):
                last = te.matmul(b.pb[2][0:8, 0:1], b.GATEW[:, c * 8:(c + 1) * 8],
                                 b.ONES[:, 0:1], start=(c == 0), stop=(c == 7))
            b.tk["wsume_ps"] = b.tdrain(te)

        @b.on("scalar")
        def _(s):
            b.w(s, b.tk["wsume_ps"])
            ins = s.copy(b.SCAL[0:8, 16:17], b.pb[2][0:8, 0:1])
            tok = b.sS.inc(ins)
            b.tk["wsume"] = tok
            b.bfree(tok, 2)

    def ln_stats(self, layer):
        b = self
        mr_row = 0 if layer == 0 else 2
        sr_row = 1 if layer == 0 else 3
        ready = [b.tk["x"], b.tk["ones"]] if layer == 0 else [b.tk["x1"], b.tk["ones"]]

        for g in range(4):
            s0 = g * 512
            sq = [None] * 8
            tebase = b.sT.n

            @b.on("scalar")
            def _(s, s0=s0, sq=sq, g=g, layer=layer, tebase=tebase):
                b.w(s, *ready)
                for c in range(8):
                    if c >= 2:
                        b.w(s, (b.sT, tebase + c - 1))
                    ins = s.activation(b.STAT[:, (c % 2) * 512:(c % 2) * 512 + 512],
                                       b.ACC[:, c * T + s0:c * T + s0 + 512], AF.Square)
                    sq[c] = b.sS.inc(ins)

            @b.on("tensor")
            def _(te, s0=s0, sq=sq, g=g, layer=layer):
                b.w(te, *ready)
                b.bw(te, 0, 1)
                for c in range(8):
                    te.matmul(b.pb[0][0:1, :], b.ONES[:, 0:1],
                              b.ACC[:, c * T + s0:c * T + s0 + 512],
                              start=(c == 0), stop=(c == 7))
                for c in range(8):
                    b.w(te, sq[c])
                    ins = te.matmul(b.pb[1][0:1, :], b.ONES[:, 0:1],
                                    b.STAT[:, (c % 2) * 512:(c % 2) * 512 + 512],
                                    start=(c == 0), stop=(c == 7))
                    b.sT.inc(ins)
                b.tk[f"lnmm{layer}_{g}"] = b.tdrain(te)
            assert b.sT.n == tebase + 9, (b.sT.n, tebase)

            @b.on("vector")
            def _(v, s0=s0, g=g, layer=layer):
                b.w(v, b.tk[f"lnmm{layer}_{g}"])
                mr = b.rln(mr_row, s0, 512)
                v.tensor_scalar(mr, b.pb[0][0:1, :512], float(np.float32(1.0 / D)), None, ALU.mult)
                v.tensor_mul(b.ROW1[0:1, :], mr, mr)
                v.tensor_scalar(b.ROW1[0:1, :], b.ROW1[0:1, :], -1.0, None, ALU.mult)
                ins = v.scalar_tensor_tensor(b.ROW1[0:1, :], b.pb[1][0:1, :512],
                                             float(np.float32(1.0 / D)), b.ROW1[0:1, :],
                                             ALU.mult, ALU.add)
                tok = b.sV.inc(ins)
                b.tk["var"] = tok
                b.bfree(tok, 0, 1)

            @b.on("scalar")
            def _(s, s0=s0, g=g, layer=layer):
                b.w(s, b.tk["var"])
                ins = s.activation(b.ROW1[0:1, :], b.ROW1[0:1, :],
                                   AF.Sqrt, bias=b.SCAL[0:1, 17:18], scale=1.0)
                b.tk["lnsqrt"] = b.sS.inc(ins)

            @b.on("vector")
            def _(v, s0=s0, g=g, layer=layer):
                b.w(v, b.tk["lnsqrt"])
                ins = v.reciprocal(b.rln(sr_row, s0, 512), b.ROW1[0:1, :])
                b.tk[f"ln{layer}_{g}"] = b.sV.inc(ins)
        b.tk[f"ln{layer}"] = b.sV.tok()

    def ffn0(self):
        b = self
        nc = self.nc
        with ExitStack() as fstk:
            fsb = lambda name, shape, dt=F32: fstk.enter_context(
                nc.sbuf_tensor(name, shape, dt))
            TMPI = fstk.enter_context(nc.sbuf_tensor("TMPI", [128, 1024], I32))
            FSCR = fsb("FSCR", [128, 4096])
            XQT = fsb("XQT", [128, 8 * 128], BF16)
            XQ2 = fsb("XQ2", [128, 4096])
            XQ2FM = fsb("XQ2FM", [128, 32 * 128], BF16)
            WQB = fsb("WQB", [128, 8 * 512], BF16)
            WQB2 = fsb("WQB2", [128, 8 * 512], BF16)
            WQD = fsb("WQD", [128, 2 * 1024], BF16)
            WQO = fsb("WQO", [128, 2 * 1024], BF16)
            QC = fsb("QC", [128, 8])

            self._ffn_wprep(TMPI, WQO)
            if DEBUG:
                self.dbg_dump("pa", b.tk["wq_scratch"])
            for tt in range(16):
                self._ffn_tile(tt, TMPI, FSCR, XQT, XQ2, XQ2FM, WQB, WQB2, WQD, QC)
            b.tk["ffn"] = b.sV.tok()

    def _ffn_wprep(self, TMPI, WQO):
        b = self
        for wi, (h_w, h_q) in enumerate(((b.h_wgT, b.h_wqg), (b.h_wuT, b.h_wqu),
                                         (b.h_wdT, b.h_wqd))):
            nrow = h_w.shape[0] // 128
            npc = h_w.shape[1] // 1024
            npieces = nrow * npc

            sbase = b.sS.n
            for k in range(npieces):
                c, p = divmod(k, npc)

                @b.on("sync")
                def _(s, k=k, c=c, p=p):
                    if k >= 2:
                        b.w(s, (b.sS, sbase + k - 1))
                    else:
                        b.w(s, b.win2_use[0], b.win2_use[1], b.win2_use[2],
                            b.win2_use[3])
                    sm = b.sR[(k % 2) * 2]
                    sm.n += 16
                    s.dma_start(out=b.WIN2[:, (k % 2) * 1024:(k % 2) * 1024 + 1024],
                                in_=h_w[c * 128:(c + 1) * 128, p * 1024:(p + 1) * 1024]
                                ).then_inc(sm.h, 16)
                    b.tk["wp_dma"] = sm.tok()

                @b.on("scalar")
                def _(s, k=k):
                    b.w(s, b.tk["wp_dma"])
                    if k == 0:
                        b.w(s, b.tk.get("wp_tmp_free"))
                        b.w(s, b.tk["ln0"])
                    ins = s.activation(b.STAT[:, :1024],
                                       b.WIN2[:, (k % 2) * 1024:(k % 2) * 1024 + 1024],
                                       AF.Abs, accum_out=b.TMP[:, k:k + 1])
                    b.sS.inc(ins)
            assert b.sS.n == sbase + npieces
            b.tk["wp_abs"] = b.sS.tok()
            for j in range(4):
                b.win2_use[j] = b.tk["wp_abs"]

            @b.on("vector")
            def _(v, npieces=npieces):
                b.w(v, b.tk["wp_abs"])
                ins = v.reduce_sum(b.SCAL[:, 20:21], b.TMP[:, 0:npieces], AX.X)
                tok = b.sV.inc(ins)
                b.tk["wp_red"] = tok
                b.tk["wp_tmp_free"] = tok

            @b.on("tensor")
            def _(te):
                b.w(te, b.tk["wp_red"])
                b.bw(te, 4)
                ins = te.matmul(b.pb[4][0:1, 0:1], b.ONES[:, 0:1], b.SCAL[:, 20:21],
                                start=True, stop=True)
                b.tk["wp_sum"] = b.tdrain(te)

            @b.on("vector")
            def _(v, wi=wi, h_w=h_w):
                b.w(v, b.tk["wp_sum"])
                n_el = float(h_w.shape[0] * h_w.shape[1])
                r = b.ROW1[0:1, 0:1]
                b.ch(v, v.tensor_scalar(r, b.pb[4][0:1, 0:1],
                                        float(np.float32(1.0 / n_el)), None, ALU.mult))
                b.ch(v, v.tensor_scalar(r, r, 1e-5, None, ALU.max))
                ins = v.reciprocal(b.ROW1[0:1, 1:2], r)
                tok = b.sV.inc(ins)
                b.tk["wp_row"] = tok
                b.bfree(tok, 4)

            @b.on("tensor")
            def _(te, wi=wi):
                b.w(te, b.tk["wp_row"])
                b.bw(te, 4)
                ins = te.matmul(b.pb[4][:, 8:9], b.ONES[0:1, :], b.ROW1[0:1, 1:2],
                                start=True, stop=True)
                b.tk["wp_bc"] = b.tdrain(te)

            @b.on("scalar")
            def _(s, wi=wi):
                b.w(s, b.tk["wp_bc"])
                ins = s.copy(b.SCAL[:, wi:wi + 1], b.pb[4][:, 8:9])
                tok = b.sS.inc(ins)
                b.tk["sw"] = tok
                b.bfree(tok, 4)

            vbase = b.sV.n
            gbase = b.sG.n
            for k in range(npieces):
                c, p = divmod(k, npc)

                @b.on("sync")
                def _(s, k=k, c=c, p=p):
                    if k >= 2:
                        b.w(s, (b.sV, vbase + k - 1))
                    else:
                        b.w(s, b.tk["wp_abs"])
                    sm = b.sR[(k % 2) * 2]
                    sm.n += 16
                    s.dma_start(out=b.WIN2[:, (k % 2) * 1024:(k % 2) * 1024 + 1024],
                                in_=h_w[c * 128:(c + 1) * 128, p * 1024:(p + 1) * 1024]
                                ).then_inc(sm.h, 16)
                    b.tk["wq_dma"] = sm.tok()

                @b.on("vector")
                def _(v, k=k, wi=wi):
                    b.w(v, b.tk["wq_dma"], b.tk["sw"])
                    b.w(v, b.tk.get(f"wq_half{k % 2}"))
                    wsrc = b.WIN2[:, (k % 2) * 1024:(k % 2) * 1024 + 1024]
                    t_ = b.TMP[:, :1024]
                    u_ = b.STAT[:, :1024]
                    v.tensor_scalar(t_, wsrc, b.SCAL[:, wi:wi + 1], None, ALU.mult)
                    v.tensor_copy(TMPI[:, :1024], t_)
                    v.tensor_sub(u_, t_, TMPI[:, :1024])
                    v.scalar_tensor_tensor(t_, u_, 0.5, TMPI[:, :1024],
                                           ALU.is_gt, ALU.add)
                    v.scalar_tensor_tensor(u_, u_, -0.5, t_, ALU.is_ge, ALU.add)
                    v.tensor_scalar(t_, u_, -1.0, None, ALU.add)
                    ins = v.tensor_scalar(WQO[:, (k % 2) * 1024:(k % 2) * 1024 + 1024],
                                          t_, -1.0, 1.0, ALU.max, ALU.min)
                    b.sV.inc(ins)
                    b.tk["wq_q"] = b.sV.tok()

                @b.on("scalar")
                def _(g, k=k, c=c, p=p, h_q=h_q):
                    b.w(g, b.tk["wq_q"])
                    sm = b.sWQ if k % 2 == 0 else b.sXI
                    sm.n += 16
                    g.dma_start(out=h_q[c * 128:(c + 1) * 128, p * 1024:(p + 1) * 1024],
                                in_=WQO[:, (k % 2) * 1024:(k % 2) * 1024 + 1024]
                                ).then_inc(sm.h, 16)
                    b.tk[f"wq_half{k % 2}"] = sm.tok()
            assert b.sV.n == vbase + npieces
            for j in range(4):
                b.win2_use[j] = b.sV.tok()
        b.tk["wq_scratch"] = b.sWQ.tok()
        b.tk["wq_scratch2"] = b.sXI.tok()

        @b.on("vector")
        def _(v):
            b.w(v, b.tk["sw"])
            v.reciprocal(b.SCAL[:, 21:22], b.SCAL[:, 0:1])
            v.reciprocal(b.SCAL[:, 22:23], b.SCAL[:, 1:2])
            ins = v.reciprocal(b.SCAL[:, 23:24], b.SCAL[:, 2:3])
            b.tk["swinv"] = b.sV.inc(ins)

    def _ffn_tile(self, tt, TMPI, FSCR, XQT, XQ2, XQ2FM, WQB, WQB2, WQD, QC):
        b = self
        t0 = tt * 128
        g = tt // 4

        @b.on("tensor")
        def _(te):
            b.w(te, b.tk[f"ln0_{g}"])
            b.bw(te, 2)
            te.matmul(b.pb[2][:, 0:128], b.selbc(0, 64), b.rln_rhs(0, t0, 128),
                      start=True, stop=True)
            ins = te.matmul(b.pb[2][:, 128:256], b.selbc(32, 64), b.rln_rhs(0, t0, 128),
                            start=True, stop=True)
            b.tk["fq_bc"] = b.tdrain(te)

        @b.on("vector")
        def _(v):
            b.w(v, b.tk["fq_bc"])
            am = b.STAT[:, 0:128]
            v.memset(am, 0.0)
            ins = None
            for c in range(8):
                xn = FSCR[:, c * 128:(c + 1) * 128]
                v.tensor_sub(xn, b.ACC[:, c * T + t0:c * T + t0 + 128], b.pb[2][:, 0:128])
                v.tensor_mul(xn, xn, b.pb[2][:, 128:256])
                v.scalar_tensor_tensor(b.TMP[:, 0:128], xn, -1.0, am, ALU.mult, ALU.max)
                ins = v.scalar_tensor_tensor(am, xn, 1.0, b.TMP[:, 0:128],
                                             ALU.mult, ALU.max)
            tok = b.sV.inc(ins)
            b.tk["fq_am"] = tok
            b.bfree(tok, 2)

        @b.on("tensor")
        def _(te):
            b.w(te, b.tk["fq_am"])
            b.bw(te, 3)
            ins = te.transpose(b.pb[3][:, 0:128], b.STAT[:, 0:128], b.IDENT[:, :])
            b.tk["fq_tp"] = b.tdrain(te)

        @b.on("vector")
        def _(v):
            b.w(v, b.tk["fq_tp"], b.tk["swinv"])
            b.ch(v, v.tensor_reduce(b.STAT[:, 128:129], b.pb[3][:, 0:128],
                                    AX.X, ALU.max))
            b.ch(v, v.tensor_scalar(b.STAT[:, 129:130], b.STAT[:, 128:129],
                                    1e-5, None, ALU.max))
            b.ch(v, v.reciprocal(b.STAT[:, 130:131], b.STAT[:, 129:130]))
            b.ch(v, v.tensor_scalar(b.STAT[:, 131:132], b.STAT[:, 130:131],
                                    127.0, None, ALU.mult))
            b.ch(v, v.reciprocal(b.STAT[:, 132:133], b.STAT[:, 131:132]))
            v.tensor_scalar(QC[:, 0:1], b.STAT[:, 132:133], b.SCAL[:, 21:22], None, ALU.mult)
            ins = v.tensor_scalar(QC[:, 1:2], b.STAT[:, 132:133], b.SCAL[:, 22:23],
                                  None, ALU.mult)
            tok = b.sV.inc(ins)
            b.tk["fq_sx"] = tok
            b.bfree(tok, 3)

        @b.on("tensor")
        def _(te):
            b.w(te, b.tk["fq_sx"])
            b.bw(te, 3)
            ins = te.transpose(b.pb[3][0:1, 128:256], b.STAT[:, 131:132], b.IDENT[:, :])
            b.tk["fq_sxr_ps"] = b.tdrain(te)

        @b.on("scalar")
        def _(s):
            b.w(s, b.tk["fq_sxr_ps"])
            ins = s.copy(b.ROW1[0:1, 0:128], b.pb[3][0:1, 128:256])
            tok = b.sS.inc(ins)
            b.tk["fq_sxr"] = tok
            b.bfree(tok, 3)

        @b.on("tensor")
        def _(te):
            b.w(te, b.tk["fq_sxr"])
            b.bw(te, 3)
            ins = te.matmul(b.pb[3][:, 128:256], b.ONES[0:1, :], b.ROW1[0:1, 0:128],
                            start=True, stop=True)
            b.tk["fq_sxbc"] = b.tdrain(te)

        @b.on("vector")
        def _(v):
            b.w(v, b.tk["fq_sxbc"])
            b.w(v, b.tk.get("fg_mm7"))
            ins = None
            for c in range(8):
                xn = FSCR[:, c * 128:(c + 1) * 128]
                t_ = b.TMP[:, 0:128]
                u_ = b.STAT[:, 256:384]
                v.tensor_mul(t_, xn, b.pb[3][:, 128:256])
                v.tensor_copy(TMPI[:, 0:128], t_)
                v.tensor_sub(u_, t_, TMPI[:, 0:128])
                v.scalar_tensor_tensor(t_, u_, 0.5, TMPI[:, 0:128], ALU.is_gt, ALU.add)
                v.scalar_tensor_tensor(u_, u_, -0.5, t_, ALU.is_ge, ALU.add)
                ins = v.tensor_scalar(XQT[:, c * 128:(c + 1) * 128], u_,
                                      -1.0, None, ALU.add)
            tok = b.sV.inc(ins)
            b.tk["fq_xq"] = tok
            b.bfree(tok, 3)

        for hg in range(8):
            @b.on("sync")
            def _(s, hg=hg):
                b.w(s, b.tk["wq_scratch"], b.tk["wq_scratch2"])
                b.w(s, b.tk[f"fg_mm{hg - 1}"] if hg > 0 else b.tk.get("fg_mm7"))
                for c in range(8):
                    b.sB1.n += 16
                    s.dma_start(out=WQB[:, c * 512:(c + 1) * 512],
                                in_=self.h_wqg[c * 128:(c + 1) * 128,
                                               hg * 512:(hg + 1) * 512]).then_inc(b.sB1.h, 16)
                for c in range(8):
                    b.sB2.n += 16
                    s.dma_start(out=WQB2[:, c * 512:(c + 1) * 512],
                                in_=self.h_wqu[c * 128:(c + 1) * 128,
                                               hg * 512:(hg + 1) * 512]).then_inc(b.sB2.h, 16)
                b.tk["fg_wdma"] = b.sB1.tok()
                b.tk["fg_wdma2"] = b.sB2.tok()

            @b.on("tensor")
            def _(te, hg=hg):
                b.w(te, b.tk["fg_wdma"], b.tk["fg_wdma2"], b.tk["fq_xq"])
                b.bw(te, 0, 1)
                for c in range(8):
                    te.matmul(b.pb[0][:, :], XQT[:, c * 128:(c + 1) * 128],
                              WQB[:, c * 512:(c + 1) * 512],
                              start=(c == 0), stop=(c == 7))
                last = None
                for c in range(8):
                    last = te.matmul(b.pb[1][:, :], XQT[:, c * 128:(c + 1) * 128],
                                     WQB2[:, c * 512:(c + 1) * 512],
                                     start=(c == 0), stop=(c == 7))
                tok = b.tdrain(te)
                b.tk[f"fg_mm{hg}"] = tok
                if hg == 7:
                    b.tk["fg_mm7"] = tok

            @b.on("vector")
            def _(v, hg=hg):
                b.w(v, b.tk[f"fg_mm{hg}"])
                ins = v.tensor_scalar(b.TMP[:, 0:512], b.pb[0][:, :], QC[:, 0:1],
                                      None, ALU.mult)
                b.tk["fg_gd"] = b.sV.inc(ins)

            @b.on("scalar")
            def _(s, hg=hg):
                b.w(s, b.tk["fg_gd"])
                ins = s.activation(b.STAT[:, 512:1024], b.TMP[:, 0:512], AF.Silu)
                b.tk["fg_silu"] = b.sS.inc(ins)

            @b.on("vector")
            def _(v, hg=hg):
                b.w(v, b.tk["fg_silu"])
                gu = FSCR[:, hg * 512:(hg + 1) * 512]
                v.tensor_scalar(b.TMP[:, 512:1024], b.pb[1][:, :], QC[:, 1:2],
                                None, ALU.mult)
                v.tensor_mul(gu, b.STAT[:, 512:1024], b.TMP[:, 512:1024])
                ins = v.tensor_reduce(b.STAT[:, 133 + hg:134 + hg], gu, AX.X,
                                      ALU.max, apply_absolute_value=True)
                tok = b.sV.inc(ins)
                b.tk[f"fg_ev{hg}"] = tok
                b.bfree(tok, 0, 1)

        @b.on("vector")
        def _(v):
            b.w(v, b.tk["fg_ev7"])
            b.ch(v, v.tensor_reduce(b.STAT[:, 141:142], b.STAT[:, 133:141],
                                    AX.X, ALU.max))
            b.ch(v, v.tensor_scalar(b.STAT[:, 142:143], b.STAT[:, 141:142],
                                    1e-5, None, ALU.max))
            b.ch(v, v.reciprocal(b.STAT[:, 143:144], b.STAT[:, 142:143]))
            b.ch(v, v.tensor_scalar(b.STAT[:, 144:145], b.STAT[:, 143:144],
                                    127.0, None, ALU.mult))
            b.ch(v, v.reciprocal(b.STAT[:, 145:146], b.STAT[:, 144:145]))
            t_ = b.TMP[:, 0:1024]
            ins = None
            for q in range(4):
                sl = slice(q * 1024, (q + 1) * 1024)
                v.tensor_scalar(t_, FSCR[:, sl], b.STAT[:, 144:145], None, ALU.mult)
                v.tensor_copy(TMPI[:, :1024], t_)
                u_ = FSCR[:, sl]
                v.tensor_sub(u_, t_, TMPI[:, :1024])
                v.scalar_tensor_tensor(t_, u_, 0.5, TMPI[:, :1024], ALU.is_gt, ALU.add)
                v.scalar_tensor_tensor(u_, u_, -0.5, t_, ALU.is_ge, ALU.add)
                ins = v.tensor_scalar(XQ2[:, sl], u_, -1.0, None, ALU.add)
            b.tk["f2_xq2"] = b.sV.inc(ins)

        @b.on("tensor")
        def _(te):
            b.w(te, b.tk["f2_xq2"])
            b.bw(te, 2)
            ins = te.transpose(b.pb[2][0:1, 0:128], b.STAT[:, 145:146], b.IDENT[:, :])
            b.tk["f2_qr_ps"] = b.tdrain(te)

        @b.on("scalar")
        def _(s):
            b.w(s, b.tk["f2_qr_ps"])
            ins = s.copy(b.ROW1[0:1, 128:256], b.pb[2][0:1, 0:128])
            tok = b.sS.inc(ins)
            b.tk["f2_qr"] = tok
            b.bfree(tok, 2)

        @b.on("vector")
        def _(v):
            b.w(v, b.tk["f2_qr"])
            ins = v.tensor_scalar(b.ROW1[0:1, 256:384], b.ROW1[0:1, 128:256],
                                  b.SCAL[0:1, 23:24], None, ALU.mult)
            b.tk["f2_c2r"] = b.sV.inc(ins)

        for q in range(4):
            @b.on("tensor")
            def _(te, q=q):
                b.w(te, b.tk["f2_xq2"])
                b.bw(te, 4, 5)
                last = None
                for j in range(8):
                    c2 = q * 8 + j
                    bank = b.pb[4 + (j // 4)]
                    last = te.transpose(bank[:, (j % 4) * 128:(j % 4 + 1) * 128],
                                        XQ2[:, c2 * 128:(c2 + 1) * 128], b.IDENT[:, :])
                b.tk[f"f2_tp{q}"] = b.tdrain(te)

            @b.on("scalar")
            def _(s, q=q):
                b.w(s, b.tk[f"f2_tp{q}"])
                ins = None
                for j in range(8):
                    c2 = q * 8 + j
                    bank = b.pb[4 + (j // 4)]
                    ins = s.copy(XQ2FM[:, c2 * 128:(c2 + 1) * 128],
                                 bank[:, (j % 4) * 128:(j % 4 + 1) * 128])
                tok = b.sS.inc(ins)
                b.tk[f"f2_tpc{q}"] = tok
                b.bfree(tok, 4, 5)

        if DEBUG and tt == 0:
            @b.on("gpsimd")
            def _(g):
                b.w(g, b.tk["fq_am"])
                g.dma_start(out=b.h_dbgxn, in_=FSCR[:, 0:1024]).then_inc(b.sO.h, 16)
                b.sO.n += 16
                b.tk["dbgxn"] = b.sO.tok()

            @b.on("vector")
            def _(v):
                b.w(v, b.tk["dbgxn"])

            @b.on("gpsimd")
            def _(g):
                b.w(g, b.tk["fq_sx"])
                g.dma_start(out=b.h_dbgst, in_=b.STAT[:, 130:146]).then_inc(b.sO.h, 16)
                b.sO.n += 16
                b.tk["dbgst"] = b.sO.tok()

            @b.on("vector")
            def _(v):
                b.w(v, b.tk["dbgst"])

            @b.on("gpsimd")
            def _(g):
                b.w(g, b.tk["fq_xq"], b.tk["fg_ev7"], b.tk["f2_xq2"],
                    b.tk["f2_tpc3"], b.tk["f2_c2r"], b.tk["fq_sx"])
                for dst, src in ((b.h_dbgxq, XQT[:, :]),
                                 (b.h_dbggu, FSCR[:, :]),
                                 (b.h_dbgxq2, XQ2[:, :]),
                                 (b.h_dbgfm, XQ2FM[:, :]),
                                 (b.h_dbgrow, b.ROW1[0:1, :]),
                                 (b.h_dbgqc, QC[:, :])):
                    g.dma_start(out=dst, in_=src).then_inc(b.sO.h, 16)
                    b.sO.n += 16
                b.tk["dbgf"] = b.sO.tok()

            @b.on("vector")
            def _(v):
                b.w(v, b.tk["dbgf"])

        tbase = b.sT.n
        for c2 in range(32):
            @b.on("sync")
            def _(s, c2=c2):
                if c2 >= 2:
                    b.w(s, (b.sT, tbase + c2 - 1))
                else:
                    b.w(s, (b.sT, tbase))
                sm = b.sD0 if c2 % 2 == 0 else b.sD1
                sm.n += 16
                s.dma_start(out=WQD[:, (c2 % 2) * 1024:(c2 % 2) * 1024 + 1024],
                            in_=self.h_wqd[c2 * 128:(c2 + 1) * 128, :]).then_inc(sm.h, 16)
                b.tk["fd_wdma"] = sm.tok()

            @b.on("tensor")
            def _(te, c2=c2):
                b.w(te, b.tk["fd_wdma"], b.tk["f2_tpc3"])
                if c2 == 0:
                    b.bw(te, 6, 7)
                last = None
                for dc in range(8):
                    bank = b.pb[6 + (dc // 4)]
                    last = te.matmul(bank[:, (dc % 4) * 128:(dc % 4 + 1) * 128],
                                     WQD[:, (c2 % 2) * 1024 + dc * 128:
                                         (c2 % 2) * 1024 + (dc + 1) * 128],
                                     XQ2FM[:, c2 * 128:(c2 + 1) * 128],
                                     start=(c2 == 0 and dc % 4 == 0),
                                     stop=(c2 == 31 and dc % 4 == 3))
                b.sT.inc(last)
                if c2 == 31:
                    b.tk["fd_mm"] = b.tdrain(te)
        assert b.sT.n == tbase + 33

        @b.on("tensor")
        def _(te):
            b.w(te, b.tk["f2_c2r"])
            b.bw(te, 2)
            ins = te.matmul(b.pb[2][:, 128:256], b.ONES[0:1, :], b.ROW1[0:1, 256:384],
                            start=True, stop=True)
            b.tk["fd_bc"] = b.tdrain(te)

        @b.on("scalar")
        def _(s):
            b.w(s, b.tk["fd_bc"])
            ins = s.copy(b.TMP[:, 896:1024], b.pb[2][:, 128:256])
            tok = b.sS.inc(ins)
            b.tk["fd_bcs"] = tok
            b.bfree(tok, 2)

        @b.on("vector")
        def _(v):
            b.w(v, b.tk["fd_mm"], b.tk["fd_bcs"])
            ins = None
            for dc in range(8):
                bank = b.pb[6 + (dc // 4)]
                t_ = b.TMP[:, 0:128]
                v.tensor_mul(t_, bank[:, (dc % 4) * 128:(dc % 4 + 1) * 128],
                             b.TMP[:, 896:1024])
                ins = v.tensor_add(b.ACC[:, dc * T + t0:dc * T + t0 + 128],
                                   b.ACC[:, dc * T + t0:dc * T + t0 + 128], t_)
            tok = b.sV.inc(ins)
            b.tk["fd_acc"] = tok
            b.bfree(tok, 6, 7)

    def attn_pass(self, dr):
        b = self
        nc = self.nc
        with ExitStack() as astk:
            asb = lambda name, shape, dt=F32: astk.enter_context(
                nc.sbuf_tensor(name, shape, dt))
            XB = asb(f"XB{dr}", [128, 16 * 264])
            DT = asb(f"DT{dr}", [128, 16 * 256])
            XCD = asb(f"XCD{dr}", [128, 16 * 256])
            Y = asb(f"Y{dr}", [128, 16 * 256])
            XIN = asb(f"XIN{dr}", [128, 8 * 256])
            XDBL = asb(f"XDBL{dr}", [128, 256])
            ADA = asb(f"ADA{dr}", [128, 8 * 256])
            DBX = asb(f"DBX{dr}", [128, 256])
            HS = asb(f"HS{dr}", [128, 256])
            HCAR = asb(f"HCAR{dr}", [128, 256])
            XPT = asb(f"XPT{dr}", [128, 16 * 96])

            @b.on("sync")
            def _(s):
                b.w(s, b.tk.get(f"apdone{dr - 1}"))
                b.sX.n += 16
                s.dma_start(out=XPT[:, :], in_=bass.AP(
                    b.h_xpT.tensor, dr * DI * 96, [[96, 128], [128 * 96, 16], [1, 96]])
                ).then_inc(b.sX.h, 16)
                b.tk["ap_w"] = b.sX.tok()

            @b.on("vector")
            def _(v):
                b.w(v, b.tk["ffn"])
                v.memset(HCAR[:, :], 0.0)
                ins = None
                for cc in range(16):
                    ins = v.memset(XB[:, cc * 264 + 1:cc * 264 + 4], 0.0)
                b.tk["ap_init"] = b.sV.inc(ins)

            for tau in range(NTC):
                self._attn_chunk(dr, tau, XB, DT, XCD, Y, XIN, XDBL,
                                 ADA, DBX, HS, HCAR, XPT)
            b.tk[f"apdone{dr}"] = b.sV.tok()

    def _attn_chunk(self, dr, tau, XB, DT, XCD, Y, XIN, XDBL,
                    ADA, DBX, HS, HCAR, XPT):
        b = self
        o0 = tau * TC if dr == 0 else T - TC * (tau + 1)
        vc0 = dr * 16

        def xstg_rhs(c):
            if dr == 0:
                return XIN[:, c * TC:(c + 1) * TC]
            return bass.AP(XIN, c * TC + TC - 1, [[8 * TC, 128], [-1, TC]])

        @b.on("scalar")
        def _(g):
            b.w(g, b.tk.get(f"ap_xstg_done{dr}_{tau - 1}"))
            for c in range(8):
                b.sXI.n += 16
                g.dma_start(out=XIN[:, c * TC:(c + 1) * TC],
                            in_=b.h_x[c * 128:(c + 1) * 128, o0:o0 + TC]
                            ).then_inc(b.sXI.h, 16)
            b.tk["ap_xin"] = b.sXI.tok()

        @b.on("tensor")
        def _(te):
            b.w(te, b.tk["ln0"])
            b.bw(te, 2)
            te.matmul(b.pb[2][:, 0:TC], b.selbc(0, 64),
                      b.rln_rhs(0, o0, TC), start=True, stop=True)
            ins = te.matmul(b.pb[2][:, TC:2 * TC], b.selbc(32, 64),
                            b.rln_rhs(0, o0, TC), start=True, stop=True)
            b.tk["ap_msbc"] = b.tdrain(te)

        @b.on("vector")
        def _(v):
            b.w(v, b.tk["ap_msbc"], b.tk["ap_xin"], b.tk["ap_init"])
            b.w(v, b.tk.get(f"ap_xstg_use{dr}_{tau - 1}"))
            ins = None
            for c in range(8):
                xs = XIN[:, c * TC:(c + 1) * TC]
                v.tensor_sub(xs, xs, b.pb[2][:, 0:TC])
                ins = v.tensor_mul(xs, xs, b.pb[2][:, TC:2 * TC])
            tok = b.sV.inc(ins)
            b.tk["ap_xstg"] = tok
            b.tk[f"ap_xstg_done{dr}_{tau}"] = tok
            b.bfree(tok, 2)

        for mw in range(4):
            self._inproj_mw(dr, tau, mw, xstg_rhs, XB, None, is_z=False)

        for cc in range(16):
            @b.on("vector")
            def _(v, cc=cc):
                b.w(v, b.tk[f"ap_evac{dr}_{tau}_{cc // 4}"])
                b.w(v, b.tk.get(f"ap_silu{dr}_{tau}_{cc - 4}"))
                x0 = cc * 264
                t_ = b.TMP[:, (cc % 4) * 256:(cc % 4) * 256 + 256]
                vb = (vc0 + cc) * 8
                v.tensor_scalar(t_, XB[:, x0 + 1:x0 + 257],
                                b.VEC0[:, vb:vb + 1], b.VEC0[:, vb + 4:vb + 5],
                                ALU.mult, ALU.add)
                for k in (1, 2, 3):
                    v.scalar_tensor_tensor(t_, XB[:, x0 + 1 + k:x0 + 257 + k],
                                           b.VEC0[:, vb + k:vb + k + 1], t_,
                                           ALU.mult, ALU.add)
                ins = v.tensor_copy(XB[:, x0 + 1:x0 + 4], XB[:, x0 + 257:x0 + 260])
                b.tk[f"ap_conv{dr}_{tau}_{cc}"] = b.sV.inc(ins)

            @b.on("scalar")
            def _(s, cc=cc):
                b.w(s, b.tk[f"ap_conv{dr}_{tau}_{cc}"])
                t_ = b.TMP[:, (cc % 4) * 256:(cc % 4) * 256 + 256]
                ins = s.activation(XCD[:, cc * 256:(cc + 1) * 256], t_, AF.Silu)
                b.tk[f"ap_silu{dr}_{tau}_{cc}"] = b.sS.inc(ins)

        @b.on("tensor")
        def _(te):
            b.w(te, b.tk["ap_w"])
            b.bw(te, 4)
            last = None
            for cc in range(16):
                b.w(te, b.tk[f"ap_silu{dr}_{tau}_{cc}"])
                last = te.matmul(b.pb[4][0:96, 0:TC], XPT[:, cc * 96:(cc + 1) * 96],
                                 XCD[:, cc * 256:(cc + 1) * 256],
                                 start=(cc == 0), stop=(cc == 15))
            b.tk["ap_xp"] = b.tdrain(te)

        @b.on("scalar")
        def _(s):
            b.w(s, b.tk["ap_xp"])
            ins = s.copy(XDBL[0:96, :], b.pb[4][0:96, 0:TC])
            tok = b.sS.inc(ins)
            b.tk["ap_xdbl"] = tok
            b.bfree(tok, 4)

        @b.on("sync")
        def _(s):
            b.w(s, b.win2_use[0], b.win2_use[1], b.win2_use[2], b.win2_use[3])
            b.sR[0].n += 16
            s.dma_start(out=b.WIN2[0:64, :], in_=b.h_dtT[dr * 64:(dr + 1) * 64, :]
                        ).then_inc(b.sR[0].h, 16)
            b.tk["ap_dtw"] = b.sR[0].tok()

        for mq in range(4):
            @b.on("tensor")
            def _(te, mq=mq):
                b.w(te, b.tk["ap_xdbl"], b.tk["ap_dtw"])
                b.bw(te, 0, 1, 2, 3)
                last = None
                for j in range(4):
                    m = mq * 4 + j
                    last = te.matmul(b.pb[j][:, 0:TC],
                                     b.WIN2[0:64, m * 128:(m + 1) * 128],
                                     XDBL[0:64, 0:TC], start=True, stop=True)
                tok = b.tdrain(te)
                b.tk[f"ap_dtmm{dr}_{tau}_{mq}"] = tok
                if mq == 3:
                    for j in range(4):
                        b.win2_use[j] = tok

            @b.on("scalar")
            def _(s, mq=mq):
                b.w(s, b.tk[f"ap_dtmm{dr}_{tau}_{mq}"])
                ins = None
                for j in range(4):
                    m = mq * 4 + j
                    vb = (vc0 + m) * 8
                    t_ = b.TMP[:, 512 + (j % 2) * 256:512 + (j % 2) * 256 + 256]
                    s.activation(t_, b.pb[j][:, 0:TC], AF.Exp,
                                 bias=b.VEC0[:, vb + 5:vb + 6], scale=1.0)
                    ins = s.activation(DT[:, m * 256:(m + 1) * 256], t_, AF.Ln,
                                       bias=1.0, scale=1.0)
                tok = b.sS.inc(ins)
                b.tk[f"ap_dtev{dr}_{tau}_{mq}"] = tok
                b.bfree(tok, 0, 1, 2, 3)

        @b.on("vector")
        def _(v):
            b.w(v, b.tk["ap_xp"])
            for mq in range(4):
                b.w(v, b.tk[f"ap_dtev{dr}_{tau}_{mq}"])
            b.w(v, b.tk.get(f"ap_opmm{dr}_{tau - 1}_1_15"))
            for cc in range(16):
                vb = (vc0 + cc) * 8
                v.tensor_scalar(Y[:, cc * 256:(cc + 1) * 256],
                                XCD[:, cc * 256:(cc + 1) * 256],
                                b.VEC0[:, vb + 6:vb + 7], None, ALU.mult)
            ins = None
            for cc in range(16):
                ins = v.tensor_mul(XCD[:, cc * 256:(cc + 1) * 256],
                                   DT[:, cc * 256:(cc + 1) * 256],
                                   XCD[:, cc * 256:(cc + 1) * 256])
            b.tk["ap_dtx"] = b.sV.inc(ins)

        for n in range(NST):
            bank = b.pb[4 + (n % 2)]

            @b.on("tensor")
            def _(te, n=n, bank=bank):
                b.w(te, b.tk["ap_xdbl"])
                b.bw(te, 4 + (n % 2))
                te.matmul(bank[:, 0:TC], b.selbc(64 + n, 96),
                          XDBL[0:96, 0:TC], start=True, stop=True)
                te.matmul(bank[:, TC:2 * TC], b.selbc(80 + n, 96),
                          XDBL[0:96, 0:TC], start=True, stop=True)
                b.tk[f"ap_bc{dr}_{tau}_{n}"] = b.tdrain(te)

            for h in range(2):
                @b.on("scalar")
                def _(s, n=n, h=h):
                    b.w(s, b.tk["aprep"])
                    if h == 1:
                        b.w(s, b.tk[f"ap_scv{dr}_{tau}_{n}_0"])
                    ins = None
                    for j in range(8):
                        cc = h * 8 + j
                        ins = s.activation(ADA[:, j * 256:(j + 1) * 256],
                                           DT[:, cc * 256:(cc + 1) * 256], AF.Exp,
                                           bias=0.0,
                                           scale=b.A0SB[:, (vc0 + cc) * NST + n:
                                                        (vc0 + cc) * NST + n + 1])
                    b.tk[f"ap_da{dr}_{tau}_{n}_{h}"] = b.sS.inc(ins)

                @b.on("vector")
                def _(v, n=n, h=h, bank=bank):
                    b.w(v, b.tk[f"ap_da{dr}_{tau}_{n}_{h}"],
                        b.tk[f"ap_bc{dr}_{tau}_{n}"], b.tk["ap_dtx"])
                    Bbc = bank[:, 0:TC]
                    Cbc = bank[:, TC:2 * TC]
                    ins = None
                    for j in range(8):
                        cc = h * 8 + j
                        q = cc * NST + n
                        v.tensor_mul(DBX[:, :], XCD[:, cc * 256:(cc + 1) * 256], Bbc)
                        v.tensor_tensor_scan(HS[:, :], ADA[:, j * 256:(j + 1) * 256],
                                             DBX[:, :], HCAR[:, q:q + 1],
                                             ALU.mult, ALU.add)
                        v.tensor_mul(DBX[:, :], HS[:, :], Cbc)
                        v.tensor_add(Y[:, cc * 256:(cc + 1) * 256],
                                     Y[:, cc * 256:(cc + 1) * 256], DBX[:, :])
                        ins = v.tensor_copy(HCAR[:, q:q + 1], HS[:, 255:256])
                    tok = b.sV.inc(ins)
                    b.tk[f"ap_scv{dr}_{tau}_{n}_{h}"] = tok
                    if h == 1:
                        b.bfree(tok, 4 + (n % 2))
        b.tk[f"ap_scan{dr}_{tau}"] = b.sV.tok()

        for mw in range(4, 8):
            self._inproj_mw(dr, tau, mw, xstg_rhs, XB, Y, is_z=True)
        b.tk["ap_ydone"] = b.sV.tok()
        b.tk[f"ap_xstg_use{dr}_{tau}"] = b.sT.tok()

        for ow in range(2):
            for c in range(16):
                slot = c % 4

                @b.on("sync")
                def _(s, ow=ow, c=c, slot=slot):
                    b.w(s, b.win2_use[slot])
                    b.sR[slot].n += 16
                    s.dma_start(out=b.WIN2[:, slot * 512:slot * 512 + 512],
                                in_=b.h_outT[c * 128:(c + 1) * 128,
                                             ow * 512:(ow + 1) * 512]).then_inc(b.sR[slot].h, 16)
                    b.tk[f"ap_opw{dr}_{tau}_{ow}_{c}"] = b.sR[slot].tok()

                @b.on("tensor")
                def _(te, ow=ow, c=c, slot=slot):
                    b.w(te, b.tk[f"ap_opw{dr}_{tau}_{ow}_{c}"])
                    if c == 0:
                        b.w(te, b.tk["ap_ydone"])
                        b.bw(te, 6, 7)
                    last = None
                    for m in range(4):
                        bank = b.pb[6 + (m // 2)]
                        last = te.matmul(bank[:, (m % 2) * 256:(m % 2) * 256 + 256],
                                         b.WIN2[:, slot * 512 + m * 128:
                                                slot * 512 + (m + 1) * 128],
                                         Y[:, c * 256:(c + 1) * 256],
                                         start=(c == 0 and m % 2 == 0),
                                         stop=(c == 15 and m % 2 == 1))
                    tok = b.tdrain(te) if c == 15 else b.sT.inc(last)
                    b.tk[f"ap_opmm{dr}_{tau}_{ow}_{c}"] = tok
                    b.win2_use[slot] = tok
                    if ow == 1 and c == 15:
                        b.tk[f"ap_opmm{dr}_{tau}_1_15"] = tok

            @b.on("vector")
            def _(v, ow=ow):
                b.w(v, b.tk[f"ap_opmm{dr}_{tau}_{ow}_{15}"])
                ins = None
                for m in range(4):
                    mc = ow * 4 + m
                    bank = b.pb[6 + (m // 2)]
                    dst = b.ACC[:, mc * T + o0:mc * T + o0 + TC]
                    if dr == 0:
                        src = bank[:, (m % 2) * 256:(m % 2) * 256 + 256]
                    else:
                        src = bass.AP(bank, (m % 2) * 256 + 255, [[512, 128], [-1, 256]])
                    ins = v.tensor_add(dst, dst, src)
                tok = b.sV.inc(ins)
                b.tk[f"ap_opev{dr}_{tau}_{ow}"] = tok
                b.bfree(tok, 6, 7)

    def _inproj_mw(self, dr, tau, mw, xstg_rhs, XB, Y, is_z):
        b = self
        for c in range(8):
            slot = c % 4

            @b.on("sync")
            def _(s, mw=mw, c=c, slot=slot):
                b.w(s, b.win_use[slot])
                b.sQ[slot].n += 16
                s.dma_start(out=b.WIN[:, slot * 512:slot * 512 + 512],
                            in_=b.h_inT[c * 128:(c + 1) * 128,
                                        mw * 512:(mw + 1) * 512]).then_inc(b.sQ[slot].h, 16)
                b.tk[f"ap_ipw{dr}_{tau}_{mw}_{c}"] = b.sQ[slot].tok()

            @b.on("tensor")
            def _(te, mw=mw, c=c, slot=slot):
                b.w(te, b.tk[f"ap_ipw{dr}_{tau}_{mw}_{c}"])
                if c == 0:
                    b.w(te, b.tk["ap_xstg"])
                    b.bw(te, 0, 1, 2, 3)
                last = None
                for m in range(4):
                    last = te.matmul(b.pb[m][:, (mw % 2) * 256:(mw % 2) * 256 + 256],
                                     b.WIN[:, slot * 512 + m * 128:
                                            slot * 512 + (m + 1) * 128],
                                     xstg_rhs(c),
                                     start=(c == 0), stop=(c == 7))
                tok = b.tdrain(te) if c == 7 else b.sT.inc(last)
                b.win_use[slot] = tok
                if c == 7:
                    b.tk[f"ap_ipmm{dr}_{tau}_{mw}"] = tok

        if not is_z:
            @b.on("scalar")
            def _(s, mw=mw):
                b.w(s, b.tk[f"ap_ipmm{dr}_{tau}_{mw}"])
                b.w(s, b.tk.get(f"ap_conv{dr}_{tau - 1}_{mw * 4 + 3}"))
                ins = None
                for m in range(4):
                    cc = mw * 4 + m
                    ins = s.copy(XB[:, cc * 264 + 4:cc * 264 + 260],
                                 b.pb[m][:, (mw % 2) * 256:(mw % 2) * 256 + 256])
                tok = b.sS.inc(ins)
                b.tk[f"ap_evac{dr}_{tau}_{mw}"] = tok
                b.bfree(tok, 0, 1, 2, 3)
        else:
            @b.on("scalar")
            def _(s, mw=mw):
                b.w(s, b.tk[f"ap_ipmm{dr}_{tau}_{mw}"])
                ins = None
                for m in range(4):
                    ins = s.activation(b.TMP[:, m * 256:(m + 1) * 256],
                                       b.pb[m][:, (mw % 2) * 256:(mw % 2) * 256 + 256],
                                       AF.Silu)
                tok = b.sS.inc(ins)
                b.tk[f"ap_zsilu{dr}_{tau}_{mw}"] = tok
                b.bfree(tok, 0, 1, 2, 3)

            @b.on("vector")
            def _(v, mw=mw):
                b.w(v, b.tk[f"ap_zsilu{dr}_{tau}_{mw}"])
                ins = None
                for m in range(4):
                    cc = (mw - 4) * 4 + m
                    ins = v.tensor_mul(Y[:, cc * 256:(cc + 1) * 256],
                                       Y[:, cc * 256:(cc + 1) * 256],
                                       b.TMP[:, m * 256:(m + 1) * 256])
                b.tk[f"ap_zg{dr}_{tau}_{mw}"] = b.sV.inc(ins)

    def layer1(self):
        b = self
        nc = self.nc
        with ExitStack() as lstk:
            lsb = lambda name, shape, dt=F32: lstk.enter_context(
                nc.sbuf_tensor(name, shape, dt))
            X1P = lsb("LX1P", [128, 8 * 512])
            DT1 = lsb("LDT1", [128, 8 * 512])
            Y1 = lsb("LY1", [128, 8 * 512])
            XN1G = lsb("LXN1G", [128, 8 * 512])
            XD1B = lsb("LXD1B", [128, 512])
            XD1C = lsb("LXD1C", [128, 512])
            ADA1 = lsb("LADA1", [128, 4 * 512])
            DBX1 = lsb("LDBX1", [128, 512])
            HS1 = lsb("LHS1", [128, 512])
            HC1 = lsb("LHC1", [128, 128])
            XG = lsb("LXG", [128, 8 * 256])
            GBC = lsb("LGBC", [128, 256])
            HE = lsb("LHE", [128, 16 * 256])

            @b.on("vector")
            def _(v):
                b.w(v, b.tk["ln1"])
                ins = v.memset(HC1[:, :], 0.0)
                b.tk["l1_init"] = b.sV.inc(ins)

            for g in range(4):
                self._gating_group(g)
                self._xn1_group(g, XN1G)
                self._l1attn_group(g, X1P, DT1, Y1, XN1G, XD1B, XD1C,
                                   ADA1, DBX1, HS1, HC1)
                for sg in range(2):
                    self._moe_subgroup(g, sg, XN1G, XG, GBC, HE)

    def _gating_group(self, g):
        b = self
        s0 = g * 512

        @b.on("tensor")
        def _(te):
            b.w(te, b.tk["ln1"], b.tk["wsume"])
            b.bw(te, 2, 3, 4)
            for c in range(8):
                te.matmul(b.pb[2][0:8, 0:512], b.GATEW[:, c * 8:(c + 1) * 8],
                          b.ACC[:, c * T + s0:c * T + s0 + 512],
                          start=(c == 0), stop=(c == 7))
            te.matmul(b.pb[3][:, 0:512], b.selbc(0, 64), b.rln_rhs(1, s0, 512),
                      start=True, stop=True)
            ins = te.matmul(b.pb[4][:, 0:512], b.selbc(32, 64), b.rln_rhs(1, s0, 512),
                            start=True, stop=True)
            b.tk["gt_mm"] = b.tdrain(te)

        @b.on("vector")
        def _(v):
            b.w(v, b.tk["gt_mm"])
            GL = b.STAT[0:8, 0:512]
            v.tensor_copy(GL, b.pb[2][0:8, 0:512])
            v.scalar_tensor_tensor(GL, b.pb[3][0:8, 0:512], b.SCAL[0:8, 16:17],
                                   GL, ALU.mult, ALU.subtract)
            v.tensor_mul(GL, GL, b.pb[4][0:8, 0:512])
            ins = v.tensor_scalar(GL, GL, -1.0, None, ALU.mult)
            tok = b.sV.inc(ins)
            b.tk["gt_log"] = tok
            b.bfree(tok, 2, 3, 4)

        for j in range(4):
            @b.on("tensor")
            def _(te, j=j):
                b.w(te, b.tk["gt_log"])
                b.bw(te, 5)
                ins = te.transpose(b.pb[5][:, 0:8], b.STAT[0:8, j * 128:(j + 1) * 128],
                                   b.IDENT[0:8, 0:8])
                b.tk["gt_tp"] = b.tdrain(te)

            @b.on("scalar")
            def _(s, j=j):
                b.w(s, b.tk["gt_tp"])
                b.w(s, b.tk.get("gt_gtok_free"))
                ins = s.copy(b.GTOK[:, 0:8], b.pb[5][:, 0:8])
                tok = b.sS.inc(ins)
                b.tk["gt_tok"] = tok
                b.bfree(tok, 5)

            @b.on("vector")
            def _(v, j=j):
                b.w(v, b.tk["gt_tok"])
                ins = v.reduce_max(b.SCAL[:, 24:25], b.GTOK[:, 0:8], AX.X, negate=True)
                b.tk["gt_nm"] = b.sV.inc(ins)

            @b.on("scalar")
            def _(s, j=j):
                b.w(s, b.tk["gt_nm"])
                ins = s.activation(b.GTOK[:, 0:8], b.GTOK[:, 0:8], AF.Exp,
                                   bias=b.SCAL[:, 24:25], scale=1.0)
                b.tk["gt_exp"] = b.sS.inc(ins)

            @b.on("vector")
            def _(v, j=j):
                b.w(v, b.tk["gt_exp"])
                P = b.GTOK[:, 0:8]
                sc = b.SCAL
                b.ch(v, v.reduce_sum(sc[:, 25:26], P, AX.X))
                b.ch(v, v.reciprocal(sc[:, 26:27], sc[:, 25:26]))
                b.ch(v, v.tensor_scalar(P, P, sc[:, 26:27], None, ALU.mult))
                b.ch(v, v.reduce_max(sc[:, 27:28], P, AX.X))
                b.ch(v, v.scalar_tensor_tensor(b.TMP[:, 0:8], P, sc[:, 27:28], P,
                                               ALU.is_lt, ALU.mult))
                b.ch(v, v.reduce_max(sc[:, 28:29], b.TMP[:, 0:8], AX.X))
                b.ch(v, v.scalar_tensor_tensor(P, P, sc[:, 28:29], P,
                                               ALU.is_ge, ALU.mult))
                b.ch(v, v.reduce_sum(sc[:, 29:30], P, AX.X))
                b.ch(v, v.tensor_scalar(sc[:, 29:30], sc[:, 29:30], 1e-12,
                                        None, ALU.max))
                b.ch(v, v.reciprocal(sc[:, 30:31], sc[:, 29:30]))
                ins = v.tensor_scalar(P, P, sc[:, 30:31], None, ALU.mult)
                b.tk["gt_p"] = b.sV.inc(ins)

            @b.on("tensor")
            def _(te, j=j):
                b.w(te, b.tk["gt_p"])
                b.bw(te, 5)
                ins = te.transpose(b.pb[5][0:8, 128:256], b.GTOK[:, 0:8],
                                   b.IDENT[:, :])
                b.tk["gt_ptp"] = b.tdrain(te)

            @b.on("scalar")
            def _(s, j=j, g=g):
                b.w(s, b.tk["gt_ptp"])
                ins = s.copy(b.GROWS[0:8, s0 + j * 128:s0 + (j + 1) * 128],
                             b.pb[5][0:8, 128:256])
                tok = b.sS.inc(ins)
                b.bfree(tok, 5)
                b.tk["gt_gtok_free"] = tok
        b.tk[f"gt_done{g}"] = b.sS.tok()

    def _xn1_group(self, g, XN1G):
        b = self
        s0 = g * 512

        @b.on("tensor")
        def _(te):
            b.w(te, b.tk["ln1"])
            b.bw(te, 2, 3)
            te.matmul(b.pb[2][:, 0:512], b.selbc(0, 64), b.rln_rhs(1, s0, 512),
                      start=True, stop=True)
            ins = te.matmul(b.pb[3][:, 0:512], b.selbc(32, 64), b.rln_rhs(1, s0, 512),
                            start=True, stop=True)
            b.tk["xn1_bc"] = b.tdrain(te)

        @b.on("vector")
        def _(v, g=g):
            b.w(v, b.tk["xn1_bc"], b.tk.get(f"xn1_cons{g - 1}"))
            ins = None
            for c in range(8):
                xs = XN1G[:, c * 512:(c + 1) * 512]
                v.tensor_sub(xs, b.ACC[:, c * T + s0:c * T + s0 + 512], b.pb[2][:, 0:512])
                ins = v.tensor_mul(xs, xs, b.pb[3][:, 0:512])
            tok = b.sV.inc(ins)
            b.tk[f"xn1_{g}"] = tok
            b.bfree(tok, 2, 3)

    def _l1attn_group(self, g, X1P, DT1, Y1, XN1G, XD1B, XD1C,
                      ADA1, DBX1, HS1, HC1):
        b = self
        s0 = g * 512

        for mw in range(2):
            self._l1_proj_mw(g, b.h_wxz, mw * 512, mw, XN1G, "xz", X1P,
                             kind="copy", rdy=b.tk[f"xn1_{g}"])
        b.tk[f"l1x_xz{g}"] = b.sS.tok()
        for mw in range(2):
            self._l1_proj_mw(g, b.h_wdt, mw * 512, mw, X1P, "dt", DT1,
                             kind="softplus", rdy=b.tk[f"l1x_xz{g}"])
        b.tk[f"l1x_dt{g}"] = b.sS.tok()

        @b.on("tensor")
        def _(te):
            b.w(te, b.tk[f"xn1_{g}"])
            b.bw(te, 4, 5)
            last = None
            for c in range(8):
                last = te.matmul(b.pb[4][0:16, 0:512], b.WBSB[:, c * NST:(c + 1) * NST],
                                 XN1G[:, c * 512:(c + 1) * 512],
                                 start=(c == 0), stop=(c == 7))
            for c in range(8):
                last = te.matmul(b.pb[5][0:16, 0:512], b.WCSB[:, c * NST:(c + 1) * NST],
                                 XN1G[:, c * 512:(c + 1) * 512],
                                 start=(c == 0), stop=(c == 7))
            b.tk["l1_bcmm"] = b.tdrain(te)

        @b.on("scalar")
        def _(s):
            b.w(s, b.tk["l1_bcmm"])
            s.copy(XD1B[0:16, :], b.pb[4][0:16, 0:512])
            ins = s.copy(XD1C[0:16, :], b.pb[5][0:16, 0:512])
            tok = b.sS.inc(ins)
            b.tk["l1_bcc"] = tok
            b.bfree(tok, 4, 5)

        @b.on("vector")
        def _(v, g=g):
            b.w(v, b.tk[f"l1x_dt{g}"])
            for c in range(8):
                v.tensor_scalar(Y1[:, c * 512:(c + 1) * 512],
                                X1P[:, c * 512:(c + 1) * 512],
                                bass.AP(b.VEC1, c * 4 + 1, [[32, 128], [4, 1]]),
                                None, ALU.mult)
            ins = None
            for c in range(8):
                ins = v.tensor_mul(X1P[:, c * 512:(c + 1) * 512],
                                   DT1[:, c * 512:(c + 1) * 512],
                                   X1P[:, c * 512:(c + 1) * 512])
            b.tk["l1_dtx"] = b.sV.inc(ins)

        for n in range(NST):
            bB = b.pb[4 + 2 * (n % 2)]
            bC = b.pb[5 + 2 * (n % 2)]

            @b.on("tensor")
            def _(te, n=n, bB=bB, bC=bC):
                b.w(te, b.tk["l1_bcc"])
                b.bw(te, 4 + 2 * (n % 2), 5 + 2 * (n % 2))
                te.matmul(bB[:, 0:512], b.selbc(n, 16), XD1B[0:16, 0:512],
                          start=True, stop=True)
                te.matmul(bC[:, 0:512], b.selbc(n, 16), XD1C[0:16, 0:512],
                          start=True, stop=True)
                b.tk[f"l1_nbc{g}_{n}"] = b.tdrain(te)

            for h in range(2):
                @b.on("scalar")
                def _(s, n=n, h=h):
                    b.w(s, b.tk["aprep"])
                    if h == 0:
                        b.w(s, b.tk["l1_dtx"] if n == 0
                            else b.tk[f"l1_scv{g}_{n - 1}_1"])
                    else:
                        b.w(s, b.tk[f"l1_scv{g}_{n}_0"])
                    ins = None
                    for j in range(4):
                        c = h * 4 + j
                        ins = s.activation(ADA1[:, j * 512:(j + 1) * 512],
                                           DT1[:, c * 512:(c + 1) * 512], AF.Exp,
                                           bias=0.0,
                                           scale=b.A1SB[:, c * NST + n:c * NST + n + 1])
                    b.tk[f"l1_da{g}_{n}_{h}"] = b.sS.inc(ins)

                @b.on("vector")
                def _(v, n=n, g=g, h=h, bB=bB, bC=bC):
                    b.w(v, b.tk[f"l1_da{g}_{n}_{h}"], b.tk[f"l1_nbc{g}_{n}"],
                        b.tk["l1_dtx"])
                    ins = None
                    for j in range(4):
                        c = h * 4 + j
                        q = c * NST + n
                        v.tensor_mul(DBX1[:, :], X1P[:, c * 512:(c + 1) * 512],
                                     bB[:, 0:512])
                        v.tensor_tensor_scan(HS1[:, :], ADA1[:, j * 512:(j + 1) * 512],
                                             DBX1[:, :], HC1[:, q:q + 1],
                                             ALU.mult, ALU.add)
                        v.tensor_mul(DBX1[:, :], HS1[:, :], bC[:, 0:512])
                        v.tensor_add(Y1[:, c * 512:(c + 1) * 512],
                                     Y1[:, c * 512:(c + 1) * 512], DBX1[:, :])
                        ins = v.tensor_copy(HC1[:, q:q + 1], HS1[:, 511:512])
                    tok = b.sV.inc(ins)
                    b.tk[f"l1_scv{g}_{n}_{h}"] = tok
                    if h == 1:
                        b.bfree(tok, 4 + 2 * (n % 2), 5 + 2 * (n % 2))
        b.tk[f"l1_scan{g}"] = b.sV.tok()

        for c in range(8):
            @b.on("scalar")
            def _(s, c=c):
                b.w(s, b.tk[f"l1_scan{g}"])
                b.w(s, b.tk.get(f"l1_spk{g}_{c - 2}"))
                ins = s.activation(b.TMP[:, (c % 2) * 512:(c % 2) * 512 + 512],
                                   Y1[:, c * 512:(c + 1) * 512], AF.Sigmoid,
                                   bias=b.SCAL[:, 8 + c:9 + c],
                                   scale=b.SCAL[:, 18:19])
                b.tk[f"l1_sg{g}_{c}"] = b.sS.inc(ins)

            @b.on("vector")
            def _(v, c=c):
                b.w(v, b.tk[f"l1_sg{g}_{c}"])
                ins = v.tensor_mul(Y1[:, c * 512:(c + 1) * 512],
                                   Y1[:, c * 512:(c + 1) * 512],
                                   b.TMP[:, (c % 2) * 512:(c % 2) * 512 + 512])
                b.tk[f"l1_spk{g}_{c}"] = b.sV.inc(ins)

        for mw in range(2):
            self._l1_proj_mw(g, b.h_wxz, D + mw * 512, mw, XN1G, f"z{mw}", Y1,
                             kind="siluz", zoff=mw * 4, rdy=b.tk[f"xn1_{g}"])
        b.tk[f"l1_yg{g}"] = b.sV.tok()

        for ow in range(2):
            for c in range(8):
                slot = c % 4

                @b.on("sync")
                def _(s, ow=ow, c=c, slot=slot):
                    b.w(s, b.win2_use[slot])
                    b.sR[slot].n += 16
                    s.dma_start(out=b.WIN2[:, slot * 512:slot * 512 + 512],
                                in_=b.h_wo1[c * 128:(c + 1) * 128,
                                            ow * 512:(ow + 1) * 512]).then_inc(b.sR[slot].h, 16)
                    b.tk[f"l1_opw{g}_{ow}_{c}"] = b.sR[slot].tok()

                @b.on("tensor")
                def _(te, ow=ow, c=c, slot=slot):
                    b.w(te, b.tk[f"l1_opw{g}_{ow}_{c}"])
                    if c == 0:
                        b.w(te, b.tk[f"l1_yg{g}"])
                        b.bw(te, 0, 1, 2, 3)
                    last = None
                    for m in range(4):
                        last = te.matmul(b.pb[m][:, 0:512],
                                         b.WIN2[:, slot * 512 + m * 128:
                                                slot * 512 + (m + 1) * 128],
                                         Y1[:, c * 512:(c + 1) * 512],
                                         start=(c == 0), stop=(c == 7))
                    tok = b.tdrain(te) if c == 7 else b.sT.inc(last)
                    b.win2_use[slot] = tok
                    b.tk[f"l1_opmm{g}_{ow}_{c}"] = tok

            @b.on("vector")
            def _(v, ow=ow):
                b.w(v, b.tk[f"l1_opmm{g}_{ow}_{7}"])
                ins = None
                for m in range(4):
                    mc = ow * 4 + m
                    dst = b.ACC[:, mc * T + s0:mc * T + s0 + 512]
                    v.tensor_add(dst, dst, b.pb[m][:, 0:512])
                    ins = v.tensor_sub(dst, dst, XN1G[:, mc * 512:(mc + 1) * 512])
                tok = b.sV.inc(ins)
                b.tk[f"l1_opev{g}_{ow}"] = tok
                b.bfree(tok, 0, 1, 2, 3)
        b.tk[f"l1_done{g}"] = b.sV.tok()

    def _l1_proj_mw(self, g, h_w, colbase, mw, RHS, tag, DST, kind="copy",
                    zoff=0, rdy=None):
        b = self

        for c in range(8):
            slot = c % 4

            @b.on("sync")
            def _(s, c=c, slot=slot):
                b.w(s, b.win_use[slot])
                b.sQ[slot].n += 16
                s.dma_start(out=b.WIN[:, slot * 512:slot * 512 + 512],
                            in_=h_w[c * 128:(c + 1) * 128, colbase:colbase + 512]
                            ).then_inc(b.sQ[slot].h, 16)
                b.tk[f"l1p_{tag}_w{g}_{c}"] = b.sQ[slot].tok()

            @b.on("tensor")
            def _(te, c=c, slot=slot):
                b.w(te, b.tk[f"l1p_{tag}_w{g}_{c}"])
                if c == 0:
                    b.w(te, rdy)
                    b.bw(te, 0, 1, 2, 3)
                last = None
                for m in range(4):
                    last = te.matmul(b.pb[m][:, 0:512],
                                     b.WIN[:, slot * 512 + m * 128:
                                            slot * 512 + (m + 1) * 128],
                                     RHS[:, c * 512:(c + 1) * 512],
                                     start=(c == 0), stop=(c == 7))
                tok = b.tdrain(te) if c == 7 else b.sT.inc(last)
                b.win_use[slot] = tok
                if c == 7:
                    b.tk[f"l1p_{tag}_mm{g}"] = tok

        if kind == "copy":
            @b.on("scalar")
            def _(s):
                b.w(s, b.tk[f"l1p_{tag}_mm{g}"])
                ins = None
                for m in range(4):
                    cc = mw * 4 + m
                    ins = s.copy(DST[:, cc * 512:(cc + 1) * 512], b.pb[m][:, 0:512])
                tok = b.sS.inc(ins)
                b.bfree(tok, 0, 1, 2, 3)
        elif kind == "softplus":
            @b.on("scalar")
            def _(s):
                b.w(s, b.tk[f"l1p_{tag}_mm{g}"])
                ins = None
                for m in range(4):
                    cc = mw * 4 + m
                    t_ = b.TMP[:, (m % 2) * 512:(m % 2) * 512 + 512]
                    s.activation(t_, b.pb[m][:, 0:512], AF.Exp,
                                 bias=bass.AP(b.VEC1, cc * 4, [[32, 128], [4, 1]]),
                                 scale=1.0)
                    ins = s.activation(DST[:, cc * 512:(cc + 1) * 512], t_, AF.Ln,
                                       bias=1.0, scale=1.0)
                tok = b.sS.inc(ins)
                b.bfree(tok, 0, 1, 2, 3)
        else:
            for pair in range(2):
                @b.on("scalar")
                def _(s, pair=pair):
                    b.w(s, b.tk[f"l1p_{tag}_mm{g}"],
                        b.tk.get(f"l1p_{tag}_gv{g}_{pair - 1}"))
                    ins = None
                    for j in range(2):
                        m = pair * 2 + j
                        ins = s.activation(b.STAT[:, j * 512:(j + 1) * 512],
                                           b.pb[m][:, 0:512], AF.Silu)
                    b.tk[f"l1p_{tag}_sl{g}_{pair}"] = b.sS.inc(ins)

                @b.on("vector")
                def _(v, pair=pair):
                    b.w(v, b.tk[f"l1p_{tag}_sl{g}_{pair}"])
                    ins = None
                    for j in range(2):
                        cc = zoff + pair * 2 + j
                        ins = v.tensor_mul(DST[:, cc * 512:(cc + 1) * 512],
                                           DST[:, cc * 512:(cc + 1) * 512],
                                           b.STAT[:, j * 512:(j + 1) * 512])
                    tok = b.sV.inc(ins)
                    b.tk[f"l1p_{tag}_gv{g}_{pair}"] = tok
                    if pair == 1:
                        b.bfree(tok, 0, 1, 2, 3)

    def _moe_subgroup(self, g, sg, XN1G, XG, GBC, HE):
        b = self
        s0 = g * 512
        sgc = s0 + sg * 256

        for e in range(8):
            @b.on("tensor")
            def _(te, e=e):
                b.w(te, b.tk[f"gt_done{g}"], b.tk[f"xn1_{g}"])
                b.bw(te, 2)
                ins = te.matmul(b.pb[2][:, 0:256], b.selbc(e, 8),
                                b.GROWS[0:8, sgc:sgc + 256],
                                start=True, stop=True)
                b.tk["moe_gbc"] = b.tdrain(te)

            @b.on("vector")
            def _(v, e=e):
                b.w(v, b.tk["moe_gbc"])
                b.w(v, b.tk.get(f"moe_w1use{g}_{sg}_{e - 1}"))
                b.w(v, b.tk.get(f"moe_hev_v{g}_{sg}_{e - 1}_1"))
                v.tensor_copy(GBC[:, :], b.pb[2][:, 0:256])
                ins = None
                for c in range(8):
                    ins = v.tensor_mul(XG[:, c * 256:(c + 1) * 256],
                                       XN1G[:, c * 512 + sg * 256:
                                            c * 512 + sg * 256 + 256],
                                       b.pb[2][:, 0:256])
                tok = b.sV.inc(ins)
                b.tk[f"moe_xg{g}_{sg}_{e}"] = tok
                b.bfree(tok, 2)

            for mh in range(2):
                for c in range(8):
                    slot = c % 4

                    @b.on("sync")
                    def _(s, e=e, mh=mh, c=c, slot=slot):
                        b.w(s, b.win_use[slot], b.win2_use[slot])
                        for half in range(2):
                            src = bass.AP(b.h_W1.tensor,
                                          e * D * 2048 + c * 128 * 2048 +
                                          mh * 1024 + half * 512,
                                          [[2048, 128], [1, 512]])
                            dst = b.WIN if half == 0 else b.WIN2
                            sm = b.sQ[slot] if half == 0 else b.sR[slot]
                            sm.n += 16
                            s.dma_start(out=dst[:, slot * 512:slot * 512 + 512],
                                        in_=src).then_inc(sm.h, 16)
                        b.tk[f"moe_w1w{g}_{sg}_{e}_{mh}_{c}"] = b.sQ[slot].tok()
                        b.tk[f"moe_w1w2{g}_{sg}_{e}_{mh}_{c}"] = b.sR[slot].tok()

                    @b.on("tensor")
                    def _(te, e=e, mh=mh, c=c, slot=slot):
                        b.w(te, b.tk[f"moe_w1w{g}_{sg}_{e}_{mh}_{c}"],
                            b.tk[f"moe_w1w2{g}_{sg}_{e}_{mh}_{c}"],
                            b.tk[f"moe_xg{g}_{sg}_{e}"])
                        if c == 0:
                            b.bw(te, 0, 1, 2, 3)
                        last = None
                        for m in range(8):
                            buf = b.WIN if m < 4 else b.WIN2
                            mm_ = m % 4
                            last = te.matmul(
                                b.pb[m // 2][:, (m % 2) * 256:(m % 2) * 256 + 256],
                                buf[:, slot * 512 + mm_ * 128:
                                    slot * 512 + (mm_ + 1) * 128],
                                XG[:, c * 256:(c + 1) * 256],
                                start=(c == 0 and m % 2 == 0),
                                stop=(c == 7 and m % 2 == 1))
                        tok = b.tdrain(te) if c == 7 else b.sT.inc(last)
                        b.win_use[slot] = tok
                        b.win2_use[slot] = tok
                        b.tk[f"moe_w1mm{g}_{sg}_{e}_{mh}_{c}"] = tok
                        if mh == 1 and c == 7:
                            b.tk[f"moe_w1use{g}_{sg}_{e}"] = tok

                @b.on("vector")
                def _(v, e=e, mh=mh):
                    b.w(v, b.tk[f"moe_w1mm{g}_{sg}_{e}_{mh}_7"])
                    ins = None
                    for m in range(8):
                        hm = mh * 8 + m
                        ins = v.scalar_tensor_tensor(
                            HE[:, hm * 256:(hm + 1) * 256], GBC[:, :],
                            b.B1C[:, e * 16 + hm:e * 16 + hm + 1],
                            b.pb[m // 2][:, (m % 2) * 256:(m % 2) * 256 + 256],
                            ALU.mult, ALU.add)
                    tok = b.sV.inc(ins)
                    b.tk[f"moe_hev_v{g}_{sg}_{e}_{mh}"] = tok
                    b.bfree(tok, 0, 1, 2, 3)

                @b.on("scalar")
                def _(s, e=e, mh=mh):
                    b.w(s, b.tk[f"moe_hev_v{g}_{sg}_{e}_{mh}"])
                    ins = None
                    for m in range(8):
                        hm = mh * 8 + m
                        ins = s.activation(HE[:, hm * 256:(hm + 1) * 256],
                                           HE[:, hm * 256:(hm + 1) * 256], AF.Relu)
                    tok = b.sS.inc(ins)
                    b.tk[f"moe_hev{g}_{sg}_{e}_{mh}"] = tok

            for ch in range(16):
                @b.on("sync")
                def _(s, e=e, ch=ch):
                    b.w(s, b.tk.get(f"moe_w2use{g}_{sg}_{e}_{ch - 2}"))
                    if ch < 2:
                        b.w(s, b.win_use[0], b.win_use[1], b.win_use[2], b.win_use[3])
                    src = bass.AP(b.h_W2.tensor, e * 2048 * D + ch * 128 * D,
                                  [[D, 128], [1, D]])
                    sm = b.sQ[(ch % 2) * 2]
                    sm.n += 16
                    s.dma_start(out=b.WIN[:, (ch % 2) * 1024:(ch % 2) * 1024 + 1024],
                                in_=src).then_inc(sm.h, 16)
                    b.tk[f"moe_w2w{g}_{sg}_{e}_{ch}"] = sm.tok()

                @b.on("tensor")
                def _(te, e=e, ch=ch):
                    b.w(te, b.tk[f"moe_w2w{g}_{sg}_{e}_{ch}"],
                        b.tk[f"moe_hev{g}_{sg}_{e}_{ch // 8}"])
                    if e == 0 and ch == 0:
                        b.bw(te, 4, 5, 6, 7)
                    last = None
                    for m in range(8):
                        last = te.matmul(
                            b.pb[4 + m // 2][:, (m % 2) * 256:(m % 2) * 256 + 256],
                            b.WIN[:, (ch % 2) * 1024 + m * 128:
                                  (ch % 2) * 1024 + (m + 1) * 128],
                            HE[:, ch * 256:(ch + 1) * 256],
                            start=(e == 0 and ch == 0 and m % 2 == 0), stop=False)
                    tok = b.sT.inc(last)
                    b.tk[f"moe_w2use{g}_{sg}_{e}_{ch}"] = tok
                    if ch >= 14:
                        b.win_use[0] = tok
                        b.win_use[1] = tok
                        b.win_use[2] = tok
                        b.win_use[3] = tok

        @b.on("tensor")
        def _(te):
            b.w(te, b.tk[f"moe_w2use{g}_{sg}_7_15"])
            last = None
            for m in range(8):
                last = te.matmul(
                    b.pb[4 + m // 2][:, (m % 2) * 256:(m % 2) * 256 + 256],
                    b.B2R[0:8, m * 128:(m + 1) * 128],
                    b.GROWS[0:8, sgc:sgc + 256],
                    start=False, stop=(m % 2 == 1))
            b.tk[f"moe_b2{g}_{sg}"] = b.tdrain(te)

        @b.on("vector")
        def _(v):
            b.w(v, b.tk[f"moe_b2{g}_{sg}"])
            ins = None
            for m in range(8):
                dst = b.ACC[:, m * T + sgc:m * T + sgc + 256]
                ins = v.tensor_add(dst, dst,
                                   b.pb[4 + m // 2][:, (m % 2) * 256:(m % 2) * 256 + 256])
            tok = b.sV.inc(ins)
            b.tk[f"moe_acc{g}_{sg}"] = tok
            b.bfree(tok, 4, 5, 6, 7)
            if sg == 1:
                b.tk[f"xn1_cons{g}"] = tok

    def epilogue(self):
        b = self
        done = b.tk.get("final", b.tk["x"])
        for tt in range(16):
            @b.on("tensor")
            def _(te, tt=tt):
                b.w(te, done, b.tk["ones"])
                b.bw(te, 6, 7)
                last = None
                for c in range(8):
                    bank = b.pb[6 + (c // 4)]
                    last = te.transpose(bank[:, (c % 4) * 128:(c % 4 + 1) * 128],
                                        b.ACC[:, c * T + tt * 128: c * T + (tt + 1) * 128],
                                        b.IDENT[:, :])
                b.tk[f"eptp{tt}"] = b.tdrain(te)

            @b.on("vector")
            def _(v, tt=tt):
                b.w(v, b.tk[f"eptp{tt}"], b.tk.get(f"epst{tt - 1}"))
                v.tensor_copy(b.TMP[:, 0:512], b.pb[6][:, :512])
                ins = v.tensor_copy(b.TMP[:, 512:1024], b.pb[7][:, :512])
                tok = b.sV.inc(ins)
                b.tk[f"epcp{tt}"] = tok
                b.bfree(tok, 6, 7)

            @b.on("gpsimd")
            def _(g, tt=tt):
                b.w(g, b.tk[f"epcp{tt}"])
                g.dma_start(out=b.h_out[tt * 128:(tt + 1) * 128, :],
                            in_=b.TMP[:, :1024]).then_inc(b.sO.h, 16)
                b.sO.n += 16
                b.tk[f"epst{tt}"] = b.sO.tok()

        @b.on("gpsimd")
        def _(g):
            g.wait_ge(b.sO.h, b.sO.n)


def _prep_inputs(i):
    x = np.asarray(i["L0_attn_hidden_states"], dtype=np.float32)
    consts = np.eye(128, dtype=np.float32)
    inT = np.ascontiguousarray(np.asarray(i["L0_attn_in_proj_w"], np.float32).T)
    outT = np.ascontiguousarray(np.asarray(i["L0_attn_out_proj_w"], np.float32).T)
    xpT = np.concatenate([np.ascontiguousarray(np.asarray(i[f"L0_attn_x_proj_w_{d}"], np.float32).T)
                          for d in "fr"], axis=0)
    dtT = np.concatenate([np.ascontiguousarray(np.asarray(i[f"L0_attn_dt_proj_w_{d}"], np.float32).T)
                          for d in "fr"], axis=0)
    vecs0 = np.zeros((128, 32, 8), np.float32)
    A0 = np.zeros((128, 32 * NST), np.float32)
    for di, d in enumerate("fr"):
        cw = np.asarray(i[f"L0_attn_conv_w_{d}"], np.float32)
        cb = np.asarray(i[f"L0_attn_conv_b_{d}"], np.float32)
        db = np.asarray(i[f"L0_attn_dt_proj_b_{d}"], np.float32)
        Dd = np.asarray(i[f"L0_attn_D_{d}"], np.float32)
        Al = np.asarray(i[f"L0_attn_A_log_{d}"], np.float32)
        for c in range(16):
            slc = slice(c * 128, (c + 1) * 128)
            vecs0[:, di * 16 + c, 0:4] = cw[slc]
            vecs0[:, di * 16 + c, 4] = cb[slc]
            vecs0[:, di * 16 + c, 5] = db[slc]
            vecs0[:, di * 16 + c, 6] = Dd[slc]
            A0[:, (di * 16 + c) * NST:(di * 16 + c + 1) * NST] = Al[slc]
    vecs0 = vecs0.reshape(128, 256)
    wgT = np.ascontiguousarray(np.asarray(i["L0_ffn_w_gate"], np.float32).T)
    wuT = np.ascontiguousarray(np.asarray(i["L0_ffn_w_up"], np.float32).T)
    wdT = np.ascontiguousarray(np.asarray(i["L0_ffn_w_down"], np.float32).T)
    wxz = np.ascontiguousarray(np.asarray(i["L1_attn_W_xz"], np.float32))
    wdt = np.ascontiguousarray(np.asarray(i["L1_attn_W_dt"], np.float32))
    wB = np.ascontiguousarray(np.asarray(i["L1_attn_W_B"], np.float32))
    wC = np.ascontiguousarray(np.asarray(i["L1_attn_W_C"], np.float32))
    wo1 = np.ascontiguousarray(np.asarray(i["L1_attn_W_out"], np.float32))
    vecs1 = np.zeros((128, 8, 4), np.float32)
    A1 = np.zeros((128, 8 * NST), np.float32)
    bdt = np.asarray(i["L1_attn_b_dt"], np.float32)
    dsk = np.asarray(i["L1_attn_D_skip"], np.float32)
    vth = np.asarray(i["L1_attn_v_th"], np.float32)
    Al1 = np.asarray(i["L1_attn_A_log"], np.float32)
    for c in range(8):
        slc = slice(c * 128, (c + 1) * 128)
        vecs1[:, c, 0] = bdt[slc]
        vecs1[:, c, 1] = dsk[slc]
        vecs1[:, c, 2] = vth[slc]
        A1[:, c * NST:(c + 1) * NST] = Al1[slc]
    vecs1 = vecs1.reshape(128, 32)
    gate = np.ascontiguousarray(np.asarray(i["L1_ffn_gate"], np.float32))
    W1 = np.ascontiguousarray(np.asarray(i["L1_ffn_W1"], np.float32))
    W2 = np.ascontiguousarray(np.asarray(i["L1_ffn_W2"], np.float32))
    b1c = np.ascontiguousarray(
        np.asarray(i["L1_ffn_b1"], np.float32).reshape(8, 16, 128)
        .transpose(2, 0, 1).reshape(128, 128))
    b2r = np.ascontiguousarray(np.asarray(i["L1_ffn_b2"], np.float32))

    shared = dict(consts=consts, inT=inT, outT=outT, xpT=xpT, dtT=dtT,
                  vecs0=vecs0, A0=A0, wgT=wgT, wuT=wuT, wdT=wdT,
                  wxz=wxz, wdt=wdt, wB=wB, wC=wC, wo1=wo1, vecs1=vecs1, A1=A1,
                  gateW=gate, W1=W1, W2=W2, b1c=b1c, b2r=b2r)
    in_maps = []
    for core in range(8):
        m = dict(shared)
        m["x_fm"] = np.ascontiguousarray(x[core].T)
        in_maps.append(m)
    return in_maps


def kernel(**inputs):
    in_maps = _prep_inputs(inputs)
    nc = B().build()
    res = run_bass_kernel_spmd(nc, in_maps, core_ids=list(range(8)))
    return np.stack([np.asarray(r["out_tok"]) for r in res.results])
